# version 02
# baseline (speedup 1.0000x reference)
import math
from contextlib import ExitStack

import numpy as np
import ml_dtypes
import concourse.bass as bass
import concourse.mybir as mybir
from concourse.bass import IndirectOffsetOnAxis
from concourse.bass_utils import run_bass_kernel_spmd
from concourse.aot_env import allow_ici_all_to_all

F32 = mybir.dt.float32
BF16 = mybir.dt.bfloat16
I32 = mybir.dt.int32
AF = mybir.ActivationFunctionType
OP = mybir.AluOpType
AX = mybir.AxisListType.X
ENG = mybir.EngineType

NCORES = 8
D = 1024
TOWN = 2048
NTILE = TOWN // 128
TBATCH = 8192
NBT = TBATCH // 128
EPS = 1e-5
DFF0 = 4096
NE0 = 16
CAP0 = 1280
SCAP = 320
NE1 = 8
SM_SHIFT = -20.0


class Ctx:

    def __init__(self, nc):
        self.nc = nc
        self.sems = {}
        self.cnt = {}

    def s(self, name):
        if name not in self.sems:
            self.sems[name] = self.nc.alloc_semaphore(name)
            self.cnt[name] = 0
        return self.sems[name]

    def inc(self, inst, name, n=1):
        inst.then_inc(self.s(name), n)
        self.cnt[name] += n
        return self.cnt[name]

    def bump(self, name, n):
        self.s(name)
        self.cnt[name] += n

    def val(self, name):
        self.s(name)
        return self.cnt[name]

    def wait(self, eng, name, val=None):
        eng.wait_ge(self.s(name), self.cnt[name] if val is None else val)


class IO:
    def __init__(self, nc):
        self.nc = nc
        self.inputs = {}
        self.outputs = {}

    def inp(self, name, shape, dtype):
        if name not in self.inputs:
            self.inputs[name] = self.nc.dram_tensor(name, list(shape), dtype, kind="ExternalInput")
        return self.inputs[name]

    def out(self, name, shape, dtype):
        if name not in self.outputs:
            self.outputs[name] = self.nc.dram_tensor(name, list(shape), dtype, kind="ExternalOutput")
        return self.outputs[name]


def emit_ln_sweep(ctx, st, xin, xln_dram, pfx, row0_dram=None):
    nc = ctx.nc
    xt = [st.enter_context(nc.sbuf_tensor(f"{pfx}xt{i}", [128, D], F32)) for i in range(2)]
    sq = st.enter_context(nc.sbuf_tensor(f"{pfx}sq", [128, D], F32))
    stats = st.enter_context(nc.sbuf_tensor(f"{pfx}stats", [128, 8], F32))
    stats_s = st.enter_context(nc.sbuf_tensor(f"{pfx}statss", [128, 8], F32))
    xlnb = [st.enter_context(nc.sbuf_tensor(f"{pfx}xlnb{i}", [128, D], BF16)) for i in range(2)]
    row0 = st.enter_context(nc.sbuf_tensor(f"{pfx}row0", [1, D], F32)) if row0_dram is not None else None

    s_sq, s_v, s_r = (pfx + n for n in ("sq", "dve", "rstd"))
    s_ld = [pfx + "ld0", pfx + "ld1"]
    s_st = [pfx + "st0", pfx + "st1"]

    with nc.Block() as block:
        @block.sync
        def _(sync):
            for j in range(NTILE):
                if j >= 2:
                    ctx.wait(sync, s_v, 8 * (j - 1))
                ctx.inc(sync.dma_start(out=xt[j % 2][:], in_=xin[j * 128:(j + 1) * 128, :]),
                        s_ld[j % 2], 16)

        @block.scalar
        def _(scalar):
            for j in range(NTILE):
                scalar.wait_ge(ctx.s(s_ld[j % 2]), 16 * (j // 2 + 1))
                if j >= 1:
                    ctx.wait(scalar, s_v, 8 * j)
                ctx.inc(scalar.activation(out=sq[:], in_=xt[j % 2][:], func=AF.Square), s_sq, 1)
                ctx.wait(scalar, s_v, 8 * j + 6)
                if j == 0:
                    ctx.wait(scalar, "cinit", 17)
                ctx.inc(scalar.activation(out=stats_s[:, 0:1], in_=stats[:, 4:5], func=AF.Ln,
                                          bias=ctx.consts["eps_col"][:]), pfx + "sc", 1)
                ctx.wait(scalar, pfx + "sc")
                ctx.inc(scalar.activation(out=stats_s[:, 1:2], in_=stats_s[:, 0:1], func=AF.Exp,
                                          scale=-0.5), s_r, 1)

        @block.vector
        def _(vector):
            for j in range(NTILE):
                vector.wait_ge(ctx.s(s_ld[j % 2]), 16 * (j // 2 + 1))
                ctx.inc(vector.reduce_sum(stats[:, 0:1], xt[j % 2][:], axis=AX), s_v, 1)
                ctx.wait(vector, s_v)
                ctx.inc(vector.tensor_scalar_mul(stats[:, 2:3], stats[:, 0:1], 1.0 / D), s_v, 1)
                ctx.wait(vector, s_sq, j + 1)
                ctx.inc(vector.reduce_sum(stats[:, 1:2], sq[:], axis=AX), s_v, 1)
                ctx.wait(vector, s_v)
                ctx.inc(vector.tensor_mul(stats[:, 3:4], stats[:, 2:3], stats[:, 2:3]), s_v, 1)
                ctx.inc(vector.tensor_scalar_mul(stats[:, 7:8], stats[:, 1:2], 1.0 / D), s_v, 1)
                ctx.wait(vector, s_v)
                if j >= 1:
                    ctx.wait(vector, s_r, j)
                ctx.inc(vector.tensor_sub(stats[:, 4:5], stats[:, 7:8], stats[:, 3:4]), s_v, 1)
                ctx.inc(vector.tensor_scalar_sub(sq[:], xt[j % 2][:], stats[:, 2:3]), s_v, 1)
                ctx.wait(vector, s_v)
                ctx.wait(vector, s_r, j + 1)
                if j >= 2:
                    vector.wait_ge(ctx.s(s_st[j % 2]), 16 * (j // 2))
                ctx.inc(vector.tensor_scalar_mul(xlnb[j % 2][:], sq[:], stats_s[:, 1:2]), s_v, 1)
                if row0_dram is not None and j == 0:
                    ctx.inc(vector.tensor_scalar_mul(row0[:], sq[0:1, :], stats_s[0:1, 1:2]), pfx + "r0", 1)

        @block.gpsimd
        def _(gpsimd):
            for j in range(NTILE):
                ctx.wait(gpsimd, s_v, 8 * (j + 1))
                ctx.inc(gpsimd.dma_start(out=xln_dram[j * 128:(j + 1) * 128, :], in_=xlnb[j % 2][:]),
                        s_st[j % 2], 16)
            if row0_dram is not None:
                ctx.wait(gpsimd, pfx + "r0", 1)
                ctx.inc(gpsimd.dma_start(out=row0_dram, in_=row0[:]), pfx + "r0st", 16)

    deps = [(s_st[0], ctx.val(s_st[0])), (s_st[1], ctx.val(s_st[1]))]
    if row0_dram is not None:
        deps.append((pfx + "r0st", ctx.val(pfx + "r0st")))
    return deps


def wait_deps(ctx, eng, deps):
    for sem, val in deps:
        eng.wait_ge(ctx.s(sem), val)


def emit_ffn1(ctx, io, xin, xout, pfx, consts):
    nc = ctx.nc
    wr_d = io.inp("l1_wr", [D, NE1], F32)
    br_d = io.inp("l1_br", [1, NE1], F32)
    w1_d = io.inp("l1_w1", [NE1 * D, D], F32)
    b1_d = io.inp("l1_b1", [NE1, D], F32)
    w2_d = io.inp("l1_w2", [NE1 * D, D], F32)
    b2_d = io.inp("l1_b2", [NE1, D], F32)

    xln_dram = nc.dram_tensor(pfx + "xln", [TOWN, D], BF16)
    rowin = nc.dram_tensor(pfx + "rowin", [1, D], F32)
    rowout = nc.dram_tensor(pfx + "rowout", [NCORES, D], F32)

    with ExitStack() as st_ln:
        ln_deps = emit_ln_sweep(ctx, st_ln, xin, xln_dram.ap(), pfx, row0_dram=rowin.ap())

    with ExitStack() as st:
        wr_sb = st.enter_context(nc.sbuf_tensor(pfx + "wr", [128, 64], F32))
        br_sb = st.enter_context(nc.sbuf_tensor(pfx + "brr", [1, 8], F32))
        row2_sb = st.enter_context(nc.sbuf_tensor(pfx + "row", [2, D], F32))
        rowT = st.enter_context(nc.sbuf_tensor(pfx + "rowT", [128, 8], F32))
        rowidx_sb = st.enter_context(nc.sbuf_tensor(pfx + "rowidx", [2, 1], I32))
        sc = st.enter_context(nc.sbuf_tensor(pfx + "sc", [1, 40], F32))
        iota8 = st.enter_context(nc.sbuf_tensor(pfx + "io8", [1, 8], F32))
        e_i = st.enter_context(nc.sbuf_tensor(pfx + "ei", [1, 2], I32))
        wtmp = [st.enter_context(nc.sbuf_tensor(f"{pfx}wtmp{i}", [128, D], F32)) for i in range(2)]
        wbf = [st.enter_context(nc.sbuf_tensor(f"{pfx}wbf{s}", [128, 8 * D], BF16)) for s in range(2)]
        b1row = st.enter_context(nc.sbuf_tensor(pfx + "b1row", [1, D], F32))
        b2a = st.enter_context(nc.sbuf_tensor(pfx + "b2a", [1, D], F32))
        b2b = st.enter_context(nc.sbuf_tensor(pfx + "b2b", [1, D], F32))
        b2s = st.enter_context(nc.sbuf_tensor(pfx + "b2sum", [1, D], F32))
        b1c = st.enter_context(nc.sbuf_tensor(pfx + "b1c", [128, 16], F32))
        b2q = st.enter_context(nc.sbuf_tensor(pfx + "b2q", [128, D], F32))
        xlnT = st.enter_context(nc.sbuf_tensor(pfx + "xlnT", [128, 8 * TOWN], BF16))
        hT = st.enter_context(nc.sbuf_tensor(pfx + "hT", [128, 16 * TOWN], BF16))
        osb = [st.enter_context(nc.sbuf_tensor(f"{pfx}osb{i}", [128, 512], F32)) for i in range(2)]
        t3 = [st.enter_context(nc.sbuf_tensor(f"{pfx}t3{i}", [128, 512], F32)) for i in range(2)]
        xr = [st.enter_context(nc.sbuf_tensor(f"{pfx}xr{i}", [128, D], F32)) for i in range(2)]
        xnew = [st.enter_context(nc.sbuf_tensor(f"{pfx}xnew{i}", [128, D], F32)) for i in range(2)]

        ps_t = st.enter_context(nc.psum_tensor(pfx + "pst", [128, 8], F32))
        ps_sc = st.enter_context(nc.psum_tensor(pfx + "pssc", [1, 8], F32))
        ps_b = st.enter_context(nc.psum_tensor(pfx + "psb", [128, 512], F32))
        ps_h = [st.enter_context(nc.psum_tensor(f"{pfx}psh{i}", [128, 512], F32)) for i in range(2)]
        ps_o = [st.enter_context(nc.psum_tensor(f"{pfx}pso{i}", [128, 512], F32)) for i in range(2)]

        s_ld2 = pfx + "ld2"
        s_cc = pfx + "cc"
        s_row = pfx + "rowd"
        s_trr = pfx + "trr"
        s_rtc = pfx + "rtc"
        s_scm = pfx + "scm"
        s_io8 = pfx + "io8s"
        s_e = pfx + "e"
        s_b1l = pfx + "b1l"
        s_b2l = pfx + "b2l"
        s_b1t = pfx + "b1t"
        s_b1c = pfx + "b1cs"
        s_b2s = pfx + "b2s"
        s_bcm = pfx + "bcm"
        s_b2q = pfx + "b2qs"
        s_w1l = pfx + "w1l"
        s_w1c = pfx + "w1c"
        s_w2l = pfx + "w2l"
        s_w2c = pfx + "w2c"
        s_ldT = pfx + "ldT"
        s_mmh = pfx + "mmh"
        s_ach = pfx + "ach"
        s_mmo = pfx + "mmo"
        s_aco = pfx + "aco"
        s_res = pfx + "res"
        s_xld = pfx + "xld"
        s_out = pfx + "out"

        with nc.Block() as block:
            @block.sync
            def _(sync):
                wait_deps(ctx, sync, ln_deps)
                for k in range(8):
                    ctx.inc(sync.dma_start(out=wr_sb[:, k * 8:(k + 1) * 8],
                                           in_=wr_d[k * 128:(k + 1) * 128, :]), s_ld2, 16)
                ctx.inc(sync.dma_start(out=br_sb[:], in_=br_d[:, :]), s_ld2, 16)

            @block.sync
            def _(sync):
                ctx.inc(sync.dma_start(out=iota8[:], in_=io.inp("iota8", [1, 8], F32)[:, :]), s_io8, 16)

            @block.gpsimd
            def _(gpsimd):
                wait_deps(ctx, gpsimd, ln_deps)
                ctx.inc(gpsimd.collective_compute(
                    "AllGather", OP.bypass, replica_groups=[list(range(NCORES))],
                    ins=[rowin.ap()], outs=[rowout.ap()]), s_cc, 1)

            @block.sync
            def _(sync):
                ctx.inc(sync.dma_start(out=rowidx_sb[:], in_=io.inp("rowidx", [2, 1], I32)[:, :]),
                        s_ld2, 16)

            @block.gpsimd
            def _(gpsimd):
                gpsimd.wait_ge(ctx.s(s_ld2), ctx.val(s_ld2))
                ctx.wait(gpsimd, s_cc)
                ctx.inc(gpsimd.indirect_dma_start(
                    out=row2_sb[:], out_offset=None, in_=rowout.ap(),
                    in_offset=IndirectOffsetOnAxis(ap=rowidx_sb[:], axis=0)), s_row, 16)

            @block.tensor
            def _(tensor):
                ctx.wait(tensor, s_row)
                ctx.wait(tensor, "cinit", 16)
                for k in range(8):
                    inst = tensor.transpose(ps_t[:, k:k + 1], row2_sb[0:1, k * 128:(k + 1) * 128],
                                            consts["ident_f32"][0:1, 0:1])
                ctx.inc(inst, s_trr, 1)

            @block.scalar
            def _(scalar):
                ctx.wait(scalar, s_trr)
                ctx.inc(scalar.copy(rowT[:], ps_t[:]), s_rtc, 1)

        with nc.Block() as block:
            @block.tensor
            def _(tensor):
                ctx.wait(tensor, s_rtc)
                tensor.wait_ge(ctx.s(s_ld2), 16 * 9)
                for k in range(8):
                    inst = tensor.matmul(ps_sc[:], rowT[:, k:k + 1], wr_sb[:, k * 8:(k + 1) * 8],
                                         start=(k == 0), stop=(k == 7))
                ctx.inc(inst, s_scm, 1)

            @block.vector
            def _(vector):
                ctx.wait(vector, s_scm)
                vector.wait_ge(ctx.s(s_io8), 16)
                ch = pfx + "vch"
                def _c(inst):
                    ctx.inc(inst, ch, 1)
                    ctx.wait(vector, ch)
                _c(vector.tensor_add(sc[0:1, 0:8], ps_sc[:], br_sb[:]))
                _c(vector.reduce_max(sc[0:1, 8:9], sc[0:1, 0:8], axis=AX))
                _c(vector.tensor_single_scalar(sc[0:1, 16:24], sc[0:1, 0:8], sc[0:1, 8:9], OP.is_equal))
                _c(vector.tensor_mul(sc[0:1, 24:32], sc[0:1, 16:24], iota8[:]))
                _c(vector.reduce_sum(sc[0:1, 9:10], sc[0:1, 24:32], axis=AX))
                _c(vector.scalar_tensor_tensor(sc[0:1, 32:40], sc[0:1, 16:24], -1e30, sc[0:1, 0:8],
                                               OP.mult, OP.add))
                _c(vector.reduce_max(sc[0:1, 10:11], sc[0:1, 32:40], axis=AX))
                _c(vector.tensor_single_scalar(sc[0:1, 16:24], sc[0:1, 32:40], sc[0:1, 10:11], OP.is_equal))
                _c(vector.tensor_mul(sc[0:1, 24:32], sc[0:1, 16:24], iota8[:]))
                _c(vector.reduce_sum(sc[0:1, 11:12], sc[0:1, 24:32], axis=AX))
                _c(vector.tensor_copy(e_i[0:1, 0:1], sc[0:1, 9:10]))
                ctx.inc(vector.tensor_copy(e_i[0:1, 1:2], sc[0:1, 11:12]), s_e, 1)

        with nc.Block() as block:
            @block.sync
            def _(sync):
                ctx.wait(sync, s_e, 1)
                for sl in range(2):
                    ereg = sync.alloc_register()
                    sync.load(ereg, e_i[0:1, sl:sl + 1])
                    if sl == 1:
                        ctx.wait(sync, s_b1t, 1)
                    b2dst = b2a if sl == 0 else b2b
                    for e in range(NE1):
                        with sync.If_cmp(ereg, e, "IS_EQ"):
                            sync.dma_start(out=b1row[:], in_=b1_d[e:e + 1, :]).then_inc(ctx.s(s_b1l), 16)
                            sync.dma_start(out=b2dst[:], in_=b2_d[e:e + 1, :]).then_inc(ctx.s(s_b2l), 16)
                    ctx.bump(s_b1l, 16)
                    ctx.bump(s_b2l, 16)

            @block.tensor
            def _(tensor):
                for sl in range(2):
                    tensor.wait_ge(ctx.s(s_b1l), 16 * (sl + 1))
                    if sl == 1:
                        ctx.wait(tensor, s_b1c, 1)
                    for f in range(8):
                        inst = tensor.transpose(ps_t[:, f:f + 1],
                                                b1row[0:1, f * 128:(f + 1) * 128],
                                                consts["ident_f32"][0:1, 0:1])
                    ctx.inc(inst, s_b1t, 1)

            @block.scalar
            def _(scalar):
                for sl in range(2):
                    ctx.wait(scalar, s_b1t, sl + 1)
                    ctx.inc(scalar.copy(b1c[:, sl * 8:(sl + 1) * 8], ps_t[:]), s_b1c, 1)

        with nc.Block() as block:
            @block.vector
            def _(vector):
                vector.wait_ge(ctx.s(s_b2l), 32)
                ctx.inc(vector.tensor_add(b2s[:], b2a[:], b2b[:]), s_b2s, 1)

            @block.tensor
            def _(tensor):
                ctx.wait(tensor, s_b2s)
                for n in range(2):
                    if n == 1:
                        ctx.wait(tensor, s_b2q, 1)
                    inst = tensor.matmul(ps_b[:], consts["ones1_f32"][:], b2s[0:1, n * 512:(n + 1) * 512],
                                         start=True, stop=True)
                    ctx.inc(inst, s_bcm, 1)

            @block.scalar
            def _(scalar):
                for n in range(2):
                    ctx.wait(scalar, s_bcm, n + 1)
                    ctx.inc(scalar.activation(out=b2q[:, n * 512:(n + 1) * 512], in_=ps_b[:],
                                              func=AF.Copy, scale=0.25), s_b2q, 1)

        with nc.Block() as block:
            @block.sync
            def _(sync):
                for k in range(8):
                    ctx.inc(sync.dma_start(out=xlnT[:, k * TOWN:(k + 1) * TOWN],
                                           in_=xln_dram.ap()[:, k * 128:(k + 1) * 128],
                                           transpose=True), s_ldT, 16)

            @block.sync
            def _(sync):
                for sl in range(2):
                    ereg = sync.alloc_register()
                    sync.load(ereg, e_i[0:1, sl:sl + 1])
                    for k in range(8):
                        g = sl * 8 + k
                        if g >= 2:
                            ctx.wait(sync, s_w1c, g - 1)
                        for e in range(NE1):
                            with sync.If_cmp(ereg, e, "IS_EQ"):
                                sync.dma_start(out=wtmp[g % 2][:],
                                               in_=w1_d[e * D + k * 128:e * D + (k + 1) * 128, :]
                                               ).then_inc(ctx.s(s_w1l + str(g % 2)), 16)
                        ctx.bump(s_w1l + str(g % 2), 16)

            @block.vector
            def _(vector):
                for sl in range(2):
                    for k in range(8):
                        g = sl * 8 + k
                        vector.wait_ge(ctx.s(s_w1l + str(g % 2)), 16 * (g // 2 + 1))
                        ctx.inc(vector.tensor_copy(wbf[sl][:, k * D:(k + 1) * D], wtmp[g % 2][:]), s_w1c, 1)

        with nc.Block() as block:
            @block.tensor
            def _(tensor):
                tensor.wait_ge(ctx.s(s_ldT), ctx.val(s_ldT))
                g = 0
                for sl in range(2):
                    for f in range(8):
                        for tb in range(4):
                            if g >= 2:
                                ctx.wait(tensor, s_ach, g - 1)
                            tensor.wait_ge(ctx.s(s_w1c), sl * 8 + 8)
                            for k in range(8):
                                inst = tensor.matmul(
                                    ps_h[g % 2][:],
                                    wbf[sl][:, k * D + f * 128:k * D + (f + 1) * 128],
                                    xlnT[:, k * TOWN + tb * 512:k * TOWN + (tb + 1) * 512],
                                    start=(k == 0), stop=(k == 7))
                            ctx.inc(inst, s_mmh, 1)
                            g += 1

            @block.scalar
            def _(scalar):
                ctx.wait(scalar, s_b1c, 2)
                g = 0
                for sl in range(2):
                    for f in range(8):
                        col = sl * 8 + f
                        for tb in range(4):
                            ctx.wait(scalar, s_mmh, g + 1)
                            ctx.inc(scalar.activation(
                                out=hT[:, col * TOWN + tb * 512:col * TOWN + (tb + 1) * 512],
                                in_=ps_h[g % 2][:], func=AF.Gelu, bias=b1c[:, col:col + 1]), s_ach, 1)
                            g += 1

        with nc.Block() as block:
            @block.sync
            def _(sync):
                ctx.wait(sync, s_mmh, 64)
                for sl in range(2):
                    ereg = sync.alloc_register()
                    sync.load(ereg, e_i[0:1, sl:sl + 1])
                    for k in range(8):
                        g = sl * 8 + k
                        if g >= 2:
                            ctx.wait(sync, s_w2c, g - 1)
                        for e in range(NE1):
                            with sync.If_cmp(ereg, e, "IS_EQ"):
                                sync.dma_start(out=wtmp[g % 2][:],
                                               in_=w2_d[e * D + k * 128:e * D + (k + 1) * 128, :]
                                               ).then_inc(ctx.s(s_w2l + str(g % 2)), 16)
                        ctx.bump(s_w2l + str(g % 2), 16)

            @block.vector
            def _(vector):
                for sl in range(2):
                    for k in range(8):
                        g = sl * 8 + k
                        vector.wait_ge(ctx.s(s_w2l + str(g % 2)), 16 * (g // 2 + 1))
                        ctx.inc(vector.tensor_copy(wbf[sl][:, k * D:(k + 1) * D], wtmp[g % 2][:]), s_w2c, 1)

        with nc.Block() as block:
            @block.sync
            def _(sync):
                for j in range(NTILE):
                    if j >= 2:
                        ctx.wait(sync, s_res, 2 * (j - 1))
                    ctx.inc(sync.dma_start(out=xr[j % 2][:], in_=xin[j * 128:(j + 1) * 128, :]),
                            s_xld + str(j % 2), 16)

            @block.tensor
            def _(tensor):
                ctx.wait(tensor, s_ach)
                ctx.wait(tensor, s_w2c)
                g = 0
                for j in range(NTILE):
                    for n in range(2):
                        if g >= 2:
                            ctx.wait(tensor, s_aco, g - 1)
                        for sl in range(2):
                            for f in range(8):
                                col = sl * 8 + f
                                inst = tensor.matmul(
                                    ps_o[g % 2][:],
                                    hT[:, col * TOWN + j * 128:col * TOWN + (j + 1) * 128],
                                    wbf[sl][:, f * D + n * 512:f * D + (n + 1) * 512],
                                    start=(sl == 0 and f == 0), stop=(sl == 1 and f == 7))
                        ctx.inc(inst, s_mmo, 1)
                        g += 1

            @block.scalar
            def _(scalar):
                g = 0
                for j in range(NTILE):
                    for n in range(2):
                        ctx.wait(scalar, s_mmo, g + 1)
                        if g >= 2:
                            ctx.wait(scalar, s_res, g - 1)
                        ctx.inc(scalar.activation(out=osb[g % 2][:], in_=ps_o[g % 2][:],
                                                  func=AF.Copy, scale=0.25), s_aco, 1)
                        g += 1

            @block.vector
            def _(vector):
                ctx.wait(vector, s_b2q, 2)
                g = 0
                for j in range(NTILE):
                    vector.wait_ge(ctx.s(s_xld + str(j % 2)), 16 * (j // 2 + 1))
                    if j >= 2:
                        vector.wait_ge(ctx.s(s_out + str(j % 2)), 16 * (j // 2))
                    for n in range(2):
                        ctx.wait(vector, s_aco, g + 1)
                        vector.tensor_add(t3[g % 2][:], osb[g % 2][:],
                                          b2q[:, n * 512:(n + 1) * 512])
                        ctx.inc(vector.tensor_add(xnew[j % 2][:, n * 512:(n + 1) * 512],
                                                  t3[g % 2][:],
                                                  xr[j % 2][:, n * 512:(n + 1) * 512]), s_res, 1)
                        g += 1

            @block.gpsimd
            def _(gpsimd):
                for j in range(NTILE):
                    ctx.wait(gpsimd, s_res, 2 * (j + 1))
                    ctx.inc(gpsimd.dma_start(out=xout[j * 128:(j + 1) * 128, :], in_=xnew[j % 2][:]),
                            s_out + str(j % 2), 16)

    return [(s_out + "0", ctx.val(s_out + "0")), (s_out + "1", ctx.val(s_out + "1"))]


def emit_attn(ctx, io, xin, xout, pfx, consts, layer):
    nc = ctx.nc
    NH = 4
    NSUB = TOWN // 512
    NIB = TBATCH // 512
    grp = [[0, 1, 2, 3], [4, 5, 6, 7]]

    if layer == 0:
        wqkv_d = io.inp("l0_wqkv", [D, 768], F32)
        wo_d = io.inp("l0_wo_full", [D, D], F32)
        cos_d = io.inp("rope_cos", [64, TBATCH], BF16)
        sin_d = io.inp("rope_sin", [64, TBATCH], BF16)
        sm_scale = 1.0 / 8.0
    else:
        wqkv_d = io.inp("l1_wqkvT", [D, 768], F32)
        wo_d = io.inp("l1_woT", [D, D], F32)
        bqk_d = io.inp("l1_bqk", [128, 4], F32)
        bv_d = io.inp("l1_bv", [1, 256], F32)
        wql_d = io.inp("l1_wqlsr", [64, 128], F32)
        wkl_d = io.inp("l1_wklsr", [64, 128], F32)
        bo_d = io.inp("l1_bo", [1, D], F32)
        sm_scale = 1.0 / math.sqrt(32.0)
    mask_d = io.inp("diagmask", [512, 512], F32)
    ridx_d = io.inp("rebase_idx", [128, 8], I32)

    xln_dram = nc.dram_tensor(pfx + "xln", [TOWN, D], BF16)
    g_dram = nc.dram_tensor(pfx + "g", [TBATCH, D], BF16)
    xlnT_dram = nc.dram_tensor(pfx + "xlnT", [D, TBATCH], BF16)
    qk_dram = nc.dram_tensor(pfx + "qk", [512, TBATCH], BF16)
    y_dram = nc.dram_tensor(pfx + "y", [256, TBATCH], BF16)
    Y_dram = nc.dram_tensor(pfx + "Y", [D, TBATCH], BF16)

    with ExitStack() as st_ln:
        ln_deps = emit_ln_sweep(ctx, st_ln, xin, xln_dram.ap(), pfx)

    S = lambda n: pfx + n

    with ExitStack() as st:
        y_cat = [st.enter_context(nc.sbuf_tensor(f"{pfx}ycat{i}", [128, TBATCH], BF16)) for i in range(2)]
        v_all = st.enter_context(nc.sbuf_tensor(pfx + "vall", [128, NBT * 4 * 65], BF16))
        if layer == 1:
            ones_bf = st.enter_context(nc.sbuf_tensor(pfx + "onesbf", [1, 128], BF16))
            bo_bf = st.enter_context(nc.sbuf_tensor(pfx + "bobf", [1, D], BF16))
        dmask = st.enter_context(nc.sbuf_tensor(pfx + "dmask", [128, 4 * 512], F32))
        shift_col = st.enter_context(nc.sbuf_tensor(pfx + "shift", [128, 1], F32))

        with nc.Block() as block:
            @block.gpsimd
            def _(gpsimd):
                wait_deps(ctx, gpsimd, ln_deps)
                ctx.inc(gpsimd.collective_compute(
                    "AllGather", OP.bypass, replica_groups=grp,
                    ins=[xln_dram.ap()], outs=[g_dram.ap()]), S("cc"), 1)

            @block.vector
            def _(vector):
                wait_deps(ctx, vector, ln_deps)
                vector.memset(v_all[:], 1.0)
                ctx.inc(vector.memset(shift_col[:], SM_SHIFT), S("ms"), 1)
                if layer == 1:
                    ctx.inc(vector.memset(ones_bf[:], 1.0), S("ob"), 1)

            @block.sync
            def _(sync):
                wait_deps(ctx, sync, ln_deps)
                for dvar in range(4):
                    ctx.inc(sync.dma_start(out=dmask[:, dvar * 512:(dvar + 1) * 512],
                                           in_=mask_d[dvar * 128:(dvar + 1) * 128, :]), S("msk0"), 16)


        with ExitStack() as st2:
            strip = [st2.enter_context(nc.sbuf_tensor(f"{pfx}strip{i}", [128, TBATCH], BF16))
                     for i in range(2)]
            with nc.Block() as block:
                @block.sync
                def _(sync):
                    ctx.wait(sync, S("cc"))
                    for k in range(8):
                        if k >= 2:
                            sync.wait_ge(ctx.s(S("trs") + str(k % 2)), 16 * (k // 2))
                        ctx.inc(sync.dma_start(out=strip[k % 2][:],
                                               in_=g_dram.ap()[:, k * 128:(k + 1) * 128],
                                               transpose=True), S("trl") + str(k % 2), 16)

                @block.gpsimd
                def _(gpsimd):
                    for k in range(8):
                        gpsimd.wait_ge(ctx.s(S("trl") + str(k % 2)), 16 * (k // 2 + 1))
                        ctx.inc(gpsimd.dma_start(out=xlnT_dram.ap()[k * 128:(k + 1) * 128, :],
                                                 in_=strip[k % 2][:]), S("trs") + str(k % 2), 16)
        tr_done = [(S("trs0"), ctx.val(S("trs0"))), (S("trs1"), ctx.val(S("trs1")))]

        with ExitStack() as st2:
            xb = st2.enter_context(nc.sbuf_tensor(pfx + "xb", [128, 8 * TOWN], BF16))
            wtmp = st2.enter_context(nc.sbuf_tensor(pfx + "swt", [128, 768], F32))
            wq_bf = st2.enter_context(nc.sbuf_tensor(pfx + "wqbf", [128, 8 * 768], BF16))
            ev = [st2.enter_context(nc.sbuf_tensor(f"{pfx}ev{i}", [128, 512], BF16)) for i in range(2)]
            if layer == 1:
                bqk_sb = st2.enter_context(nc.sbuf_tensor(pfx + "bqk", [128, 4], F32))
                bv_f = st2.enter_context(nc.sbuf_tensor(pfx + "bvf", [1, 256], F32))
                bv_bf = st2.enter_context(nc.sbuf_tensor(pfx + "bvbf", [1, 256], BF16))
            ps_qk = [st2.enter_context(nc.psum_tensor(f"{pfx}psqk{i}", [128, 512], F32)) for i in range(2)]
            ps_v = [st2.enter_context(nc.psum_tensor(f"{pfx}psv{i}", [128, 256], F32)) for i in range(2)]

            with nc.Block() as block:
                @block.sync
                def _(sync):
                    for k in range(8):
                        if k >= 1:
                            ctx.wait(sync, S("wcv"), k)
                        ctx.inc(sync.dma_start(out=wtmp[:], in_=wqkv_d[k * 128:(k + 1) * 128, :]),
                                S("wld"), 16)
                    if layer == 1:
                        ctx.inc(sync.dma_start(out=bqk_sb[:], in_=bqk_d[:, :]), S("bqkl"), 16)
                        ctx.inc(sync.dma_start(out=bv_f[:], in_=bv_d[:, :]), S("bvl"), 16)

                @block.vector
                def _(vector):
                    for k in range(8):
                        vector.wait_ge(ctx.s(S("wld")), 16 * (k + 1))
                        ctx.inc(vector.tensor_copy(wq_bf[:, k * 768:(k + 1) * 768], wtmp[:]), S("wcv"), 1)
                    if layer == 1:
                        vector.wait_ge(ctx.s(S("bvl")), 16)
                        ctx.inc(vector.tensor_copy(bv_bf[:], bv_f[:]), S("bvc"), 1)

            with nc.Block() as block:
                @block.sync
                def _(sync):
                    wait_deps(ctx, sync, tr_done)
                    for tb in range(4):
                        if tb > 0:
                            ctx.wait(sync, S("swmm"), tb * (NH * NSUB + NTILE))
                        for k in range(8):
                            ctx.inc(sync.dma_start(
                                out=xb[:, k * TOWN:(k + 1) * TOWN],
                                in_=xlnT_dram.ap()[k * 128:(k + 1) * 128, tb * TOWN:(tb + 1) * TOWN]),
                                S("swld"), 16)

                @block.tensor
                def _(tensor):
                    ctx.wait(tensor, S("wcv"))
                    if layer == 1:
                        ctx.wait(tensor, S("bvc"))
                        ctx.wait(tensor, S("ob"))
                    nmm = 0
                    for tb in range(4):
                        tensor.wait_ge(ctx.s(S("swld")), 16 * 8 * (tb + 1))
                        for h in range(NH):
                            for nsub in range(NSUB):
                                if nmm >= 2:
                                    tensor.wait_ge(ctx.s(S("swev")), nmm - 1)
                                for k in range(8):
                                    inst = tensor.matmul(
                                        ps_qk[nmm % 2][:],
                                        wq_bf[:, k * 768 + h * 128:k * 768 + (h + 1) * 128],
                                        xb[:, k * TOWN + nsub * 512:k * TOWN + (nsub + 1) * 512],
                                        start=(k == 0), stop=(k == 7))
                                ctx.inc(inst, S("swmm"), 1)
                                nmm += 1
                        for ts in range(NTILE):
                            if nmm >= 2:
                                tensor.wait_ge(ctx.s(S("swev")), nmm - 1)
                            for k in range(8):
                                inst = tensor.matmul(
                                    ps_v[nmm % 2][:],
                                    xb[:, k * TOWN + ts * 128:k * TOWN + (ts + 1) * 128],
                                    wq_bf[:, k * 768 + 512:k * 768 + 768],
                                    start=(k == 0), stop=(layer == 0 and k == 7))
                            if layer == 1:
                                inst = tensor.matmul(ps_v[nmm % 2][:], ones_bf[:], bv_bf[:],
                                                     start=False, stop=True)
                            ctx.inc(inst, S("swmm"), 1)
                            nmm += 1

                @block.scalar
                def _(scalar):
                    scalar.wait_ge(ctx.s(S("msk0")), ctx.val(S("msk0")))
                    scalar.wait_ge(ctx.s(S("ms")), 1)
                    if layer == 1:
                        scalar.wait_ge(ctx.s(S("bqkl")), 16)
                    nmm = 0
                    nqk = 0
                    for tb in range(4):
                        for h in range(NH):
                            for nsub in range(NSUB):
                                ctx.wait(scalar, S("swmm"), nmm + 1)
                                if nqk >= 2:
                                    scalar.wait_ge(ctx.s(S("swst") + str(nqk % 2)), 16 * (nqk // 2))
                                if layer == 1:
                                    inst = scalar.activation(out=ev[nqk % 2][:], in_=ps_qk[nmm % 2][:],
                                                             func=AF.Identity, bias=bqk_sb[:, h:h + 1])
                                else:
                                    inst = scalar.activation(out=ev[nqk % 2][:], in_=ps_qk[nmm % 2][:],
                                                             func=AF.Copy)
                                ctx.inc(inst, S("swev"), 1)
                                nmm += 1
                                nqk += 1
                        for ts in range(NTILE):
                            ctx.wait(scalar, S("swmm"), nmm + 1)
                            j_tile = tb * NTILE + ts
                            for h in range(NH):
                                blk = (j_tile * 4 + h) * 65
                                inst = scalar.activation(out=v_all[:, blk:blk + 64],
                                                         in_=ps_v[nmm % 2][:, h * 64:(h + 1) * 64],
                                                         func=AF.Copy)
                            ctx.inc(inst, S("swev"), 1)
                            nmm += 1

                @block.gpsimd
                def _(gpsimd):
                    nev = 0
                    nqk = 0
                    for tb in range(4):
                        for h in range(NH):
                            for nsub in range(NSUB):
                                gpsimd.wait_ge(ctx.s(S("swev")), nev + 1)
                                ctx.inc(gpsimd.dma_start(
                                    out=qk_dram.ap()[h * 128:(h + 1) * 128,
                                                     tb * TOWN + nsub * 512:tb * TOWN + (nsub + 1) * 512],
                                    in_=ev[nqk % 2][:]), S("swst") + str(nqk % 2), 16)
                                nev += 1
                                nqk += 1
                        nev += NTILE

        sw_done = [(S("swst0"), ctx.val(S("swst0"))), (S("swst1"), ctx.val(S("swst1"))),
                   (S("swev"), ctx.val(S("swev")))]

        st_core = ExitStack()
        if layer == 0:
            cosT = st_core.enter_context(nc.sbuf_tensor(pfx + "cosT", [64, TBATCH], BF16))
            sinT = st_core.enter_context(nc.sbuf_tensor(pfx + "sinT", [64, TBATCH], BF16))
            kk = [st_core.enter_context(nc.sbuf_tensor(f"{pfx}kk{i}", [64, TBATCH], BF16)) for i in range(4)]
        else:
            klr = [st_core.enter_context(nc.sbuf_tensor(f"{pfx}klr{i}", [32, TBATCH], BF16)) for i in range(4)]
            wql_bf = st_core.enter_context(nc.sbuf_tensor(pfx + "wql", [64, 128], BF16))

        with ExitStack() as st2:
            HB = TBATCH // 2
            kraw = st2.enter_context(nc.sbuf_tensor(
                pfx + "kraw", [64, HB if layer == 0 else TBATCH], BF16))
            if layer == 0:
                ks = st2.enter_context(nc.sbuf_tensor(pfx + "ks", [64, HB], BF16))
                u1 = st2.enter_context(nc.sbuf_tensor(pfx + "u1", [64, HB], BF16))
                with nc.Block() as block:
                    @block.sync
                    def _(sync):
                        wait_deps(ctx, sync, sw_done)
                        ctx.inc(sync.dma_start(out=cosT[:], in_=cos_d[:, :]), S("rope"), 16)
                        ctx.inc(sync.dma_start(out=sinT[:], in_=sin_d[:, :]), S("rope"), 16)
                        for g in range(2 * NH):
                            h, hf = g // 2, g % 2
                            if g > 0:
                                ctx.wait(sync, S("kp"), g)
                            ctx.inc(sync.dma_start(
                                out=kraw[:],
                                in_=qk_dram.ap()[h * 128 + 64:(h + 1) * 128, hf * HB:(hf + 1) * HB]),
                                S("kld"), 16)

                    @block.scalar
                    def _(scalar):
                        for g in range(2 * NH):
                            scalar.wait_ge(ctx.s(S("kld")), 16 * (g + 1))
                            if g > 0:
                                ctx.wait(scalar, S("kch"), 2 * g)
                            scalar.copy(ks[0:32, :], kraw[32:64, :])
                            ctx.inc(scalar.copy(ks[32:64, :], kraw[0:32, :]), S("ksc"), 1)

                    @block.vector
                    def _(vector):
                        vector.wait_ge(ctx.s(S("rope")), 32)
                        for g in range(2 * NH):
                            h, hf = g // 2, g % 2
                            vector.wait_ge(ctx.s(S("kld")), 16 * (g + 1))
                            ctx.inc(vector.tensor_mul(u1[:], kraw[:], cosT[:, hf * HB:(hf + 1) * HB]),
                                    S("kch"), 1)
                            ctx.wait(vector, S("ksc"), g + 1)
                            ctx.wait(vector, S("kch"))
                            ctx.inc(vector.tensor_mul(kraw[:], ks[:], sinT[:, hf * HB:(hf + 1) * HB]),
                                    S("kch"), 1)
                            ctx.wait(vector, S("kch"))
                            inst = vector.tensor_add(kk[h][:, hf * HB:(hf + 1) * HB], u1[:], kraw[:])
                            ctx.inc(inst, S("kp"), 1)
            else:
                wklt = st2.enter_context(nc.sbuf_tensor(pfx + "wklt", [64, 128], F32))
                wkl_bf = st2.enter_context(nc.sbuf_tensor(pfx + "wkl", [64, 128], BF16))
                wqlt = st2.enter_context(nc.sbuf_tensor(pfx + "wqlt", [64, 128], F32))
                ps_lr = [st2.enter_context(nc.psum_tensor(f"{pfx}pslr{i}", [32, 512], F32))
                         for i in range(2)]
                with nc.Block() as block:
                    @block.sync
                    def _(sync):
                        ctx.inc(sync.dma_start(out=wklt[:], in_=wkl_d[:, :]), S("wkll"), 16)
                        ctx.inc(sync.dma_start(out=wqlt[:], in_=wql_d[:, :]), S("wqll"), 16)
                        wait_deps(ctx, sync, sw_done)
                        for h in range(NH):
                            if h > 0:
                                ctx.wait(sync, S("kp"), (TBATCH // 512) * h)
                            ctx.inc(sync.dma_start(out=kraw[:],
                                                   in_=qk_dram.ap()[h * 128 + 64:(h + 1) * 128, :]),
                                    S("kld"), 16)

                    @block.vector
                    def _(vector):
                        vector.wait_ge(ctx.s(S("wkll")), 16)
                        ctx.inc(vector.tensor_copy(wkl_bf[:], wklt[:]), S("wklc"), 1)
                        vector.wait_ge(ctx.s(S("wqll")), 16)
                        ctx.inc(vector.tensor_copy(wql_bf[:], wqlt[:]), S("wqlc"), 1)

                    @block.tensor
                    def _(tensor):
                        ctx.wait(tensor, S("wklc"))
                        ng = 0
                        for h in range(NH):
                            tensor.wait_ge(ctx.s(S("kld")), 16 * (h + 1))
                            for cb in range(TBATCH // 512):
                                if ng >= 2:
                                    ctx.wait(tensor, S("kp"), ng - 1)
                                inst = tensor.matmul(ps_lr[ng % 2][:],
                                                     wkl_bf[:, h * 32:(h + 1) * 32],
                                                     kraw[:, cb * 512:(cb + 1) * 512],
                                                     start=True, stop=True)
                                ctx.inc(inst, S("kpm"), 1)
                                ng += 1

                    @block.scalar
                    def _(scalar):
                        ng = 0
                        for h in range(NH):
                            for cb in range(TBATCH // 512):
                                ctx.wait(scalar, S("kpm"), ng + 1)
                                ctx.inc(scalar.copy(klr[h][:, cb * 512:(cb + 1) * 512],
                                                    ps_lr[ng % 2][:]), S("kp"), 1)
                                ng += 1

        kp_done = [(S("kp"), ctx.val(S("kp")))]

        with ExitStack() as st2:
            qraw = st2.enter_context(nc.sbuf_tensor(pfx + "qraw", [64, 512], BF16))
            if layer == 0:
                qs = st2.enter_context(nc.sbuf_tensor(pfx + "qs", [64, 512], BF16))
                qt1 = st2.enter_context(nc.sbuf_tensor(pfx + "qt1", [64, 512], BF16))
                qt2 = st2.enter_context(nc.sbuf_tensor(pfx + "qt2", [64, 512], BF16))
                qfin = st2.enter_context(nc.sbuf_tensor(pfx + "qfin", [64, 512], BF16))
            else:
                qfin = st2.enter_context(nc.sbuf_tensor(pfx + "qfin", [32, 512], BF16))
            pt = [st2.enter_context(nc.sbuf_tensor(f"{pfx}pt{i}", [128, 512], BF16)) for i in range(2)]
            mskb = [st2.enter_context(nc.sbuf_tensor(f"{pfx}mskb{i}", [128, 512], F32)) for i in range(2)]
            rrow = st2.enter_context(nc.sbuf_tensor(pfx + "rrow", [1, 512], F32))
            rb_sb = st2.enter_context(nc.sbuf_tensor(pfx + "rb", [64, 512], F32))
            ps_s = [st2.enter_context(nc.psum_tensor(f"{pfx}pss{i}", [128, 512], F32)) for i in range(2)]
            ps_y = st2.enter_context(nc.psum_tensor(pfx + "psy", [65, 512], F32))
            ps_bc = st2.enter_context(nc.psum_tensor(pfx + "psbc", [64, 512], F32))
            if layer == 1:
                ps_ql = st2.enter_context(nc.psum_tensor(pfx + "psql", [32, 512], F32))

            blocks_sched = [(h, ib) for h in range(NH) for ib in range(NIB)]
            jcounts = [4 * ib + 4 for (h, ib) in blocks_sched]
            cumS = [0]
            for jc in jcounts:
                cumS.append(cumS[-1] + jc)
            diag_idx = []
            for gi, jc in enumerate(jcounts):
                for j in range(jc - 4, jc):
                    diag_idx.append(cumS[gi] + j)
            diag_rank = {siux: r for r, siux in enumerate(diag_idx)}

            def emit_core_chunk(gis):
                with nc.Block() as block:
                    @block.sync
                    def _(sync):
                        for gi in gis:
                            h, ib = blocks_sched[gi]
                            if gi > 0:
                                if layer == 0:
                                    ctx.wait(sync, S("qsc"), gi)
                                    ctx.wait(sync, S("qch"), 2 * gi - 1)
                                else:
                                    ctx.wait(sync, S("qlrm"), gi)
                            ctx.inc(sync.dma_start(
                                out=qraw[:], in_=qk_dram.ap()[h * 128:h * 128 + 64,
                                                              ib * 512:(ib + 1) * 512]),
                                S("qld") + str(gi % 2), 16)

                    @block.scalar
                    def _(scalar):
                        for gi in gis:
                            h, ib = blocks_sched[gi]
                            jc = jcounts[gi]
                            if layer == 0:
                                scalar.wait_ge(ctx.s(S("qld") + str(gi % 2)), 16 * (gi // 2 + 1))
                                if gi > 0:
                                    ctx.wait(scalar, S("qch"), 2 * gi)
                                scalar.copy(qs[0:32, :], qraw[32:64, :])
                                ctx.inc(scalar.copy(qs[32:64, :], qraw[0:32, :]), S("qsc"), 1)
                            else:
                                ctx.wait(scalar, S("qlrm"), gi + 1)
                                if gi > 0:
                                    ctx.wait(scalar, S("mm"), cumS[gi])
                                ctx.inc(scalar.copy(qfin[:], ps_ql[:]), S("qp"), 1)
                            for j in range(jc):
                                si = cumS[gi] + j
                                ctx.wait(scalar, S("mm"), si + 1)
                                if si >= 2:
                                    ctx.wait(scalar, S("avd"), si - 1)
                                if si in diag_rank:
                                    ctx.wait(scalar, S("mskd"), diag_rank[si] + 1)
                                    src = mskb[si % 2][:]
                                else:
                                    src = ps_s[si % 2][:]
                                ctx.inc(scalar.activation(out=pt[si % 2][:], in_=src, func=AF.Exp,
                                                          scale=sm_scale, bias=shift_col[:]),
                                        S("exp"), 1)
                            ctx.wait(scalar, S("bcm"), gi + 1)
                            if gi > 0:
                                ctx.wait(scalar, S("yfin"), gi)
                            ctx.inc(scalar.copy(rb_sb[:], ps_bc[:]), S("rbc"), 1)

                    @block.vector
                    def _(vector):
                        if gis[0] == 0:
                            wait_deps(ctx, vector, kp_done)
                        for gi in gis:
                            h, ib = blocks_sched[gi]
                            jc = jcounts[gi]
                            if layer == 0:
                                vector.wait_ge(ctx.s(S("qld") + str(gi % 2)), 16 * (gi // 2 + 1))
                                if gi > 0:
                                    ctx.wait(vector, S("mm"), cumS[gi])
                                ctx.inc(vector.tensor_mul(qt1[:], qraw[:],
                                                          cosT[:, ib * 512:(ib + 1) * 512]), S("qch"), 1)
                                ctx.wait(vector, S("qsc"), gi + 1)
                                ctx.inc(vector.tensor_mul(qt2[:], qs[:],
                                                          sinT[:, ib * 512:(ib + 1) * 512]), S("qch"), 1)
                                ctx.wait(vector, S("qch"))
                                ctx.inc(vector.tensor_add(qfin[:], qt1[:], qt2[:]), S("qp"), 1)
                            for dpos in range(4):
                                j = jc - 4 + dpos
                                si = cumS[gi] + j
                                ctx.wait(vector, S("mm"), si + 1)
                                dr = diag_rank[si]
                                if dr >= 2:
                                    prev_si = diag_idx[dr - 2]
                                    ctx.wait(vector, S("exp"), prev_si + 1)
                                ctx.inc(vector.tensor_add(mskb[si % 2][:], ps_s[si % 2][:],
                                                          dmask[:, dpos * 512:(dpos + 1) * 512]),
                                        S("mskd"), 1)
                            ctx.wait(vector, S("avd"), cumS[gi + 1])
                            if gi > 0:
                                ctx.wait(vector, S("bcm"), gi)
                            ctx.inc(vector.reciprocal(rrow[:], ps_y[64:65, :]), S("rec"), 1)
                            ctx.wait(vector, S("rbc"), gi + 1)
                            inst = vector.tensor_mul(
                                y_cat[h // 2][(h % 2) * 64:(h % 2) * 64 + 64, ib * 512:(ib + 1) * 512],
                                ps_y[0:64, :], rb_sb[:])
                            ctx.inc(inst, S("yfin"), 1)

                    @block.tensor
                    def _(tensor):
                        if gis[0] == 0:
                            ctx.wait(tensor, "cinit", 16)
                            wait_deps(ctx, tensor, kp_done)
                            if layer == 1:
                                ctx.wait(tensor, S("wqlc"))
                        for gi in gis:
                            h, ib = blocks_sched[gi]
                            jc = jcounts[gi]
                            if layer == 1:
                                tensor.wait_ge(ctx.s(S("qld") + str(gi % 2)), 16 * (gi // 2 + 1))
                                if gi > 0:
                                    ctx.wait(tensor, S("qp"), gi)
                                inst = tensor.matmul(ps_ql[:], wql_bf[:, h * 32:(h + 1) * 32],
                                                     qraw[:], start=True, stop=True)
                                ctx.inc(inst, S("qlrm"), 1)
                            ctx.wait(tensor, S("qp"), gi + 1)
                            if gi > 0:
                                ctx.wait(tensor, S("yfin"), gi)
                            for j in range(jc):
                                si = cumS[gi] + j
                                if si >= 2:
                                    ctx.wait(tensor, S("exp"), si - 1)
                                if layer == 0:
                                    lhs = kk[h][:, j * 128:(j + 1) * 128]
                                else:
                                    lhs = klr[h][:, j * 128:(j + 1) * 128]
                                inst = tensor.matmul(ps_s[si % 2][:], lhs, qfin[:],
                                                     start=True, stop=True)
                                ctx.inc(inst, S("mm"), 1)
                                if j >= 1:
                                    siprev = si - 1
                                    ctx.wait(tensor, S("exp"), siprev + 1)
                                    jp = j - 1
                                    inst = tensor.matmul(
                                        ps_y[:],
                                        v_all[:, (jp * 4 + h) * 65:(jp * 4 + h) * 65 + 65],
                                        pt[siprev % 2][:],
                                        start=(jp == 0), stop=False)
                                    ctx.inc(inst, S("avd"), 1)
                            silast = cumS[gi] + jc - 1
                            ctx.wait(tensor, S("exp"), silast + 1)
                            jp = jc - 1
                            inst = tensor.matmul(
                                ps_y[:], v_all[:, (jp * 4 + h) * 65:(jp * 4 + h) * 65 + 65],
                                pt[silast % 2][:], start=(jp == 0), stop=True)
                            ctx.inc(inst, S("avd"), 1)
                            ctx.wait(tensor, S("rec"), gi + 1)
                            if gi > 0:
                                ctx.wait(tensor, S("rbc"), gi)
                            inst = tensor.matmul(ps_bc[:], consts["ones1_f32"][:, 0:64], rrow[:],
                                                 start=True, stop=True)
                            ctx.inc(inst, S("bcm"), 1)

            CHUNK = 4
            for c0 in range(0, len(blocks_sched), CHUNK):
                emit_core_chunk(list(range(c0, min(c0 + CHUNK, len(blocks_sched)))))

        core_done = [(S("yfin"), ctx.val(S("yfin")))]
        st_core.close()

        with ExitStack() as st2:
            ridx_sb = st2.enter_context(nc.sbuf_tensor(pfx + "ridx", [128, 8], I32))
            yw = [st2.enter_context(nc.sbuf_tensor(f"{pfx}yw{k}", [128, TOWN], BF16)) for k in range(8)]
            wtmp2 = st2.enter_context(nc.sbuf_tensor(pfx + "owt", [128, D], F32))
            wo_bf = st2.enter_context(nc.sbuf_tensor(pfx + "wobf", [128, 8 * D], BF16))
            osb = [st2.enter_context(nc.sbuf_tensor(f"{pfx}osb{i}", [128, 512], F32)) for i in range(2)]
            xr = [st2.enter_context(nc.sbuf_tensor(f"{pfx}xr{i}", [128, D], F32)) for i in range(2)]
            xnew = [st2.enter_context(nc.sbuf_tensor(f"{pfx}xnew{i}", [128, D], F32)) for i in range(2)]
            if layer == 1:
                bo_f = st2.enter_context(nc.sbuf_tensor(pfx + "bof", [1, D], F32))
            ps_o = [st2.enter_context(nc.psum_tensor(f"{pfx}pso{i}", [128, 512], F32)) for i in range(2)]

            with nc.Block() as block:
                @block.gpsimd
                def _(gpsimd):
                    wait_deps(ctx, gpsimd, core_done)
                    for i in range(2):
                        ctx.inc(gpsimd.dma_start(out=y_dram.ap()[i * 128:(i + 1) * 128, :],
                                                 in_=y_cat[i][:]), S("yst"), 16)
                    gpsimd.wait_ge(ctx.s(S("yst")), 32)
                    ctx.inc(gpsimd.collective_compute(
                        "AllGather", OP.bypass, replica_groups=grp,
                        ins=[y_dram.ap()], outs=[Y_dram.ap()]), S("cc2"), 1)

                @block.sync
                def _(sync):
                    wait_deps(ctx, sync, core_done)
                    ctx.inc(sync.dma_start(out=ridx_sb[:], in_=ridx_d[:, :]), S("ridx"), 16)
                    for k in range(8):
                        if k >= 1:
                            ctx.wait(sync, S("woc"), k)
                        ctx.inc(sync.dma_start(out=wtmp2[:], in_=wo_d[k * 128:(k + 1) * 128, :]),
                                S("wol"), 16)
                    if layer == 1:
                        ctx.inc(sync.dma_start(out=bo_f[:], in_=bo_d[:, :]), S("bol"), 16)

                @block.vector
                def _(vector):
                    wait_deps(ctx, vector, core_done)
                    for k in range(8):
                        vector.wait_ge(ctx.s(S("wol")), 16 * (k + 1))
                        ctx.inc(vector.tensor_copy(wo_bf[:, k * D:(k + 1) * D], wtmp2[:]), S("woc"), 1)
                    if layer == 1:
                        vector.wait_ge(ctx.s(S("bol")), 16)
                        ctx.inc(vector.tensor_copy(bo_bf[:], bo_f[:]), S("boc"), 1)

                @block.gpsimd
                def _(gpsimd):
                    gpsimd.wait_ge(ctx.s(S("cc2")), 1)
                    gpsimd.wait_ge(ctx.s(S("ridx")), 16)
                    Yv = Y_dram.ap().rearrange("d (w t) -> (d w) t", w=4)
                    for k in range(8):
                        ctx.inc(gpsimd.indirect_dma_start(
                            out=yw[k][:], out_offset=None, in_=Yv,
                            in_offset=IndirectOffsetOnAxis(ap=ridx_sb[:, k:k + 1], axis=0)),
                            S("ywl"), 16)

                @block.sync
                def _(sync):
                    for j in range(NTILE):
                        if j >= 2:
                            ctx.wait(sync, S("res"), 2 * (j - 1))
                        ctx.inc(sync.dma_start(out=xr[j % 2][:], in_=xin[j * 128:(j + 1) * 128, :]),
                                S("xld") + str(j % 2), 16)

                @block.tensor
                def _(tensor):
                    tensor.wait_ge(ctx.s(S("ywl")), 16 * 8)
                    ctx.wait(tensor, S("woc"))
                    if layer == 1:
                        ctx.wait(tensor, S("boc"))
                        ctx.wait(tensor, S("ob"))
                    g = 0
                    for j in range(NTILE):
                        for n in range(2):
                            if g >= 2:
                                ctx.wait(tensor, S("oev"), g - 1)
                            for k in range(8):
                                inst = tensor.matmul(
                                    ps_o[g % 2][:],
                                    yw[k][:, j * 128:(j + 1) * 128],
                                    wo_bf[:, k * D + n * 512:k * D + (n + 1) * 512],
                                    start=(k == 0), stop=(layer == 0 and k == 7))
                            if layer == 1:
                                inst = tensor.matmul(ps_o[g % 2][:], ones_bf[:],
                                                     bo_bf[0:1, n * 512:(n + 1) * 512],
                                                     start=False, stop=True)
                            ctx.inc(inst, S("omm"), 1)
                            g += 1

                @block.scalar
                def _(scalar):
                    g = 0
                    for j in range(NTILE):
                        for n in range(2):
                            ctx.wait(scalar, S("omm"), g + 1)
                            if g >= 2:
                                ctx.wait(scalar, S("res"), g - 1)
                            ctx.inc(scalar.copy(osb[g % 2][:], ps_o[g % 2][:]), S("oev"), 1)
                            g += 1

                @block.vector
                def _(vector):
                    g = 0
                    for j in range(NTILE):
                        vector.wait_ge(ctx.s(S("xld") + str(j % 2)), 16 * (j // 2 + 1))
                        if j >= 2:
                            vector.wait_ge(ctx.s(S("out") + str(j % 2)), 16 * (j // 2))
                        for n in range(2):
                            ctx.wait(vector, S("oev"), g + 1)
                            ctx.inc(vector.tensor_add(xnew[j % 2][:, n * 512:(n + 1) * 512],
                                                      osb[g % 2][:],
                                                      xr[j % 2][:, n * 512:(n + 1) * 512]),
                                    S("res"), 1)
                            g += 1

                @block.gpsimd
                def _(gpsimd):
                    for j in range(NTILE):
                        ctx.wait(gpsimd, S("res"), 2 * (j + 1))
                        ctx.inc(gpsimd.dma_start(out=xout[j * 128:(j + 1) * 128, :], in_=xnew[j % 2][:]),
                                S("out") + str(j % 2), 16)

    return [(S("out0"), ctx.val(S("out0"))), (S("out1"), ctx.val(S("out1")))]


def emit_ffn0(ctx, io, xin, xout, pfx, consts):
    nc = ctx.nc
    gw_d = io.inp("l0_gwT", [D, NE0], F32)
    w1_d = io.inp("l0_w1", [2 * D, DFF0], F32)
    w2_d = io.inp("l0_w2", [2 * DFF0, D], F32)
    um_d = io.inp("ustrict", [128, 128], F32)
    iom_d = io.inp("iotam", [128, NE0], F32)
    isl_d = io.inp("islot8", [128, 8], F32)
    pm_d = io.inp("prevmask", [8, 1], F32)
    es_d = io.inp("eselr", [2, NE0], F32)
    sb_d = io.inp("sendbase", [128, NE0], F32)
    lt_d = io.inp("ltri8", [8, 8], F32)

    xln_dram = nc.dram_tensor(pfx + "xln", [TOWN, D], BF16)
    PADR = NCORES * 2 * SCAP
    send_dram = nc.dram_tensor(pfx + "send", [PADR + 8, D], BF16)
    recv_dram = nc.dram_tensor(pfx + "recv", [PADR + 8, D], BF16)
    disp_dram = nc.dram_tensor(pfx + "disp", [2 * CAP0, D], BF16)
    back_dram = nc.dram_tensor(pfx + "back", [PADR + 8, D], BF16)
    ret_dram = nc.dram_tensor(pfx + "ret", [PADR + 8, D], BF16)
    cnt_dram = nc.dram_tensor(pfx + "cnt", [1, NE0], F32)
    cag_dram = nc.dram_tensor(pfx + "cag", [8, NE0], F32)

    with ExitStack() as st_ln:
        ln_deps = emit_ln_sweep(ctx, st_ln, xin, xln_dram.ap(), pfx)

    S = lambda n: pfx + n
    NSLE = CAP0 // 128

    with ExitStack() as st:
        dst_sb = st.enter_context(nc.sbuf_tensor(pfx + "dst", [128, NTILE], I32))
        srcx_sb = st.enter_context(nc.sbuf_tensor(pfx + "srcx", [128, 2 * NSLE], I32))
        ph_sb = st.enter_context(nc.sbuf_tensor(pfx + "phv", [128, NTILE], F32))
        zt_sb = st.enter_context(nc.sbuf_tensor(pfx + "zt", [128, D], BF16))

        with ExitStack() as st2:
            xlnT = st2.enter_context(nc.sbuf_tensor(pfx + "xlnT", [128, 8 * TOWN], BF16))
            gwt = st2.enter_context(nc.sbuf_tensor(pfx + "gwt", [128, NE0], F32))
            gw_bf = st2.enter_context(nc.sbuf_tensor(pfx + "gwbf", [128, 8 * NE0], BF16))
            um_sb = st2.enter_context(nc.sbuf_tensor(pfx + "um", [128, 128], F32))
            iom_sb = st2.enter_context(nc.sbuf_tensor(pfx + "iom", [128, NE0], F32))
            isl_sb = st2.enter_context(nc.sbuf_tensor(pfx + "isl", [128, 8], F32))
            pm_sb = st2.enter_context(nc.sbuf_tensor(pfx + "pm", [8, 1], F32))
            es_sb = [st2.enter_context(nc.sbuf_tensor(f"{pfx}es{i}", [1, NE0], F32)) for i in range(2)]
            sb_sb = st2.enter_context(nc.sbuf_tensor(pfx + "sbase", [128, NE0], F32))
            lt_sb = st2.enter_context(nc.sbuf_tensor(pfx + "lt", [8, 8], F32))
            capt = st2.enter_context(nc.sbuf_tensor(pfx + "capt", [8, NE0], F32))
            c640 = st2.enter_context(nc.sbuf_tensor(pfx + "c640", [1, 8], F32))
            lg_sb = st2.enter_context(nc.sbuf_tensor(pfx + "lg", [128, NE0], F32))
            st_sb = st2.enter_context(nc.sbuf_tensor(pfx + "rstats", [128, 8], F32))
            ex_sb = st2.enter_context(nc.sbuf_tensor(pfx + "exs", [128, NE0], F32))
            msk_sb = st2.enter_context(nc.sbuf_tensor(pfx + "mask", [128, NTILE * NE0], F32))
            rnk_sb = st2.enter_context(nc.sbuf_tensor(pfx + "rnk", [128, NTILE * NE0], F32))
            carry = [st2.enter_context(nc.sbuf_tensor(f"{pfx}carry{i}", [1, NE0], F32)) for i in range(2)]
            cagg = st2.enter_context(nc.sbuf_tensor(pfx + "cagg", [8, NE0], F32))
            kcA = st2.enter_context(nc.sbuf_tensor(pfx + "kcA", [8, NE0], F32))
            kcB = st2.enter_context(nc.sbuf_tensor(pfx + "kcB", [8, NE0], F32))
            kc_sb = st2.enter_context(nc.sbuf_tensor(pfx + "kc", [8, NE0], F32))
            thr_sb = st2.enter_context(nc.sbuf_tensor(pfx + "thr", [128, NE0], F32))
            eselb = st2.enter_context(nc.sbuf_tensor(pfx + "eselb", [8, NE0], F32))
            kce = st2.enter_context(nc.sbuf_tensor(pfx + "kce", [8, 1], F32))
            cbrow = st2.enter_context(nc.sbuf_tensor(pfx + "cbrow", [1, 8], F32))
            kincb = st2.enter_context(nc.sbuf_tensor(pfx + "kincb", [128, 8], F32))
            cbb = st2.enter_context(nc.sbuf_tensor(pfx + "cbb", [128, 8], F32))
            dscr = st2.enter_context(nc.sbuf_tensor(pfx + "dscr", [128, 4 * NE0], F32))
            dcol = st2.enter_context(nc.sbuf_tensor(pfx + "dcol", [128, 8], F32))
            dgb = [st2.enter_context(nc.sbuf_tensor(f"{pfx}dgb{i}", [128, D], BF16)) for i in range(2)]
            xrl = st2.enter_context(nc.sbuf_tensor(pfx + "xrl", [128, D], BF16))
            ps_g = [st2.enter_context(nc.psum_tensor(f"{pfx}psg{i}", [128, NE0], F32)) for i in range(2)]
            ps_r = st2.enter_context(nc.psum_tensor(pfx + "psr", [128, NE0], F32))
            ps_c = st2.enter_context(nc.psum_tensor(pfx + "psc", [1, NE0], F32))
            ps_m = st2.enter_context(nc.psum_tensor(pfx + "psm", [8, NE0], F32))
            ps_b = st2.enter_context(nc.psum_tensor(pfx + "psb", [128, 16], F32))
            ps_t = st2.enter_context(nc.psum_tensor(pfx + "pstt", [8, 8], F32))

            with nc.Block() as block:
                @block.sync
                def _(sync):
                    wait_deps(ctx, sync, ln_deps)
                    for t, d_ in [(um_sb, um_d), (iom_sb, iom_d), (isl_sb, isl_d),
                                  (pm_sb, pm_d), (sb_sb, sb_d), (lt_sb, lt_d)]:
                        ctx.inc(sync.dma_start(out=t[:], in_=d_[:, :]), S("cl"), 16)
                    for i in range(2):
                        ctx.inc(sync.dma_start(out=es_sb[i][:], in_=es_d[i:i + 1, :]), S("cl"), 16)
                    for k in range(8):
                        if k >= 1:
                            ctx.wait(sync, S("gwc"), k)
                        ctx.inc(sync.dma_start(out=gwt[:], in_=gw_d[k * 128:(k + 1) * 128, :]),
                                S("gwl"), 16)
                        sync.wait_ge(ctx.s(S("gwl")), 16 * (k + 1))
                    for k in range(8):
                        ctx.inc(sync.dma_start(out=xlnT[:, k * TOWN:(k + 1) * TOWN],
                                               in_=xln_dram.ap()[:, k * 128:(k + 1) * 128],
                                               transpose=True), S("ldT"), 16)

                @block.vector
                def _(vector):
                    wait_deps(ctx, vector, ln_deps)
                    vector.memset(zt_sb[:], 0.0)
                    ctx.inc(vector.memset(carry[1][:], 0.0), S("cu"), 1)
                    vector.memset(capt[:], float(CAP0))
                    ctx.inc(vector.memset(c640[:], float(2 * SCAP)), S("cz"), 1)
                    for k in range(8):
                        vector.wait_ge(ctx.s(S("gwl")), 16 * (k + 1))
                        ctx.inc(vector.tensor_copy(gw_bf[:, k * NE0:(k + 1) * NE0], gwt[:]), S("gwc"), 1)

                @block.sync
                def _(sync):
                    ctx.wait(sync, S("cz"), 1)
                    for t in range(NCORES * 2 * SCAP // 128):
                        ctx.inc(sync.dma_start(out=send_dram.ap()[t * 128:(t + 1) * 128, :],
                                               in_=zt_sb[:]), S("sz"), 16)
                    ctx.inc(sync.dma_start(out=recv_dram.ap()[PADR:PADR + 8, :],
                                           in_=zt_sb[0:8, :]), S("pz"), 16)
                    ctx.inc(sync.dma_start(out=ret_dram.ap()[PADR:PADR + 8, :],
                                           in_=zt_sb[0:8, :]), S("pz"), 16)

            with nc.Block() as block:
                @block.tensor
                def _(tensor):
                    ctx.wait(tensor, S("gwc"))
                    tensor.wait_ge(ctx.s(S("ldT")), 16 * 8)
                    for j in range(NTILE):
                        if j >= 2:
                            ctx.wait(tensor, S("gev"), j - 1)
                        for k in range(8):
                            inst = tensor.matmul(
                                ps_g[j % 2][:],
                                xlnT[:, k * TOWN + j * 128:k * TOWN + (j + 1) * 128],
                                gw_bf[:, k * NE0:(k + 1) * NE0],
                                start=(k == 0), stop=(k == 7))
                        ctx.inc(inst, S("gmm"), 1)

                @block.scalar
                def _(scalar):
                    for j in range(NTILE):
                        ctx.wait(scalar, S("gmm"), j + 1)
                        if j >= 1:
                            ctx.wait(scalar, S("lgdone"), j)
                        ctx.inc(scalar.copy(lg_sb[:], ps_g[j % 2][:]), S("gev"), 1)
                        ctx.wait(scalar, S("rmax"), j + 1)
                        ctx.inc(scalar.activation(out=ex_sb[:], in_=lg_sb[:], func=AF.Exp,
                                                  bias=st_sb[:, 1:2]), S("gex"), 1)

                @block.vector
                def _(vector):
                    ch = S("rch")
                    for j in range(NTILE):
                        vector.wait_ge(ctx.s(S("gev")), j + 1)
                        ctx.inc(vector.reduce_max(st_sb[:, 0:1], lg_sb[:], axis=AX), ch, 1)
                        ctx.wait(vector, ch)
                        ctx.inc(vector.tensor_scalar_mul(st_sb[:, 1:2], st_sb[:, 0:1], -1.0), S("rmax"), 1)
                        ctx.wait(vector, S("gex"), j + 1)
                        ctx.inc(vector.reduce_sum(st_sb[:, 2:3], ex_sb[:], axis=AX), ch, 1)
                        ctx.wait(vector, ch)
                        ctx.inc(vector.reciprocal(st_sb[:, 3:4], st_sb[:, 2:3]), ch, 1)
                        ctx.wait(vector, ch)
                        ctx.inc(vector.tensor_scalar_mul(ph_sb[:, j:j + 1], st_sb[:, 3:4], 0.5), ch, 1)
                        ctx.inc(vector.tensor_single_scalar(msk_sb[:, j * NE0:(j + 1) * NE0],
                                                            lg_sb[:], st_sb[:, 0:1], OP.is_equal),
                                S("lgdone"), 1)

            with nc.Block() as block:
                @block.tensor
                def _(tensor):
                    ctx.wait(tensor, S("cl"), ctx.val(S("cl")))
                    for j in range(NTILE):
                        ctx.wait(tensor, S("lgdone"), j + 1)
                        ctx.wait(tensor, S("cu"), j + 1)
                        if j >= 1:
                            ctx.wait(tensor, S("rnkev"), j)
                        tensor.matmul(ps_r[:], um_sb[:], msk_sb[:, j * NE0:(j + 1) * NE0],
                                      start=True, stop=False)
                        inst = tensor.matmul(ps_r[:], consts["ones1_f32"][:],
                                             carry[(j + 1) % 2][:], start=False, stop=True)
                        ctx.inc(inst, S("rnkmm"), 1)
                        inst = tensor.matmul(ps_c[:], consts["onescol_f32"][:],
                                             msk_sb[:, j * NE0:(j + 1) * NE0],
                                             start=True, stop=True)
                        ctx.inc(inst, S("cntmm"), 1)

                @block.scalar
                def _(scalar):
                    for j in range(NTILE):
                        ctx.wait(scalar, S("rnkmm"), j + 1)
                        ctx.inc(scalar.copy(rnk_sb[:, j * NE0:(j + 1) * NE0], ps_r[:]), S("rnkev"), 1)

                @block.vector
                def _(vector):
                    for j in range(NTILE):
                        ctx.wait(vector, S("cntmm"), j + 1)
                        ctx.inc(vector.tensor_add(carry[j % 2][:], carry[(j + 1) % 2][:], ps_c[:]),
                                S("cu"), 1)

                @block.gpsimd
                def _(gpsimd):
                    ctx.wait(gpsimd, S("cu"), NTILE + 1)
                    ctx.inc(gpsimd.dma_start(out=cnt_dram.ap(), in_=carry[(NTILE + 1) % 2][:]),
                            S("cst"), 16)
                    gpsimd.wait_ge(ctx.s(S("cst")), 16)
                    ctx.inc(gpsimd.collective_compute(
                        "AllGather", OP.bypass, replica_groups=[list(range(NCORES))],
                        ins=[cnt_dram.ap()], outs=[cag_dram.ap()]), S("ccg"), 1)
                    ctx.wait(gpsimd, S("ccg"))
                    ctx.inc(gpsimd.dma_start(out=cagg[:], in_=cag_dram.ap()), S("cagl"), 16)

            with nc.Block() as block:
                @block.tensor
                def _(tensor):
                    tensor.wait_ge(ctx.s(S("cagl")), 16)
                    ctx.wait(tensor, S("cntmm"), NTILE)
                    ctx.wait(tensor, S("cu"), NTILE + 1)
                    inst = tensor.matmul(ps_c[:], pm_sb[:], cagg[:], start=True, stop=True)
                    ctx.inc(inst, S("pcm"), 1)
                    inst = tensor.matmul(ps_m[:], um_sb[0:8, 0:8], cagg[:], start=True, stop=True)
                    ctx.inc(inst, S("kcm"), 1)

                @block.vector
                def _(vector):
                    ctx.wait(vector, S("pcm"), 1)
                    ctx.inc(vector.scalar_tensor_tensor(dscr[0:1, 0:NE0], ps_c[:], -1.0,
                                                        capt[0:1, :], OP.mult, OP.add), S("th1"), 1)
                    ctx.wait(vector, S("kcm"), 1)
                    ctx.inc(vector.scalar_tensor_tensor(kcA[:], ps_m[:], -1.0, capt[:],
                                                        OP.mult, OP.add), S("kk1"), 1)
                    ctx.wait(vector, S("kk1"))
                    ctx.inc(vector.tensor_scalar_max(kcB[:], kcA[:], 0.0), S("kk1"), 1)
                    ctx.wait(vector, S("kk1"))
                    ctx.inc(vector.tensor_tensor(kc_sb[:], kcB[:], cagg[:], OP.min), S("kk1"), 1)

                @block.tensor
                def _(tensor):
                    ctx.wait(tensor, S("th1"), 1)
                    inst = tensor.matmul(ps_r[:], consts["ones1_f32"][:], dscr[0:1, 0:NE0],
                                         start=True, stop=True)
                    ctx.inc(inst, S("thm"), 1)

                @block.scalar
                def _(scalar):
                    ctx.wait(scalar, S("thm"), 1)
                    ctx.inc(scalar.copy(thr_sb[:], ps_r[:]), S("thc"), 1)

            with nc.Block() as block:
                @block.vector
                def _(vector):
                    ch = S("dch")
                    ctx.wait(vector, S("thc"), 1)
                    for j in range(NTILE):
                        mj = msk_sb[:, j * NE0:(j + 1) * NE0]
                        rj = rnk_sb[:, j * NE0:(j + 1) * NE0]
                        ctx.wait(vector, S("rnkev"), j + 1)
                        ctx.inc(vector.tensor_mul(dscr[:, 0:NE0], rj, mj), ch, 1)
                        ctx.inc(vector.tensor_mul(dscr[:, NE0:2 * NE0], sb_sb[:], mj), ch, 1)
                        ctx.inc(vector.tensor_tensor(dscr[:, 2 * NE0:3 * NE0], rj, thr_sb[:],
                                                     OP.is_lt), ch, 1)
                        ctx.wait(vector, ch)
                        ctx.inc(vector.tensor_mul(dscr[:, 3 * NE0:4 * NE0],
                                                  dscr[:, 2 * NE0:3 * NE0], mj), ch, 1)
                        ctx.inc(vector.reduce_sum(dcol[:, 0:1], dscr[:, 0:NE0], axis=AX), ch, 1)
                        ctx.inc(vector.reduce_sum(dcol[:, 1:2], dscr[:, NE0:2 * NE0], axis=AX), ch, 1)
                        ctx.wait(vector, ch)
                        ctx.inc(vector.reduce_sum(dcol[:, 2:3], dscr[:, 3 * NE0:4 * NE0], axis=AX),
                                ch, 1)
                        ctx.wait(vector, ch)
                        ctx.inc(vector.tensor_scalar(dcol[:, 3:4], dcol[:, 2:3], 1.0, -1e9,
                                                     OP.subtract, OP.mult), ch, 1)
                        ctx.inc(vector.tensor_scalar(dcol[:, 4:5], dcol[:, 0:1], float(SCAP), 1e9,
                                                     OP.is_ge, OP.mult), ch, 1)
                        ctx.wait(vector, ch)
                        ctx.inc(vector.tensor_add(dcol[:, 5:6], dcol[:, 0:1], dcol[:, 1:2]), ch, 1)
                        ctx.wait(vector, ch)
                        ctx.inc(vector.tensor_add(dcol[:, 6:7], dcol[:, 5:6], dcol[:, 3:4]), ch, 1)
                        ctx.wait(vector, ch)
                        ctx.inc(vector.tensor_add(dcol[:, 7:8], dcol[:, 6:7], dcol[:, 4:5]), ch, 1)
                        ctx.wait(vector, ch)
                        ctx.inc(vector.tensor_scalar_min(dcol[:, 6:7], dcol[:, 7:8],
                                                         float(PADR)), ch, 1)
                        ctx.wait(vector, ch)
                        ctx.inc(vector.tensor_copy(dst_sb[:, j:j + 1], dcol[:, 6:7]), S("dstd"), 1)

            for sl in range(2):
                with nc.Block() as block:
                    @block.tensor
                    def _(tensor):
                        if sl == 0:
                            ctx.wait(tensor, S("kk1"), 3)
                        else:
                            ctx.wait(tensor, S("srcd"), NSLE)
                        inst = tensor.matmul(ps_m[:], consts["ones1_f32"][:, 0:8],
                                             es_sb[sl][:], start=True, stop=True)
                        ctx.inc(inst, S("esm"), 1)
                        ctx.wait(tensor, S("kce"), sl + 1)
                        inst = tensor.transpose(ps_c[0:1, 0:8], kce[:],
                                                consts["ident_f32"][0:8, 0:8])
                        ctx.inc(inst, S("ktr"), 1)
                        inst = tensor.matmul(ps_t[:, 0:1], lt_sb[:], kce[:], start=True, stop=True)
                        ctx.inc(inst, S("kim"), 1)
                        ctx.wait(tensor, S("kiev"), sl + 1)
                        inst = tensor.transpose(ps_c[0:1, 8:16], dcol[0:8, 0:1],
                                                consts["ident_f32"][0:8, 0:8])
                        ctx.inc(inst, S("kirt"), 1)
                        ctx.wait(tensor, S("x1"), sl + 1)
                        inst = tensor.matmul(ps_b[:, 0:8], consts["ones1_f32"][:],
                                             dscr[0:1, 32:40], start=True, stop=True)
                        ctx.inc(inst, S("kbm"), 1)
                        ctx.wait(tensor, S("cbr"), sl + 1)
                        inst = tensor.matmul(ps_b[:, 8:16], consts["ones1_f32"][:], cbrow[:],
                                             start=True, stop=True)
                        ctx.inc(inst, S("cbm"), 1)

                    @block.scalar
                    def _(scalar):
                        ctx.wait(scalar, S("esm"), sl + 1)
                        ctx.inc(scalar.copy(eselb[:], ps_m[:]), S("esb"), 1)
                        ctx.wait(scalar, S("ktr"), sl + 1)
                        ctx.inc(scalar.copy(dscr[0:1, 40:48], ps_c[0:1, 0:8]), S("x2"), 1)
                        ctx.wait(scalar, S("kim"), sl + 1)
                        ctx.inc(scalar.copy(dcol[0:8, 0:1], ps_t[:, 0:1]), S("kiev"), 1)
                        ctx.wait(scalar, S("kirt"), sl + 1)
                        ctx.inc(scalar.copy(dscr[0:1, 32:40], ps_c[0:1, 8:16]), S("x1"), 1)
                        ctx.wait(scalar, S("kbm"), sl + 1)
                        ctx.inc(scalar.copy(kincb[:], ps_b[:, 0:8]), S("kbc"), 1)
                        ctx.wait(scalar, S("cbm"), sl + 1)
                        ctx.inc(scalar.copy(cbb[:], ps_b[:, 8:16]), S("cbc"), 1)

                    @block.vector
                    def _(vector):
                        ch = S("sch")
                        ctx.wait(vector, S("esb"), sl + 1)
                        ctx.inc(vector.tensor_mul(dscr[0:8, 0:NE0], kc_sb[:], eselb[:]), ch, 1)
                        ctx.wait(vector, ch)
                        ctx.inc(vector.reduce_sum(kce[:], dscr[0:8, 0:NE0], axis=AX),
                                S("kce"), 1)
                        ctx.wait(vector, S("x2"), sl + 1)
                        ctx.inc(vector.scalar_tensor_tensor(cbrow[:], dscr[0:1, 40:48], -1.0,
                                                            c640[:], OP.mult, OP.add),
                                S("cbr"), 1)
                        ctx.wait(vector, S("kbc"), sl + 1)
                        ctx.wait(vector, S("cbc"), sl + 1)
                        for t in range(NSLE):
                            ctx.inc(vector.tensor_scalar_add(dcol[:, 1:2], isl_sb[:, 0:1],
                                                             float(128 * t)), ch, 1)
                            ctx.wait(vector, ch)
                            ctx.inc(vector.tensor_single_scalar(dscr[:, 0:8], kincb[:],
                                                                dcol[:, 1:2], OP.is_le), ch, 1)
                            ctx.wait(vector, ch)
                            ctx.inc(vector.tensor_mul(dscr[:, 8:16], dscr[:, 0:8], cbb[:]), ch, 1)
                            ctx.wait(vector, ch)
                            ctx.inc(vector.reduce_sum(dcol[:, 2:3], dscr[:, 8:16], axis=AX), ch, 1)
                            ctx.wait(vector, ch)
                            ctx.inc(vector.tensor_add(dcol[:, 3:4], dcol[:, 1:2], dcol[:, 2:3]),
                                    ch, 1)
                            ctx.wait(vector, ch)
                            ctx.inc(vector.tensor_scalar_add(dcol[:, 4:5], dcol[:, 3:4],
                                                             float(SCAP * sl)), ch, 1)
                            ctx.wait(vector, ch)
                            ctx.inc(vector.tensor_scalar_min(dcol[:, 5:6], dcol[:, 4:5],
                                                             float(PADR)), ch, 1)
                            ctx.wait(vector, ch)
                            ctx.inc(vector.tensor_copy(srcx_sb[:, sl * NSLE + t:sl * NSLE + t + 1],
                                                       dcol[:, 5:6]), S("srcd"), 1)

            with nc.Block() as block:
                @block.sync
                def _(sync):
                    for j in range(NTILE):
                        if j > 0:
                            ctx.wait(sync, S("sct"), 16 * j)
                        ctx.inc(sync.dma_start(out=xrl[:], in_=xln_dram.ap()[j * 128:(j + 1) * 128, :]),
                                S("xrl"), 16)

                @block.gpsimd
                def _(gpsimd):
                    gpsimd.wait_ge(ctx.s(S("sz")), ctx.val(S("sz")))
                    ctx.wait(gpsimd, S("dstd"), NTILE)
                    for j in range(NTILE):
                        gpsimd.wait_ge(ctx.s(S("xrl")), 16 * (j + 1))
                        ctx.inc(gpsimd.indirect_dma_start(
                            out=send_dram.ap(), out_offset=IndirectOffsetOnAxis(
                                ap=dst_sb[:, j:j + 1], axis=0),
                            in_=xrl[:], in_offset=None), S("sct"), 16)
                    gpsimd.wait_ge(ctx.s(S("sct")), 16 * NTILE)
                    with allow_ici_all_to_all():
                        ctx.inc(gpsimd.collective_compute(
                            "AllToAll", OP.bypass, replica_groups=[list(range(NCORES))],
                            ins=[send_dram.ap()[0:PADR, :]],
                            outs=[recv_dram.ap()[0:PADR, :]]), S("a2a1"), 1)

            with nc.Block() as block:
                @block.vector
                def _(vector):
                    for t in range(2 * NSLE):
                        if t >= 2:
                            vector.wait_ge(ctx.s(S("dgs") + str(t % 2)), 16 * (t // 2))
                        ctx.inc(vector.memset(dgb[t % 2][:], 0.0), S("dgz"), 1)

                @block.sync
                def _(sync):
                    ctx.wait(sync, S("cz"), 1)
                    for t in range(NCORES * 2 * SCAP // 128):
                        ctx.inc(sync.dma_start(out=back_dram.ap()[t * 128:(t + 1) * 128, :],
                                               in_=zt_sb[:]), S("bz"), 16)

                @block.gpsimd
                def _(gpsimd):
                    ctx.wait(gpsimd, S("a2a1"))
                    ctx.wait(gpsimd, S("pz"))
                    ctx.wait(gpsimd, S("srcd"), 2 * NSLE)
                    for t in range(2 * NSLE):
                        ctx.wait(gpsimd, S("dgz"), t + 1)
                        ctx.inc(gpsimd.indirect_dma_start(
                            out=dgb[t % 2][:], out_offset=None,
                            in_=recv_dram.ap(), in_offset=IndirectOffsetOnAxis(
                                ap=srcx_sb[:, t:t + 1], axis=0)),
                            S("dgg") + str(t % 2), 16)

                @block.sync
                def _(sync):
                    for t in range(2 * NSLE):
                        sync.wait_ge(ctx.s(S("dgg") + str(t % 2)), 16 * (t // 2 + 1))
                        ctx.inc(sync.dma_start(out=disp_dram.ap()[t * 128:(t + 1) * 128, :],
                                               in_=dgb[t % 2][:]), S("dgs") + str(t % 2), 16)


        with ExitStack() as st2:
            DT = st2.enter_context(nc.sbuf_tensor(pfx + "DT", [128, 8 * 2 * CAP0], BF16))
            wt4 = [st2.enter_context(nc.sbuf_tensor(f"{pfx}wt4{i}", [128, DFF0 // 2], F32))
                   for i in range(2)]
            wbf = st2.enter_context(nc.sbuf_tensor(pfx + "wbf", [128, 8 * DFF0], BF16))
            HCAP = CAP0 // 2
            hT = st2.enter_context(nc.sbuf_tensor(pfx + "hT", [128, 32 * HCAP], BF16))
            ob_sb = [st2.enter_context(nc.sbuf_tensor(f"{pfx}ob{i}", [128, D], BF16)) for i in range(2)]
            ps_h = [st2.enter_context(nc.psum_tensor(f"{pfx}psh{i}", [128, 512], F32)) for i in range(2)]
            ps_o = [st2.enter_context(nc.psum_tensor(f"{pfx}pso{i}", [128, 512], F32)) for i in range(2)]

            with nc.Block() as block:
                @block.sync
                def _(sync):
                    ctx.wait(sync, S("dgs0"), ctx.val(S("dgs0")))
                    ctx.wait(sync, S("dgs1"), ctx.val(S("dgs1")))
                    for k in range(8):
                        ctx.inc(sync.dma_start(out=DT[:, k * 2 * CAP0:(k + 1) * 2 * CAP0],
                                               in_=disp_dram.ap()[:, k * 128:(k + 1) * 128],
                                               transpose=True), S("dT"), 16)

            SB_H = [(b0, min(512, HCAP - b0)) for b0 in range(0, HCAP, 512)]
            NBH = len(SB_H)
            NSLH = HCAP // 128
            cnt = {"w1l": 0, "w1c": 0, "w2l": 0, "w2c": 0, "hmm": 0, "hev": 0,
                   "omm": 0, "oev": 0, "obst": 0}
            for sl in range(2):
                for sh in range(2):
                    with nc.Block() as block:
                        @block.sync
                        def _(sync):
                            if cnt["hmm"] > 0:
                                ctx.wait(sync, S("hmm"), cnt["hmm"])
                            if cnt["omm"] > 0:
                                ctx.wait(sync, S("omm"), cnt["omm"])
                            for kc in range(8):
                                for hf in range(2):
                                    g = cnt["w1l"] + kc * 2 + hf
                                    if kc * 2 + hf >= 2:
                                        ctx.wait(sync, S("w1c"), g - 1)
                                    ctx.inc(sync.dma_start(
                                        out=wt4[g % 2][:],
                                        in_=w1_d[sl * D + kc * 128:sl * D + (kc + 1) * 128,
                                                 hf * 2048:(hf + 1) * 2048]), S("w1l"), 16)
                                    sync.wait_ge(ctx.s(S("w1l")), 16 * (g + 1))

                        @block.vector
                        def _(vector):
                            for kc in range(8):
                                for hf in range(2):
                                    g = cnt["w1l"] + kc * 2 + hf
                                    vector.wait_ge(ctx.s(S("w1l")), 16 * (g + 1))
                                    ctx.inc(vector.tensor_copy(
                                        wbf[:, kc * DFF0 + hf * 2048:kc * DFF0 + (hf + 1) * 2048],
                                        wt4[g % 2][:]), S("w1c"), 1)
                    cnt["w1l"] += 16
                    cnt["w1c"] += 16

                    for fgrp in range(4):
                        with nc.Block() as block:
                            @block.tensor
                            def _(tensor):
                                if fgrp == 0:
                                    ctx.wait(tensor, S("w1c"), cnt["w1c"])
                                    if sl == 0 and sh == 0:
                                        tensor.wait_ge(ctx.s(S("dT")), 16 * 8)
                                    if cnt["oev"] > 0:
                                        ctx.wait(tensor, S("oev"), cnt["oev"])
                                for f in range(fgrp * 8, (fgrp + 1) * 8):
                                    for bi, (b0, bw) in enumerate(SB_H):
                                        g = cnt["hmm"] + (f - 0) * NBH + bi
                                        if g >= 2:
                                            ctx.wait(tensor, S("hev"), g - 1)
                                        base = sl * CAP0 + sh * HCAP + b0
                                        for k in range(8):
                                            inst = tensor.matmul(
                                                ps_h[g % 2][:, 0:bw],
                                                wbf[:, k * DFF0 + f * 128:k * DFF0 + (f + 1) * 128],
                                                DT[:, k * 2 * CAP0 + base:k * 2 * CAP0 + base + bw],
                                                start=(k == 0), stop=(k == 7))
                                        ctx.inc(inst, S("hmm"), 1)

                            @block.scalar
                            def _(scalar):
                                for f in range(fgrp * 8, (fgrp + 1) * 8):
                                    for bi, (b0, bw) in enumerate(SB_H):
                                        g = cnt["hmm"] + f * NBH + bi
                                        ctx.wait(scalar, S("hmm"), g + 1)
                                        ctx.inc(scalar.activation(
                                            out=hT[:, f * HCAP + b0:f * HCAP + b0 + bw],
                                            in_=ps_h[g % 2][:, 0:bw], func=AF.Gelu_apprx_tanh),
                                            S("hev"), 1)
                    cnt["hmm"] += 32 * NBH
                    cnt["hev"] += 32 * NBH

                    with nc.Block() as block:
                        @block.sync
                        def _(sync):
                            ctx.wait(sync, S("hmm"), cnt["hmm"])
                            for kc in range(32):
                                g = cnt["w2l"] + kc
                                if kc >= 2:
                                    ctx.wait(sync, S("w2c"), g - 1)
                                ctx.inc(sync.dma_start(
                                    out=wt4[g % 2][:, 0:1024],
                                    in_=w2_d[sl * DFF0 + kc * 128:sl * DFF0 + (kc + 1) * 128, :]),
                                    S("w2l"), 16)
                                sync.wait_ge(ctx.s(S("w2l")), 16 * (g + 1))

                        @block.vector
                        def _(vector):
                            for kc in range(32):
                                g = cnt["w2l"] + kc
                                vector.wait_ge(ctx.s(S("w2l")), 16 * (g + 1))
                                ctx.inc(vector.tensor_copy(wbf[:, kc * 1024:(kc + 1) * 1024],
                                                           wt4[g % 2][:, 0:1024]), S("w2c"), 1)
                    cnt["w2l"] += 32
                    cnt["w2c"] += 32

                    with nc.Block() as block:
                        @block.tensor
                        def _(tensor):
                            ctx.wait(tensor, S("w2c"), cnt["w2c"])
                            ctx.wait(tensor, S("hev"), cnt["hev"])
                            for t in range(NSLH):
                                for n in range(2):
                                    g = cnt["omm"] + t * 2 + n
                                    if g >= 2:
                                        ctx.wait(tensor, S("oev"), g - 1)
                                    for f in range(32):
                                        inst = tensor.matmul(
                                            ps_o[g % 2][:],
                                            hT[:, f * HCAP + t * 128:f * HCAP + (t + 1) * 128],
                                            wbf[:, f * 1024 + n * 512:f * 1024 + (n + 1) * 512],
                                            start=(f == 0), stop=(f == 31))
                                    ctx.inc(inst, S("omm"), 1)

                        @block.scalar
                        def _(scalar):
                            for t in range(NSLH):
                                for n in range(2):
                                    g = cnt["omm"] + t * 2 + n
                                    ctx.wait(scalar, S("omm"), g + 1)
                                    gt = cnt["obst"] // 16 + t
                                    if gt >= 2:
                                        ctx.wait(scalar, S("obst"), 16 * (gt - 1))
                                    ctx.inc(scalar.copy(ob_sb[t % 2][:, n * 512:(n + 1) * 512],
                                                        ps_o[g % 2][:]), S("oev"), 1)

                        @block.gpsimd
                        def _(gpsimd):
                            if sl == 0 and sh == 0:
                                gpsimd.wait_ge(ctx.s(S("bz")), ctx.val(S("bz")))
                            for t in range(NSLH):
                                g2 = cnt["omm"] + t * 2
                                ctx.wait(gpsimd, S("oev"), g2 + 2)
                                col = sl * NSLE + sh * NSLH + t
                                ctx.inc(gpsimd.indirect_dma_start(
                                    out=back_dram.ap(), out_offset=IndirectOffsetOnAxis(
                                        ap=srcx_sb[:, col:col + 1], axis=0),
                                    in_=ob_sb[t % 2][:], in_offset=None),
                                    S("obst"), 16)
                    cnt["omm"] += 2 * NSLH
                    cnt["oev"] += 2 * NSLH
                    cnt["obst"] += 16 * NSLH

        with ExitStack() as st2:
            rws = st2.enter_context(nc.sbuf_tensor(pfx + "rws", [128, D], BF16))
            r32 = st2.enter_context(nc.sbuf_tensor(pfx + "r32", [128, D], F32))
            rt2 = st2.enter_context(nc.sbuf_tensor(pfx + "rt2", [128, D], F32))
            xr = [st2.enter_context(nc.sbuf_tensor(f"{pfx}xr{i}", [128, D], F32)) for i in range(2)]
            xnew = [st2.enter_context(nc.sbuf_tensor(f"{pfx}xnw{i}", [128, D], F32)) for i in range(2)]

            with nc.Block() as block:
                @block.gpsimd
                def _(gpsimd):
                    gpsimd.wait_ge(ctx.s(S("obst")), ctx.val(S("obst")))
                    with allow_ici_all_to_all():
                        ctx.inc(gpsimd.collective_compute(
                            "AllToAll", OP.bypass, replica_groups=[list(range(NCORES))],
                            ins=[back_dram.ap()[0:PADR, :]],
                            outs=[ret_dram.ap()[0:PADR, :]]), S("a2a2"), 1)

                @block.vector
                def _(vector):
                    pass

                @block.sync
                def _(sync):
                    for j in range(NTILE):
                        if j >= 2:
                            ctx.wait(sync, S("cres"), j - 1)
                        ctx.inc(sync.dma_start(out=xr[j % 2][:], in_=xin[j * 128:(j + 1) * 128, :]),
                                S("cxl") + str(j % 2), 16)

                @block.gpsimd
                def _(gpsimd):
                    ctx.wait(gpsimd, S("a2a2"))
                    gpsimd.wait_ge(ctx.s(S("pz")), 32)
                    for j in range(NTILE):
                        ctx.wait(gpsimd, S("rz"), j + 1)
                        ctx.inc(gpsimd.indirect_dma_start(
                            out=rws[:], out_offset=None,
                            in_=ret_dram.ap(), in_offset=IndirectOffsetOnAxis(
                                ap=dst_sb[:, j:j + 1], axis=0)), S("rg"), 16)
                        if j >= 1:
                            ctx.wait(gpsimd, S("cres"), j)
                            ctx.inc(gpsimd.dma_start(out=xout[(j - 1) * 128:j * 128, :],
                                                     in_=xnew[(j - 1) % 2][:]),
                                    S("cout") + str((j - 1) % 2), 16)
                    ctx.wait(gpsimd, S("cres"), NTILE)
                    ctx.inc(gpsimd.dma_start(out=xout[(NTILE - 1) * 128:NTILE * 128, :],
                                             in_=xnew[(NTILE - 1) % 2][:]),
                            S("cout") + str((NTILE - 1) % 2), 16)

                @block.vector
                def _(vector):
                    ch = S("cch")
                    ctx.inc(vector.memset(rws[:], 0.0), S("rz"), 1)
                    for j in range(NTILE):
                        vector.wait_ge(ctx.s(S("rg")), 16 * (j + 1))
                        ctx.inc(vector.tensor_copy(r32[:], rws[:]), ch, 1)
                        ctx.wait(vector, ch)
                        if j < NTILE - 1:
                            ctx.inc(vector.memset(rws[:], 0.0), S("rz"), 1)
                        ctx.inc(vector.tensor_scalar_mul(rt2[:], r32[:], ph_sb[:, j:j + 1]), ch, 1)
                        vector.wait_ge(ctx.s(S("cxl") + str(j % 2)), 16 * (j // 2 + 1))
                        if j >= 2:
                            vector.wait_ge(ctx.s(S("cout") + str(j % 2)), 16 * (j // 2))
                        ctx.wait(vector, ch)
                        ctx.inc(vector.tensor_add(xnew[j % 2][:], rt2[:], xr[j % 2][:]), S("cres"), 1)

    return [(S("cout0"), ctx.val(S("cout0"))), (S("cout1"), ctx.val(S("cout1")))]


def build_consts(ctx, st, io):
    nc = ctx.nc
    ident = st.enter_context(nc.sbuf_tensor("ident_f32", [128, 128], F32))
    ones1 = st.enter_context(nc.sbuf_tensor("ones1_f32", [1, 128], F32))
    onescol = st.enter_context(nc.sbuf_tensor("onescol_f32", [128, 1], F32))
    epsc = st.enter_context(nc.sbuf_tensor("eps_col", [128, 1], F32))
    ident_d = io.inp("ident", [128, 128], F32)
    with nc.Block() as block:
        @block.sync
        def _(sync):
            ctx.inc(sync.dma_start(out=ident[:], in_=ident_d[:, :]), "cinit", 16)

        @block.vector
        def _(vector):
            vector.memset(ones1[:], 1.0)
            vector.memset(onescol[:], 1.0)
            ctx.inc(vector.memset(epsc[:], EPS), "cinit", 1)
    consts = {"ident_f32": ident, "ones1_f32": ones1, "eps_col": epsc, "onescol_f32": onescol}
    ctx.consts = consts
    return consts


def emit_lndump(ctx, io, xin, xout, pfx, consts):
    nc = ctx.nc
    xln_dram = nc.dram_tensor(pfx + "xln", [TOWN, D], BF16)
    rowin = nc.dram_tensor(pfx + "rowin", [1, D], F32)
    with ExitStack() as st_ln:
        ln_deps = emit_ln_sweep(ctx, st_ln, xin, xln_dram.ap(), pfx, row0_dram=rowin.ap())
    with ExitStack() as st:
        t_b = st.enter_context(nc.sbuf_tensor(pfx + "tb", [128, D], BF16))
        t_f = st.enter_context(nc.sbuf_tensor(pfx + "tf", [128, D], F32))
        with nc.Block() as block:
            @block.sync
            def _(sync):
                wait_deps(ctx, sync, ln_deps)
                for j in range(NTILE):
                    if j > 0:
                        ctx.wait(sync, pfx + "dcp", j)
                    ctx.inc(sync.dma_start(out=t_b[:], in_=xln_dram.ap()[j * 128:(j + 1) * 128, :]),
                            pfx + "dld", 16)

            @block.vector
            def _(vector):
                for j in range(NTILE):
                    vector.wait_ge(ctx.s(pfx + "dld"), 16 * (j + 1))
                    if j > 0:
                        ctx.wait(vector, pfx + "dout", 16 * j)
                    ctx.inc(vector.tensor_copy(t_f[:], t_b[:]), pfx + "dcp", 1)

            @block.gpsimd
            def _(gpsimd):
                for j in range(NTILE):
                    gpsimd.wait_ge(ctx.s(pfx + "dcp"), j + 1)
                    ctx.inc(gpsimd.dma_start(out=xout[j * 128:(j + 1) * 128, :], in_=t_f[:]),
                            pfx + "dout", 16)
    return [(pfx + "dout", ctx.val(pfx + "dout"))]


PHASE_EMITTERS = {
    "lndump": emit_lndump,
    "attn0": lambda ctx, io, xi, xo, pfx, c: emit_attn(ctx, io, xi, xo, pfx, c, 0),
    "attn1": lambda ctx, io, xi, xo, pfx, c: emit_attn(ctx, io, xi, xo, pfx, c, 1),
    "ffn0": emit_ffn0,
}


def build_program(phases):
    nc = bass.Bass("TRN2", target_bir_lowering=False, debug=False, detect_race_conditions=False,
                   dynamic_dma_scratch_size=16384)
    ctx = Ctx(nc)
    io = IO(nc)
    with ExitStack() as st:
        consts = build_consts(ctx, st, io)
        xin = io.inp("x", [TOWN, D], F32)
        xout = io.out("y", [TOWN, D], F32)
        n = len(phases)
        bufs = [xin.ap()]
        for i in range(1, n):
            bufs.append(nc.dram_tensor(f"xmid{i}", [TOWN, D], F32).ap())
        bufs.append(xout.ap())
        prev = None
        for i, ph in enumerate(phases):
            pfx = f"p{i}_"
            if prev is not None:
                deps = prev
                with nc.Block() as block:
                    @block.sync
                    def _(sync):
                        wait_deps(ctx, sync, deps)

                    @block.gpsimd
                    def _(gpsimd):
                        wait_deps(ctx, gpsimd, deps)
            if ph == "ffn1":
                prev = emit_ffn1(ctx, io, bufs[i], bufs[i + 1], pfx, consts)
            elif ph in PHASE_EMITTERS:
                prev = PHASE_EMITTERS[ph](ctx, io, bufs[i], bufs[i + 1], pfx, consts)
            else:
                raise NotImplementedError(ph)
        with nc.Block() as block:
            @block.gpsimd
            def _(gpsimd):
                wait_deps(ctx, gpsimd, prev)
    return nc, io


def make_core_inputs(inputs, c, names):
    m = {}
    if "ident" in names:
        m["ident"] = np.eye(128, dtype=np.float32)
    if "iota8" in names:
        m["iota8"] = np.arange(8, dtype=np.float32).reshape(1, 8)
    if "rowidx" in names:
        m["rowidx"] = np.array([[(c // 4) * 4], [(c // 4) * 4]], dtype=np.int32)
    if "x" in names:
        x = np.asarray(inputs["L0_attn_x"], np.float32).reshape(2 * TBATCH, D)
        m["x"] = np.ascontiguousarray(x[c * TOWN:(c + 1) * TOWN])
    if "l0_wqkv" in names or "l1_wqkvT" in names:
        q = c % 4
        H = [4 * q + i for i in range(4)]
        if "l0_wqkv" in names:
            W = np.asarray(inputs["L0_attn_W_qkv"], np.float32)
            pk = np.zeros((D, 768), np.float32)
            for i, h in enumerate(H):
                pk[:, 128 * i:128 * i + 64] = W[:, h * 64:(h + 1) * 64]
                pk[:, 128 * i + 64:128 * i + 128] = W[:, D + h * 64:D + (h + 1) * 64]
                pk[:, 512 + 64 * i:512 + 64 * (i + 1)] = W[:, 2 * D + h * 64:2 * D + (h + 1) * 64]
            m["l0_wqkv"] = pk
            m["l0_wo_full"] = np.ascontiguousarray(np.asarray(inputs["L0_attn_W_o"], np.float32))
            half = 32
            ii = np.arange(half, dtype=np.float64)
            theta = 1.0 / (10000.0 ** (ii / half))
            t = np.arange(TBATCH, dtype=np.float64)
            ang = t[:, None] * theta[None, :]
            cos = np.cos(ang).T.astype(np.float32)
            sin = np.sin(ang).T.astype(np.float32)
            m["rope_cos"] = np.concatenate([cos, cos], axis=0).astype(ml_dtypes.bfloat16)
            m["rope_sin"] = np.concatenate([-sin, sin], axis=0).astype(ml_dtypes.bfloat16)
        if "l1_wqkvT" in names:
            Wq = np.asarray(inputs["L1_attn_Wq"], np.float32)
            Wk = np.asarray(inputs["L1_attn_Wk"], np.float32)
            Wv = np.asarray(inputs["L1_attn_Wv"], np.float32)
            bq = np.asarray(inputs["L1_attn_bq"], np.float32)
            bk = np.asarray(inputs["L1_attn_bk"], np.float32)
            bv = np.asarray(inputs["L1_attn_bv"], np.float32)
            pk = np.zeros((D, 768), np.float32)
            bqk = np.zeros((128, 4), np.float32)
            bvp = np.zeros((1, 256), np.float32)
            wql = np.zeros((64, 128), np.float32)
            wkl = np.zeros((64, 128), np.float32)
            Wql = np.asarray(inputs["L1_attn_Wq_lsr"], np.float32)
            Wkl = np.asarray(inputs["L1_attn_Wk_lsr"], np.float32)
            for i, h in enumerate(H):
                pk[:, 128 * i:128 * i + 64] = Wq[h * 64:(h + 1) * 64, :].T
                pk[:, 128 * i + 64:128 * i + 128] = Wk[h * 64:(h + 1) * 64, :].T
                pk[:, 512 + 64 * i:512 + 64 * (i + 1)] = Wv[h * 64:(h + 1) * 64, :].T
                bqk[0:64, i] = bq[h * 64:(h + 1) * 64]
                bqk[64:128, i] = bk[h * 64:(h + 1) * 64]
                bvp[0, 64 * i:64 * (i + 1)] = bv[h * 64:(h + 1) * 64]
                wql[:, 32 * i:32 * (i + 1)] = Wql[h]
                wkl[:, 32 * i:32 * (i + 1)] = Wkl[h]
            m["l1_wqkvT"] = pk
            m["l1_bqk"] = bqk
            m["l1_bv"] = bvp
            m["l1_wqlsr"] = wql
            m["l1_wklsr"] = wkl
            m["l1_woT"] = np.ascontiguousarray(np.asarray(inputs["L1_attn_Wo"], np.float32).T)
            m["l1_bo"] = np.asarray(inputs["L1_attn_bo"], np.float32).reshape(1, D)
    if "diagmask" in names:
        dm = np.zeros((512, 512), np.float32)
        for dvar in range(4):
            srow = np.arange(128)[:, None] + dvar * 128
            col = np.arange(512)[None, :]
            dm[dvar * 128:(dvar + 1) * 128] = np.where(srow <= col, 0.0, -1e30)
        m["diagmask"] = dm
    if "rebase_idx" in names:
        w = c % 4
        p = np.arange(128)[:, None]
        k = np.arange(8)[None, :]
        m["rebase_idx"] = ((128 * k + p) * 4 + w).astype(np.int32)
    if "l0_gwT" in names:
        m["l0_gwT"] = np.ascontiguousarray(np.asarray(inputs["L0_ffn_gate_w"], np.float32).T)
        m["l0_w1"] = np.ascontiguousarray(
            np.asarray(inputs["L0_ffn_w1"], np.float32)[2 * c:2 * c + 2].reshape(2 * D, DFF0))
        m["l0_w2"] = np.ascontiguousarray(
            np.asarray(inputs["L0_ffn_w2"], np.float32)[2 * c:2 * c + 2].reshape(2 * DFF0, D))
        ss = np.arange(128)[:, None]
        tt = np.arange(128)[None, :]
        m["ustrict"] = (ss < tt).astype(np.float32)
        m["iotam"] = np.tile(np.arange(NE0, dtype=np.float32)[None, :], (128, 1))
        m["islot8"] = np.tile(np.arange(128, dtype=np.float32)[:, None], (1, 8))
        pmv = np.zeros((8, 1), np.float32)
        pmv[:c, 0] = 1.0
        m["prevmask"] = pmv
        esr = np.zeros((2, NE0), np.float32)
        esr[0, 2 * c] = 1.0
        esr[1, 2 * c + 1] = 1.0
        m["eselr"] = esr
        ee = np.arange(NE0)
        sbase = (ee // 2) * (2 * SCAP) + (ee % 2) * SCAP
        m["sendbase"] = np.tile(sbase.astype(np.float32)[None, :], (128, 1))
        s8 = np.arange(8)[:, None]
        o8 = np.arange(8)[None, :]
        m["ltri8"] = (s8 <= o8).astype(np.float32)
    if "l1_wr" in names:
        m["l1_wr"] = np.ascontiguousarray(np.asarray(inputs["L1_ffn_Wr"], np.float32))
        m["l1_br"] = np.asarray(inputs["L1_ffn_br"], np.float32).reshape(1, NE1)
        m["l1_w1"] = np.ascontiguousarray(np.asarray(inputs["L1_ffn_W1"], np.float32).reshape(NE1 * D, D))
        m["l1_b1"] = np.ascontiguousarray(np.asarray(inputs["L1_ffn_b1"], np.float32))
        m["l1_w2"] = np.ascontiguousarray(np.asarray(inputs["L1_ffn_W2"], np.float32).reshape(NE1 * D, D))
        m["l1_b2"] = np.ascontiguousarray(np.asarray(inputs["L1_ffn_b2"], np.float32))
    return m


_PROG_CACHE = {}


def _get_program(phases):
    key = tuple(phases)
    if key not in _PROG_CACHE:
        _PROG_CACHE[key] = build_program(phases)
    return _PROG_CACHE[key]


def run_phases(phases, inputs):
    nc, io = _get_program(phases)
    names = set(io.inputs.keys())
    in_maps = [make_core_inputs(inputs, c, names) for c in range(NCORES)]
    res = run_bass_kernel_spmd(nc, in_maps, core_ids=list(range(NCORES)))
    y = np.concatenate([res.results[c]["y"] for c in range(NCORES)], axis=0)
    return y.reshape(2, TBATCH, D)


def kernel(**inputs):
    cur = dict(inputs)
    x = np.asarray(inputs["L0_attn_x"], np.float32).reshape(2, TBATCH, D)
    for phase in ["ffn0", "attn0", "ffn0", "ffn1", "attn1", "ffn1"]:
        cur["L0_attn_x"] = x
        x = run_phases([phase], cur)
    return np.ascontiguousarray(x.astype(np.float32))


# version 03 (final)
# speedup vs baseline: 1.1286x; 1.1286x over previous
import math
import re
from contextlib import ExitStack

import numpy as np
import ml_dtypes
import concourse.bass as bass
import concourse.mybir as mybir
from concourse.bass import IndirectOffsetOnAxis
from concourse.bass_utils import run_bass_kernel_spmd
from concourse.aot_env import allow_ici_all_to_all

F32 = mybir.dt.float32
BF16 = mybir.dt.bfloat16
I32 = mybir.dt.int32
AF = mybir.ActivationFunctionType
OP = mybir.AluOpType
AX = mybir.AxisListType.X
ENG = mybir.EngineType

NCORES = 8
D = 1024
TOWN = 2048
NTILE = TOWN // 128
TBATCH = 8192
NBT = TBATCH // 128
EPS = 1e-5
DFF0 = 4096
NE0 = 16
CAP0 = 1280
SCAP = 320
NE1 = 8
SM_SHIFT = -20.0


_PFX_RE = re.compile(r"^p\d+_")


class Ctx:

    def __init__(self, nc):
        self.nc = nc
        self.sems = {}
        self.ctot = {}
        self.base = {}
        self.canon = {}
        self.pord = {}

    def _c(self, name):
        cn = self.canon.get(name)
        if cn is None:
            m = _PFX_RE.match(name)
            if m:
                ph = m.group(0)
                k = self.pord.get(ph, 0)
                self.pord[ph] = k + 1
                cn = f"g{k}"
            else:
                cn = "u_" + name
            self.canon[name] = cn
            if cn not in self.sems:
                self.sems[cn] = self.nc.alloc_semaphore(cn)
                self.ctot[cn] = 0
            self.base[name] = self.ctot[cn]
        return cn

    def s(self, name):
        return self.sems[self._c(name)]

    def inc(self, inst, name, n=1):
        cn = self._c(name)
        inst.then_inc(self.sems[cn], n)
        self.ctot[cn] += n
        return self.ctot[cn] - self.base[name]

    def bump(self, name, n):
        cn = self._c(name)
        self.ctot[cn] += n

    def val(self, name):
        cn = self._c(name)
        return self.ctot[cn] - self.base[name]

    def wait(self, eng, name, val=None):
        cn = self._c(name)
        tgt = self.ctot[cn] if val is None else self.base[name] + val
        eng.wait_ge(self.sems[cn], tgt)

    def wge(self, eng, name, val):
        cn = self._c(name)
        eng.wait_ge(self.sems[cn], self.base[name] + val)

    def reset_phase(self):
        keep = {n: c for n, c in self.canon.items() if c.startswith("u_")}
        ksems = {c: self.sems[c] for c in keep.values()}
        ktot = {c: self.ctot[c] for c in keep.values()}
        kbase = {n: self.base[n] for n in keep}
        self.sems, self.ctot, self.base, self.canon = ksems, ktot, kbase, keep
        self.pord = {}


class IO:
    def __init__(self, nc):
        self.nc = nc
        self.inputs = {}
        self.outputs = {}

    def inp(self, name, shape, dtype):
        if name not in self.inputs:
            self.inputs[name] = self.nc.dram_tensor(name, list(shape), dtype, kind="ExternalInput")
        return self.inputs[name]

    def out(self, name, shape, dtype):
        if name not in self.outputs:
            self.outputs[name] = self.nc.dram_tensor(name, list(shape), dtype, kind="ExternalOutput")
        return self.outputs[name]


def emit_ln_sweep(ctx, st, xin, xln_dram, pfx, row0_dram=None):
    nc = ctx.nc
    xt = [st.enter_context(nc.sbuf_tensor(f"{pfx}xt{i}", [128, D], F32)) for i in range(2)]
    sq = st.enter_context(nc.sbuf_tensor(f"{pfx}sq", [128, D], F32))
    stats = st.enter_context(nc.sbuf_tensor(f"{pfx}stats", [128, 8], F32))
    stats_s = st.enter_context(nc.sbuf_tensor(f"{pfx}statss", [128, 8], F32))
    xlnb = [st.enter_context(nc.sbuf_tensor(f"{pfx}xlnb{i}", [128, D], BF16)) for i in range(2)]
    row0 = st.enter_context(nc.sbuf_tensor(f"{pfx}row0", [1, D], F32)) if row0_dram is not None else None

    s_sq, s_v, s_r = (pfx + n for n in ("sq", "dve", "rstd"))
    s_ld = [pfx + "ld0", pfx + "ld1"]
    s_st = [pfx + "st0", pfx + "st1"]

    with nc.Block() as block:
        @block.sync
        def _(sync):
            for j in range(NTILE):
                if j >= 2:
                    ctx.wait(sync, s_v, 8 * (j - 1))
                ctx.inc(sync.dma_start(out=xt[j % 2][:], in_=xin[j * 128:(j + 1) * 128, :]),
                        s_ld[j % 2], 16)

        @block.scalar
        def _(scalar):
            for j in range(NTILE):
                ctx.wge(scalar, s_ld[j % 2], 16 * (j // 2 + 1))
                if j >= 1:
                    ctx.wait(scalar, s_v, 8 * j)
                ctx.inc(scalar.activation(out=sq[:], in_=xt[j % 2][:], func=AF.Square), s_sq, 1)
                ctx.wait(scalar, s_v, 8 * j + 6)
                if j == 0:
                    ctx.wait(scalar, "cinit", 17)
                ctx.inc(scalar.activation(out=stats_s[:, 0:1], in_=stats[:, 4:5], func=AF.Ln,
                                          bias=ctx.consts["eps_col"][:]), pfx + "sc", 1)
                ctx.wait(scalar, pfx + "sc")
                ctx.inc(scalar.activation(out=stats_s[:, 1:2], in_=stats_s[:, 0:1], func=AF.Exp,
                                          scale=-0.5), s_r, 1)

        @block.vector
        def _(vector):
            for j in range(NTILE):
                ctx.wge(vector, s_ld[j % 2], 16 * (j // 2 + 1))
                ctx.inc(vector.reduce_sum(stats[:, 0:1], xt[j % 2][:], axis=AX), s_v, 1)
                ctx.wait(vector, s_v)
                ctx.inc(vector.tensor_scalar_mul(stats[:, 2:3], stats[:, 0:1], 1.0 / D), s_v, 1)
                ctx.wait(vector, s_sq, j + 1)
                ctx.inc(vector.reduce_sum(stats[:, 1:2], sq[:], axis=AX), s_v, 1)
                ctx.wait(vector, s_v)
                ctx.inc(vector.tensor_mul(stats[:, 3:4], stats[:, 2:3], stats[:, 2:3]), s_v, 1)
                ctx.inc(vector.tensor_scalar_mul(stats[:, 7:8], stats[:, 1:2], 1.0 / D), s_v, 1)
                ctx.wait(vector, s_v)
                if j >= 1:
                    ctx.wait(vector, s_r, j)
                ctx.inc(vector.tensor_sub(stats[:, 4:5], stats[:, 7:8], stats[:, 3:4]), s_v, 1)
                ctx.inc(vector.tensor_scalar_sub(sq[:], xt[j % 2][:], stats[:, 2:3]), s_v, 1)
                ctx.wait(vector, s_v)
                ctx.wait(vector, s_r, j + 1)
                if j >= 2:
                    ctx.wge(vector, s_st[j % 2], 16 * (j // 2))
                ctx.inc(vector.tensor_scalar_mul(xlnb[j % 2][:], sq[:], stats_s[:, 1:2]), s_v, 1)
                if row0_dram is not None and j == 0:
                    ctx.inc(vector.tensor_scalar_mul(row0[:], sq[0:1, :], stats_s[0:1, 1:2]), pfx + "r0", 1)

        @block.gpsimd
        def _(gpsimd):
            for j in range(NTILE):
                ctx.wait(gpsimd, s_v, 8 * (j + 1))
                ctx.inc(gpsimd.dma_start(out=xln_dram[j * 128:(j + 1) * 128, :], in_=xlnb[j % 2][:]),
                        s_st[j % 2], 16)
            if row0_dram is not None:
                ctx.wait(gpsimd, pfx + "r0", 1)
                ctx.inc(gpsimd.dma_start(out=row0_dram, in_=row0[:]), pfx + "r0st", 16)

    deps = [(s_st[0], ctx.val(s_st[0])), (s_st[1], ctx.val(s_st[1]))]
    if row0_dram is not None:
        deps.append((pfx + "r0st", ctx.val(pfx + "r0st")))
    return deps


def wait_deps(ctx, eng, deps):
    for sem, val in deps:
        ctx.wge(eng, sem, val)


def emit_ffn1(ctx, io, xin, xout, pfx, consts):
    nc = ctx.nc
    wr_d = io.inp("l1_wr", [D, NE1], F32)
    br_d = io.inp("l1_br", [1, NE1], F32)
    w1_d = io.inp("l1_w1", [NE1 * D, D], F32)
    b1_d = io.inp("l1_b1", [NE1, D], F32)
    w2_d = io.inp("l1_w2", [NE1 * D, D], F32)
    b2_d = io.inp("l1_b2", [NE1, D], F32)

    xln_dram = nc.dram_tensor(pfx + "xln", [TOWN, D], BF16)
    rowin = nc.dram_tensor(pfx + "rowin", [1, D], F32)
    rowout = nc.dram_tensor(pfx + "rowout", [NCORES, D], F32)

    with ExitStack() as st_ln:
        ln_deps = emit_ln_sweep(ctx, st_ln, xin, xln_dram.ap(), pfx, row0_dram=rowin.ap())

    with ExitStack() as st:
        wr_sb = st.enter_context(nc.sbuf_tensor(pfx + "wr", [128, 64], F32))
        br_sb = st.enter_context(nc.sbuf_tensor(pfx + "brr", [1, 8], F32))
        row2_sb = st.enter_context(nc.sbuf_tensor(pfx + "row", [2, D], F32))
        rowT = st.enter_context(nc.sbuf_tensor(pfx + "rowT", [128, 8], F32))
        rowidx_sb = st.enter_context(nc.sbuf_tensor(pfx + "rowidx", [2, 1], I32))
        sc = st.enter_context(nc.sbuf_tensor(pfx + "sc", [1, 40], F32))
        iota8 = st.enter_context(nc.sbuf_tensor(pfx + "io8", [1, 8], F32))
        e_i = st.enter_context(nc.sbuf_tensor(pfx + "ei", [1, 2], I32))
        wtmp = [st.enter_context(nc.sbuf_tensor(f"{pfx}wtmp{i}", [128, D], F32)) for i in range(2)]
        wbf = [st.enter_context(nc.sbuf_tensor(f"{pfx}wbf{s}", [128, 8 * D], BF16)) for s in range(2)]
        b1row = st.enter_context(nc.sbuf_tensor(pfx + "b1row", [1, D], F32))
        b2a = st.enter_context(nc.sbuf_tensor(pfx + "b2a", [1, D], F32))
        b2b = st.enter_context(nc.sbuf_tensor(pfx + "b2b", [1, D], F32))
        b2s = st.enter_context(nc.sbuf_tensor(pfx + "b2sum", [1, D], F32))
        b1c = st.enter_context(nc.sbuf_tensor(pfx + "b1c", [128, 16], F32))
        b2q = st.enter_context(nc.sbuf_tensor(pfx + "b2q", [128, D], F32))
        xlnT = st.enter_context(nc.sbuf_tensor(pfx + "xlnT", [128, 8 * TOWN], BF16))
        hT = st.enter_context(nc.sbuf_tensor(pfx + "hT", [128, 16 * TOWN], BF16))
        osb = [st.enter_context(nc.sbuf_tensor(f"{pfx}osb{i}", [128, 512], F32)) for i in range(2)]
        t3 = [st.enter_context(nc.sbuf_tensor(f"{pfx}t3{i}", [128, 512], F32)) for i in range(2)]
        xr = [st.enter_context(nc.sbuf_tensor(f"{pfx}xr{i}", [128, D], F32)) for i in range(2)]
        xnew = [st.enter_context(nc.sbuf_tensor(f"{pfx}xnew{i}", [128, D], F32)) for i in range(2)]

        ps_t = st.enter_context(nc.psum_tensor(pfx + "pst", [128, 8], F32))
        ps_sc = st.enter_context(nc.psum_tensor(pfx + "pssc", [1, 8], F32))
        ps_b = st.enter_context(nc.psum_tensor(pfx + "psb", [128, 512], F32))
        ps_h = [st.enter_context(nc.psum_tensor(f"{pfx}psh{i}", [128, 512], F32)) for i in range(2)]
        ps_o = [st.enter_context(nc.psum_tensor(f"{pfx}pso{i}", [128, 512], F32)) for i in range(2)]

        s_ld2 = pfx + "ld2"
        s_cc = pfx + "cc"
        s_row = pfx + "rowd"
        s_trr = pfx + "trr"
        s_rtc = pfx + "rtc"
        s_scm = pfx + "scm"
        s_io8 = pfx + "io8s"
        s_e = pfx + "e"
        s_b1l = pfx + "b1l"
        s_b2l = pfx + "b2l"
        s_b1t = pfx + "b1t"
        s_b1c = pfx + "b1cs"
        s_b2s = pfx + "b2s"
        s_bcm = pfx + "bcm"
        s_b2q = pfx + "b2qs"
        s_w1l = pfx + "w1l"
        s_w1c = pfx + "w1c"
        s_w2l = pfx + "w2l"
        s_w2c = pfx + "w2c"
        s_ldT = pfx + "ldT"
        s_mmh = pfx + "mmh"
        s_ach = pfx + "ach"
        s_mmo = pfx + "mmo"
        s_aco = pfx + "aco"
        s_res = pfx + "res"
        s_xld = pfx + "xld"
        s_out = pfx + "out"

        with nc.Block() as block:
            @block.sync
            def _(sync):
                wait_deps(ctx, sync, ln_deps)
                for k in range(8):
                    ctx.inc(sync.dma_start(out=wr_sb[:, k * 8:(k + 1) * 8],
                                           in_=wr_d[k * 128:(k + 1) * 128, :]), s_ld2, 16)
                ctx.inc(sync.dma_start(out=br_sb[:], in_=br_d[:, :]), s_ld2, 16)

            @block.sync
            def _(sync):
                ctx.inc(sync.dma_start(out=iota8[:], in_=io.inp("iota8", [1, 8], F32)[:, :]), s_io8, 16)

            @block.gpsimd
            def _(gpsimd):
                wait_deps(ctx, gpsimd, ln_deps)
                ctx.inc(gpsimd.collective_compute(
                    "AllGather", OP.bypass, replica_groups=[list(range(NCORES))],
                    ins=[rowin.ap()], outs=[rowout.ap()]), s_cc, 1)

            @block.sync
            def _(sync):
                ctx.inc(sync.dma_start(out=rowidx_sb[:], in_=io.inp("rowidx", [2, 1], I32)[:, :]),
                        s_ld2, 16)

            @block.gpsimd
            def _(gpsimd):
                ctx.wge(gpsimd, s_ld2, ctx.val(s_ld2))
                ctx.wait(gpsimd, s_cc)
                ctx.inc(gpsimd.indirect_dma_start(
                    out=row2_sb[:], out_offset=None, in_=rowout.ap(),
                    in_offset=IndirectOffsetOnAxis(ap=rowidx_sb[:], axis=0)), s_row, 16)

            @block.tensor
            def _(tensor):
                ctx.wait(tensor, s_row)
                ctx.wait(tensor, "cinit", 16)
                for k in range(8):
                    inst = tensor.transpose(ps_t[:, k:k + 1], row2_sb[0:1, k * 128:(k + 1) * 128],
                                            consts["ident_f32"][0:1, 0:1])
                ctx.inc(inst, s_trr, 1)

            @block.scalar
            def _(scalar):
                ctx.wait(scalar, s_trr)
                ctx.inc(scalar.copy(rowT[:], ps_t[:]), s_rtc, 1)

        with nc.Block() as block:
            @block.tensor
            def _(tensor):
                ctx.wait(tensor, s_rtc)
                ctx.wge(tensor, s_ld2, 16 * 9)
                for k in range(8):
                    inst = tensor.matmul(ps_sc[:], rowT[:, k:k + 1], wr_sb[:, k * 8:(k + 1) * 8],
                                         start=(k == 0), stop=(k == 7))
                ctx.inc(inst, s_scm, 1)

            @block.vector
            def _(vector):
                ctx.wait(vector, s_scm)
                ctx.wge(vector, s_io8, 16)
                ch = pfx + "vch"
                def _c(inst):
                    ctx.inc(inst, ch, 1)
                    ctx.wait(vector, ch)
                _c(vector.tensor_add(sc[0:1, 0:8], ps_sc[:], br_sb[:]))
                _c(vector.reduce_max(sc[0:1, 8:9], sc[0:1, 0:8], axis=AX))
                _c(vector.tensor_single_scalar(sc[0:1, 16:24], sc[0:1, 0:8], sc[0:1, 8:9], OP.is_equal))
                _c(vector.tensor_mul(sc[0:1, 24:32], sc[0:1, 16:24], iota8[:]))
                _c(vector.reduce_sum(sc[0:1, 9:10], sc[0:1, 24:32], axis=AX))
                _c(vector.scalar_tensor_tensor(sc[0:1, 32:40], sc[0:1, 16:24], -1e30, sc[0:1, 0:8],
                                               OP.mult, OP.add))
                _c(vector.reduce_max(sc[0:1, 10:11], sc[0:1, 32:40], axis=AX))
                _c(vector.tensor_single_scalar(sc[0:1, 16:24], sc[0:1, 32:40], sc[0:1, 10:11], OP.is_equal))
                _c(vector.tensor_mul(sc[0:1, 24:32], sc[0:1, 16:24], iota8[:]))
                _c(vector.reduce_sum(sc[0:1, 11:12], sc[0:1, 24:32], axis=AX))
                _c(vector.tensor_copy(e_i[0:1, 0:1], sc[0:1, 9:10]))
                ctx.inc(vector.tensor_copy(e_i[0:1, 1:2], sc[0:1, 11:12]), s_e, 1)

        with nc.Block() as block:
            @block.sync
            def _(sync):
                ctx.wait(sync, s_e, 1)
                for sl in range(2):
                    ereg = sync.alloc_register()
                    sync.load(ereg, e_i[0:1, sl:sl + 1])
                    if sl == 1:
                        ctx.wait(sync, s_b1t, 1)
                    b2dst = b2a if sl == 0 else b2b
                    for e in range(NE1):
                        with sync.If_cmp(ereg, e, "IS_EQ"):
                            sync.dma_start(out=b1row[:], in_=b1_d[e:e + 1, :]).then_inc(ctx.s(s_b1l), 16)
                            sync.dma_start(out=b2dst[:], in_=b2_d[e:e + 1, :]).then_inc(ctx.s(s_b2l), 16)
                    ctx.bump(s_b1l, 16)
                    ctx.bump(s_b2l, 16)

            @block.tensor
            def _(tensor):
                for sl in range(2):
                    ctx.wge(tensor, s_b1l, 16 * (sl + 1))
                    if sl == 1:
                        ctx.wait(tensor, s_b1c, 1)
                    for f in range(8):
                        inst = tensor.transpose(ps_t[:, f:f + 1],
                                                b1row[0:1, f * 128:(f + 1) * 128],
                                                consts["ident_f32"][0:1, 0:1])
                    ctx.inc(inst, s_b1t, 1)

            @block.scalar
            def _(scalar):
                for sl in range(2):
                    ctx.wait(scalar, s_b1t, sl + 1)
                    ctx.inc(scalar.copy(b1c[:, sl * 8:(sl + 1) * 8], ps_t[:]), s_b1c, 1)

        with nc.Block() as block:
            @block.vector
            def _(vector):
                ctx.wge(vector, s_b2l, 32)
                ctx.inc(vector.tensor_add(b2s[:], b2a[:], b2b[:]), s_b2s, 1)

            @block.tensor
            def _(tensor):
                ctx.wait(tensor, s_b2s)
                for n in range(2):
                    if n == 1:
                        ctx.wait(tensor, s_b2q, 1)
                    inst = tensor.matmul(ps_b[:], consts["ones1_f32"][:], b2s[0:1, n * 512:(n + 1) * 512],
                                         start=True, stop=True)
                    ctx.inc(inst, s_bcm, 1)

            @block.scalar
            def _(scalar):
                for n in range(2):
                    ctx.wait(scalar, s_bcm, n + 1)
                    ctx.inc(scalar.activation(out=b2q[:, n * 512:(n + 1) * 512], in_=ps_b[:],
                                              func=AF.Copy, scale=0.25), s_b2q, 1)

        with nc.Block() as block:
            @block.sync
            def _(sync):
                for k in range(8):
                    ctx.inc(sync.dma_start(out=xlnT[:, k * TOWN:(k + 1) * TOWN],
                                           in_=xln_dram.ap()[:, k * 128:(k + 1) * 128],
                                           transpose=True), s_ldT, 16)

            @block.sync
            def _(sync):
                for sl in range(2):
                    ereg = sync.alloc_register()
                    sync.load(ereg, e_i[0:1, sl:sl + 1])
                    for k in range(8):
                        g = sl * 8 + k
                        if g >= 2:
                            ctx.wait(sync, s_w1c, g - 1)
                        for e in range(NE1):
                            with sync.If_cmp(ereg, e, "IS_EQ"):
                                sync.dma_start(out=wtmp[g % 2][:],
                                               in_=w1_d[e * D + k * 128:e * D + (k + 1) * 128, :]
                                               ).then_inc(ctx.s(s_w1l + str(g % 2)), 16)
                        ctx.bump(s_w1l + str(g % 2), 16)

            @block.vector
            def _(vector):
                for sl in range(2):
                    for k in range(8):
                        g = sl * 8 + k
                        ctx.wge(vector, s_w1l + str(g % 2), 16 * (g // 2 + 1))
                        ctx.inc(vector.tensor_copy(wbf[sl][:, k * D:(k + 1) * D], wtmp[g % 2][:]), s_w1c, 1)

        with nc.Block() as block:
            @block.tensor
            def _(tensor):
                ctx.wge(tensor, s_ldT, ctx.val(s_ldT))
                g = 0
                for sl in range(2):
                    for f in range(8):
                        for tb in range(4):
                            if g >= 2:
                                ctx.wait(tensor, s_ach, g - 1)
                            ctx.wge(tensor, s_w1c, sl * 8 + 8)
                            for k in range(8):
                                inst = tensor.matmul(
                                    ps_h[g % 2][:],
                                    wbf[sl][:, k * D + f * 128:k * D + (f + 1) * 128],
                                    xlnT[:, k * TOWN + tb * 512:k * TOWN + (tb + 1) * 512],
                                    start=(k == 0), stop=(k == 7))
                            ctx.inc(inst, s_mmh, 1)
                            g += 1

            @block.scalar
            def _(scalar):
                ctx.wait(scalar, s_b1c, 2)
                g = 0
                for sl in range(2):
                    for f in range(8):
                        col = sl * 8 + f
                        for tb in range(4):
                            ctx.wait(scalar, s_mmh, g + 1)
                            ctx.inc(scalar.activation(
                                out=hT[:, col * TOWN + tb * 512:col * TOWN + (tb + 1) * 512],
                                in_=ps_h[g % 2][:], func=AF.Gelu, bias=b1c[:, col:col + 1]), s_ach, 1)
                            g += 1

        with nc.Block() as block:
            @block.sync
            def _(sync):
                ctx.wait(sync, s_mmh, 64)
                for sl in range(2):
                    ereg = sync.alloc_register()
                    sync.load(ereg, e_i[0:1, sl:sl + 1])
                    for k in range(8):
                        g = sl * 8 + k
                        if g >= 2:
                            ctx.wait(sync, s_w2c, g - 1)
                        for e in range(NE1):
                            with sync.If_cmp(ereg, e, "IS_EQ"):
                                sync.dma_start(out=wtmp[g % 2][:],
                                               in_=w2_d[e * D + k * 128:e * D + (k + 1) * 128, :]
                                               ).then_inc(ctx.s(s_w2l + str(g % 2)), 16)
                        ctx.bump(s_w2l + str(g % 2), 16)

            @block.vector
            def _(vector):
                for sl in range(2):
                    for k in range(8):
                        g = sl * 8 + k
                        ctx.wge(vector, s_w2l + str(g % 2), 16 * (g // 2 + 1))
                        ctx.inc(vector.tensor_copy(wbf[sl][:, k * D:(k + 1) * D], wtmp[g % 2][:]), s_w2c, 1)

        with nc.Block() as block:
            @block.sync
            def _(sync):
                for j in range(NTILE):
                    if j >= 2:
                        ctx.wait(sync, s_res, 2 * (j - 1))
                    ctx.inc(sync.dma_start(out=xr[j % 2][:], in_=xin[j * 128:(j + 1) * 128, :]),
                            s_xld + str(j % 2), 16)

            @block.tensor
            def _(tensor):
                ctx.wait(tensor, s_ach)
                ctx.wait(tensor, s_w2c)
                g = 0
                for j in range(NTILE):
                    for n in range(2):
                        if g >= 2:
                            ctx.wait(tensor, s_aco, g - 1)
                        for sl in range(2):
                            for f in range(8):
                                col = sl * 8 + f
                                inst = tensor.matmul(
                                    ps_o[g % 2][:],
                                    hT[:, col * TOWN + j * 128:col * TOWN + (j + 1) * 128],
                                    wbf[sl][:, f * D + n * 512:f * D + (n + 1) * 512],
                                    start=(sl == 0 and f == 0), stop=(sl == 1 and f == 7))
                        ctx.inc(inst, s_mmo, 1)
                        g += 1

            @block.scalar
            def _(scalar):
                g = 0
                for j in range(NTILE):
                    for n in range(2):
                        ctx.wait(scalar, s_mmo, g + 1)
                        if g >= 2:
                            ctx.wait(scalar, s_res, g - 1)
                        ctx.inc(scalar.activation(out=osb[g % 2][:], in_=ps_o[g % 2][:],
                                                  func=AF.Copy, scale=0.25), s_aco, 1)
                        g += 1

            @block.vector
            def _(vector):
                ctx.wait(vector, s_b2q, 2)
                g = 0
                for j in range(NTILE):
                    ctx.wge(vector, s_xld + str(j % 2), 16 * (j // 2 + 1))
                    if j >= 2:
                        ctx.wge(vector, s_out + str(j % 2), 16 * (j // 2))
                    for n in range(2):
                        ctx.wait(vector, s_aco, g + 1)
                        vector.tensor_add(t3[g % 2][:], osb[g % 2][:],
                                          b2q[:, n * 512:(n + 1) * 512])
                        ctx.inc(vector.tensor_add(xnew[j % 2][:, n * 512:(n + 1) * 512],
                                                  t3[g % 2][:],
                                                  xr[j % 2][:, n * 512:(n + 1) * 512]), s_res, 1)
                        g += 1

            @block.gpsimd
            def _(gpsimd):
                for j in range(NTILE):
                    ctx.wait(gpsimd, s_res, 2 * (j + 1))
                    ctx.inc(gpsimd.dma_start(out=xout[j * 128:(j + 1) * 128, :], in_=xnew[j % 2][:]),
                            s_out + str(j % 2), 16)

    return [(s_out + "0", ctx.val(s_out + "0")), (s_out + "1", ctx.val(s_out + "1"))]


def emit_attn(ctx, io, xin, xout, pfx, consts, layer):
    nc = ctx.nc
    NH = 4
    NSUB = TOWN // 512
    NIB = TBATCH // 512
    grp = [[0, 1, 2, 3], [4, 5, 6, 7]]

    if layer == 0:
        wqkv_d = io.inp("l0_wqkv", [D, 768], F32)
        wo_d = io.inp("l0_wo_full", [D, D], F32)
        cos_d = io.inp("rope_cos", [64, TBATCH], BF16)
        sin_d = io.inp("rope_sin", [64, TBATCH], BF16)
        sm_scale = 1.0 / 8.0
    else:
        wqkv_d = io.inp("l1_wqkvT", [D, 768], F32)
        wo_d = io.inp("l1_woT", [D, D], F32)
        bqk_d = io.inp("l1_bqk", [128, 4], F32)
        bv_d = io.inp("l1_bv", [1, 256], F32)
        wql_d = io.inp("l1_wqlsr", [64, 128], F32)
        wkl_d = io.inp("l1_wklsr", [64, 128], F32)
        bo_d = io.inp("l1_bo", [1, D], F32)
        sm_scale = 1.0 / math.sqrt(32.0)
    mask_d = io.inp("diagmask", [512, 512], F32)
    ridx_d = io.inp("rebase_idx", [128, 8], I32)

    xln_dram = nc.dram_tensor(pfx + "xln", [TOWN, D], BF16)
    g_dram = nc.dram_tensor(pfx + "g", [TBATCH, D], BF16)
    xlnT_dram = nc.dram_tensor(pfx + "xlnT", [D, TBATCH], BF16)
    qk_dram = nc.dram_tensor(pfx + "qk", [512, TBATCH], BF16)
    y_dram = nc.dram_tensor(pfx + "y", [256, TBATCH], BF16)
    Y_dram = nc.dram_tensor(pfx + "Y", [D, TBATCH], BF16)

    with ExitStack() as st_ln:
        ln_deps = emit_ln_sweep(ctx, st_ln, xin, xln_dram.ap(), pfx)

    S = lambda n: pfx + n

    with ExitStack() as st:
        y_cat = [st.enter_context(nc.sbuf_tensor(f"{pfx}ycat{i}", [128, TBATCH], BF16)) for i in range(2)]
        v_all = st.enter_context(nc.sbuf_tensor(pfx + "vall", [128, NBT * 4 * 65], BF16))
        if layer == 1:
            ones_bf = st.enter_context(nc.sbuf_tensor(pfx + "onesbf", [1, 128], BF16))
            bo_bf = st.enter_context(nc.sbuf_tensor(pfx + "bobf", [1, D], BF16))
        dmask = st.enter_context(nc.sbuf_tensor(pfx + "dmask", [128, 4 * 512], F32))
        shift_col = st.enter_context(nc.sbuf_tensor(pfx + "shift", [128, 1], F32))

        with nc.Block() as block:
            @block.gpsimd
            def _(gpsimd):
                wait_deps(ctx, gpsimd, ln_deps)
                ctx.inc(gpsimd.collective_compute(
                    "AllGather", OP.bypass, replica_groups=grp,
                    ins=[xln_dram.ap()], outs=[g_dram.ap()]), S("cc"), 1)

            @block.vector
            def _(vector):
                wait_deps(ctx, vector, ln_deps)
                vector.memset(v_all[:], 1.0)
                ctx.inc(vector.memset(shift_col[:], SM_SHIFT), S("ms"), 1)
                if layer == 1:
                    ctx.inc(vector.memset(ones_bf[:], 1.0), S("ob"), 1)

            @block.sync
            def _(sync):
                wait_deps(ctx, sync, ln_deps)
                for dvar in range(4):
                    ctx.inc(sync.dma_start(out=dmask[:, dvar * 512:(dvar + 1) * 512],
                                           in_=mask_d[dvar * 128:(dvar + 1) * 128, :]), S("msk0"), 16)


        with ExitStack() as st2:
            strip = [st2.enter_context(nc.sbuf_tensor(f"{pfx}strip{i}", [128, TBATCH], BF16))
                     for i in range(2)]
            with nc.Block() as block:
                @block.sync
                def _(sync):
                    ctx.wait(sync, S("cc"))
                    for k in range(8):
                        if k >= 2:
                            ctx.wge(sync, S("trs") + str(k % 2), 16 * (k // 2))
                        ctx.inc(sync.dma_start(out=strip[k % 2][:],
                                               in_=g_dram.ap()[:, k * 128:(k + 1) * 128],
                                               transpose=True), S("trl") + str(k % 2), 16)

                @block.gpsimd
                def _(gpsimd):
                    for k in range(8):
                        ctx.wge(gpsimd, S("trl") + str(k % 2), 16 * (k // 2 + 1))
                        ctx.inc(gpsimd.dma_start(out=xlnT_dram.ap()[k * 128:(k + 1) * 128, :],
                                                 in_=strip[k % 2][:]), S("trs") + str(k % 2), 16)
        tr_done = [(S("trs0"), ctx.val(S("trs0"))), (S("trs1"), ctx.val(S("trs1")))]

        with ExitStack() as st2:
            xb = st2.enter_context(nc.sbuf_tensor(pfx + "xb", [128, 8 * TOWN], BF16))
            wtmp = st2.enter_context(nc.sbuf_tensor(pfx + "swt", [128, 768], F32))
            wq_bf = st2.enter_context(nc.sbuf_tensor(pfx + "wqbf", [128, 8 * 768], BF16))
            ev = [st2.enter_context(nc.sbuf_tensor(f"{pfx}ev{i}", [128, 512], BF16)) for i in range(2)]
            if layer == 1:
                bqk_sb = st2.enter_context(nc.sbuf_tensor(pfx + "bqk", [128, 4], F32))
                bv_f = st2.enter_context(nc.sbuf_tensor(pfx + "bvf", [1, 256], F32))
                bv_bf = st2.enter_context(nc.sbuf_tensor(pfx + "bvbf", [1, 256], BF16))
            ps_qk = [st2.enter_context(nc.psum_tensor(f"{pfx}psqk{i}", [128, 512], F32)) for i in range(2)]
            ps_v = [st2.enter_context(nc.psum_tensor(f"{pfx}psv{i}", [128, 256], F32)) for i in range(2)]

            with nc.Block() as block:
                @block.sync
                def _(sync):
                    for k in range(8):
                        if k >= 1:
                            ctx.wait(sync, S("wcv"), k)
                        ctx.inc(sync.dma_start(out=wtmp[:], in_=wqkv_d[k * 128:(k + 1) * 128, :]),
                                S("wld"), 16)
                    if layer == 1:
                        ctx.inc(sync.dma_start(out=bqk_sb[:], in_=bqk_d[:, :]), S("bqkl"), 16)
                        ctx.inc(sync.dma_start(out=bv_f[:], in_=bv_d[:, :]), S("bvl"), 16)

                @block.vector
                def _(vector):
                    for k in range(8):
                        ctx.wge(vector, S("wld"), 16 * (k + 1))
                        ctx.inc(vector.tensor_copy(wq_bf[:, k * 768:(k + 1) * 768], wtmp[:]), S("wcv"), 1)
                    if layer == 1:
                        ctx.wge(vector, S("bvl"), 16)
                        ctx.inc(vector.tensor_copy(bv_bf[:], bv_f[:]), S("bvc"), 1)

            with nc.Block() as block:
                @block.sync
                def _(sync):
                    wait_deps(ctx, sync, tr_done)
                    for tb in range(4):
                        if tb > 0:
                            ctx.wait(sync, S("swmm"), tb * (NH * NSUB + NTILE))
                        for k in range(8):
                            ctx.inc(sync.dma_start(
                                out=xb[:, k * TOWN:(k + 1) * TOWN],
                                in_=xlnT_dram.ap()[k * 128:(k + 1) * 128, tb * TOWN:(tb + 1) * TOWN]),
                                S("swld"), 16)

                @block.tensor
                def _(tensor):
                    ctx.wait(tensor, S("wcv"))
                    if layer == 1:
                        ctx.wait(tensor, S("bvc"))
                        ctx.wait(tensor, S("ob"))
                    nmm = 0
                    for tb in range(4):
                        ctx.wge(tensor, S("swld"), 16 * 8 * (tb + 1))
                        for h in range(NH):
                            for nsub in range(NSUB):
                                if nmm >= 2:
                                    ctx.wge(tensor, S("swev"), nmm - 1)
                                for k in range(8):
                                    inst = tensor.matmul(
                                        ps_qk[nmm % 2][:],
                                        wq_bf[:, k * 768 + h * 128:k * 768 + (h + 1) * 128],
                                        xb[:, k * TOWN + nsub * 512:k * TOWN + (nsub + 1) * 512],
                                        start=(k == 0), stop=(k == 7))
                                ctx.inc(inst, S("swmm"), 1)
                                nmm += 1
                        for ts in range(NTILE):
                            if nmm >= 2:
                                ctx.wge(tensor, S("swev"), nmm - 1)
                            for k in range(8):
                                inst = tensor.matmul(
                                    ps_v[nmm % 2][:],
                                    xb[:, k * TOWN + ts * 128:k * TOWN + (ts + 1) * 128],
                                    wq_bf[:, k * 768 + 512:k * 768 + 768],
                                    start=(k == 0), stop=(layer == 0 and k == 7))
                            if layer == 1:
                                inst = tensor.matmul(ps_v[nmm % 2][:], ones_bf[:], bv_bf[:],
                                                     start=False, stop=True)
                            ctx.inc(inst, S("swmm"), 1)
                            nmm += 1

                @block.scalar
                def _(scalar):
                    ctx.wge(scalar, S("msk0"), ctx.val(S("msk0")))
                    ctx.wge(scalar, S("ms"), 1)
                    if layer == 1:
                        ctx.wge(scalar, S("bqkl"), 16)
                    nmm = 0
                    nqk = 0
                    for tb in range(4):
                        for h in range(NH):
                            for nsub in range(NSUB):
                                ctx.wait(scalar, S("swmm"), nmm + 1)
                                if nqk >= 2:
                                    ctx.wge(scalar, S("swst") + str(nqk % 2), 16 * (nqk // 2))
                                if layer == 1:
                                    inst = scalar.activation(out=ev[nqk % 2][:], in_=ps_qk[nmm % 2][:],
                                                             func=AF.Identity, bias=bqk_sb[:, h:h + 1])
                                else:
                                    inst = scalar.activation(out=ev[nqk % 2][:], in_=ps_qk[nmm % 2][:],
                                                             func=AF.Copy)
                                ctx.inc(inst, S("swev"), 1)
                                nmm += 1
                                nqk += 1
                        for ts in range(NTILE):
                            ctx.wait(scalar, S("swmm"), nmm + 1)
                            j_tile = tb * NTILE + ts
                            for h in range(NH):
                                blk = (j_tile * 4 + h) * 65
                                inst = scalar.activation(out=v_all[:, blk:blk + 64],
                                                         in_=ps_v[nmm % 2][:, h * 64:(h + 1) * 64],
                                                         func=AF.Copy)
                            ctx.inc(inst, S("swev"), 1)
                            nmm += 1

                @block.gpsimd
                def _(gpsimd):
                    nev = 0
                    nqk = 0
                    for tb in range(4):
                        for h in range(NH):
                            for nsub in range(NSUB):
                                ctx.wge(gpsimd, S("swev"), nev + 1)
                                ctx.inc(gpsimd.dma_start(
                                    out=qk_dram.ap()[h * 128:(h + 1) * 128,
                                                     tb * TOWN + nsub * 512:tb * TOWN + (nsub + 1) * 512],
                                    in_=ev[nqk % 2][:]), S("swst") + str(nqk % 2), 16)
                                nev += 1
                                nqk += 1
                        nev += NTILE

        sw_done = [(S("swst0"), ctx.val(S("swst0"))), (S("swst1"), ctx.val(S("swst1"))),
                   (S("swev"), ctx.val(S("swev")))]

        st_core = ExitStack()
        if layer == 0:
            cosT = st_core.enter_context(nc.sbuf_tensor(pfx + "cosT", [64, TBATCH], BF16))
            sinT = st_core.enter_context(nc.sbuf_tensor(pfx + "sinT", [64, TBATCH], BF16))
            kk = [st_core.enter_context(nc.sbuf_tensor(f"{pfx}kk{i}", [64, TBATCH], BF16)) for i in range(4)]
        else:
            klr = [st_core.enter_context(nc.sbuf_tensor(f"{pfx}klr{i}", [32, TBATCH], BF16)) for i in range(4)]
            wql_bf = st_core.enter_context(nc.sbuf_tensor(pfx + "wql", [64, 128], BF16))

        with ExitStack() as st2:
            HB = TBATCH // 2
            kraw = st2.enter_context(nc.sbuf_tensor(
                pfx + "kraw", [64, HB if layer == 0 else TBATCH], BF16))
            if layer == 0:
                ks = st2.enter_context(nc.sbuf_tensor(pfx + "ks", [64, HB], BF16))
                u1 = st2.enter_context(nc.sbuf_tensor(pfx + "u1", [64, HB], BF16))
                with nc.Block() as block:
                    @block.sync
                    def _(sync):
                        wait_deps(ctx, sync, sw_done)
                        ctx.inc(sync.dma_start(out=cosT[:], in_=cos_d[:, :]), S("rope"), 16)
                        ctx.inc(sync.dma_start(out=sinT[:], in_=sin_d[:, :]), S("rope"), 16)
                        for g in range(2 * NH):
                            h, hf = g // 2, g % 2
                            if g > 0:
                                ctx.wait(sync, S("kp"), g)
                            ctx.inc(sync.dma_start(
                                out=kraw[:],
                                in_=qk_dram.ap()[h * 128 + 64:(h + 1) * 128, hf * HB:(hf + 1) * HB]),
                                S("kld"), 16)

                    @block.scalar
                    def _(scalar):
                        for g in range(2 * NH):
                            ctx.wge(scalar, S("kld"), 16 * (g + 1))
                            if g > 0:
                                ctx.wait(scalar, S("kch"), 2 * g)
                            scalar.copy(ks[0:32, :], kraw[32:64, :])
                            ctx.inc(scalar.copy(ks[32:64, :], kraw[0:32, :]), S("ksc"), 1)

                    @block.vector
                    def _(vector):
                        ctx.wge(vector, S("rope"), 32)
                        for g in range(2 * NH):
                            h, hf = g // 2, g % 2
                            ctx.wge(vector, S("kld"), 16 * (g + 1))
                            ctx.inc(vector.tensor_mul(u1[:], kraw[:], cosT[:, hf * HB:(hf + 1) * HB]),
                                    S("kch"), 1)
                            ctx.wait(vector, S("ksc"), g + 1)
                            ctx.wait(vector, S("kch"))
                            ctx.inc(vector.tensor_mul(kraw[:], ks[:], sinT[:, hf * HB:(hf + 1) * HB]),
                                    S("kch"), 1)
                            ctx.wait(vector, S("kch"))
                            inst = vector.tensor_add(kk[h][:, hf * HB:(hf + 1) * HB], u1[:], kraw[:])
                            ctx.inc(inst, S("kp"), 1)
            else:
                wklt = st2.enter_context(nc.sbuf_tensor(pfx + "wklt", [64, 128], F32))
                wkl_bf = st2.enter_context(nc.sbuf_tensor(pfx + "wkl", [64, 128], BF16))
                wqlt = st2.enter_context(nc.sbuf_tensor(pfx + "wqlt", [64, 128], F32))
                ps_lr = [st2.enter_context(nc.psum_tensor(f"{pfx}pslr{i}", [32, 512], F32))
                         for i in range(2)]
                with nc.Block() as block:
                    @block.sync
                    def _(sync):
                        ctx.inc(sync.dma_start(out=wklt[:], in_=wkl_d[:, :]), S("wkll"), 16)
                        ctx.inc(sync.dma_start(out=wqlt[:], in_=wql_d[:, :]), S("wqll"), 16)
                        wait_deps(ctx, sync, sw_done)
                        for h in range(NH):
                            if h > 0:
                                ctx.wait(sync, S("kp"), (TBATCH // 512) * h)
                            ctx.inc(sync.dma_start(out=kraw[:],
                                                   in_=qk_dram.ap()[h * 128 + 64:(h + 1) * 128, :]),
                                    S("kld"), 16)

                    @block.vector
                    def _(vector):
                        ctx.wge(vector, S("wkll"), 16)
                        ctx.inc(vector.tensor_copy(wkl_bf[:], wklt[:]), S("wklc"), 1)
                        ctx.wge(vector, S("wqll"), 16)
                        ctx.inc(vector.tensor_copy(wql_bf[:], wqlt[:]), S("wqlc"), 1)

                    @block.tensor
                    def _(tensor):
                        ctx.wait(tensor, S("wklc"))
                        ng = 0
                        for h in range(NH):
                            ctx.wge(tensor, S("kld"), 16 * (h + 1))
                            for cb in range(TBATCH // 512):
                                if ng >= 2:
                                    ctx.wait(tensor, S("kp"), ng - 1)
                                inst = tensor.matmul(ps_lr[ng % 2][:],
                                                     wkl_bf[:, h * 32:(h + 1) * 32],
                                                     kraw[:, cb * 512:(cb + 1) * 512],
                                                     start=True, stop=True)
                                ctx.inc(inst, S("kpm"), 1)
                                ng += 1

                    @block.scalar
                    def _(scalar):
                        ng = 0
                        for h in range(NH):
                            for cb in range(TBATCH // 512):
                                ctx.wait(scalar, S("kpm"), ng + 1)
                                ctx.inc(scalar.copy(klr[h][:, cb * 512:(cb + 1) * 512],
                                                    ps_lr[ng % 2][:]), S("kp"), 1)
                                ng += 1

        kp_done = [(S("kp"), ctx.val(S("kp")))]

        with ExitStack() as st2:
            qraw = st2.enter_context(nc.sbuf_tensor(pfx + "qraw", [64, 512], BF16))
            if layer == 0:
                qs = st2.enter_context(nc.sbuf_tensor(pfx + "qs", [64, 512], BF16))
                qt1 = st2.enter_context(nc.sbuf_tensor(pfx + "qt1", [64, 512], BF16))
                qt2 = st2.enter_context(nc.sbuf_tensor(pfx + "qt2", [64, 512], BF16))
                qfin = st2.enter_context(nc.sbuf_tensor(pfx + "qfin", [64, 512], BF16))
            else:
                qfin = st2.enter_context(nc.sbuf_tensor(pfx + "qfin", [32, 512], BF16))
            pt = [st2.enter_context(nc.sbuf_tensor(f"{pfx}pt{i}", [128, 512], BF16)) for i in range(2)]
            mskb = [st2.enter_context(nc.sbuf_tensor(f"{pfx}mskb{i}", [128, 512], F32)) for i in range(2)]
            rrow = st2.enter_context(nc.sbuf_tensor(pfx + "rrow", [1, 512], F32))
            rb_sb = st2.enter_context(nc.sbuf_tensor(pfx + "rb", [64, 512], F32))
            ps_s = [st2.enter_context(nc.psum_tensor(f"{pfx}pss{i}", [128, 512], F32)) for i in range(2)]
            ps_y = st2.enter_context(nc.psum_tensor(pfx + "psy", [65, 512], F32))
            ps_bc = st2.enter_context(nc.psum_tensor(pfx + "psbc", [64, 512], F32))
            if layer == 1:
                ps_ql = st2.enter_context(nc.psum_tensor(pfx + "psql", [32, 512], F32))

            blocks_sched = [(h, ib) for h in range(NH) for ib in range(NIB)]
            jcounts = [4 * ib + 4 for (h, ib) in blocks_sched]
            cumS = [0]
            for jc in jcounts:
                cumS.append(cumS[-1] + jc)
            diag_idx = []
            for gi, jc in enumerate(jcounts):
                for j in range(jc - 4, jc):
                    diag_idx.append(cumS[gi] + j)
            diag_rank = {siux: r for r, siux in enumerate(diag_idx)}

            def emit_core_chunk(gis):
                with nc.Block() as block:
                    @block.sync
                    def _(sync):
                        for gi in gis:
                            h, ib = blocks_sched[gi]
                            if gi > 0:
                                if layer == 0:
                                    ctx.wait(sync, S("qsc"), gi)
                                    ctx.wait(sync, S("qch"), 2 * gi - 1)
                                else:
                                    ctx.wait(sync, S("qlrm"), gi)
                            ctx.inc(sync.dma_start(
                                out=qraw[:], in_=qk_dram.ap()[h * 128:h * 128 + 64,
                                                              ib * 512:(ib + 1) * 512]),
                                S("qld") + str(gi % 2), 16)

                    @block.scalar
                    def _(scalar):
                        for gi in gis:
                            h, ib = blocks_sched[gi]
                            jc = jcounts[gi]
                            if layer == 0:
                                ctx.wge(scalar, S("qld") + str(gi % 2), 16 * (gi // 2 + 1))
                                if gi > 0:
                                    ctx.wait(scalar, S("qch"), 2 * gi)
                                scalar.copy(qs[0:32, :], qraw[32:64, :])
                                ctx.inc(scalar.copy(qs[32:64, :], qraw[0:32, :]), S("qsc"), 1)
                            else:
                                ctx.wait(scalar, S("qlrm"), gi + 1)
                                if gi > 0:
                                    ctx.wait(scalar, S("mm"), cumS[gi])
                                ctx.inc(scalar.copy(qfin[:], ps_ql[:]), S("qp"), 1)
                            for j in range(jc):
                                si = cumS[gi] + j
                                ctx.wait(scalar, S("mm"), si + 1)
                                if si >= 2:
                                    ctx.wait(scalar, S("avd"), si - 1)
                                if si in diag_rank:
                                    ctx.wait(scalar, S("mskd"), diag_rank[si] + 1)
                                    src = mskb[si % 2][:]
                                else:
                                    src = ps_s[si % 2][:]
                                ctx.inc(scalar.activation(out=pt[si % 2][:], in_=src, func=AF.Exp,
                                                          scale=sm_scale, bias=shift_col[:]),
                                        S("exp"), 1)
                            ctx.wait(scalar, S("bcm"), gi + 1)
                            if gi > 0:
                                ctx.wait(scalar, S("yfin"), gi)
                            ctx.inc(scalar.copy(rb_sb[:], ps_bc[:]), S("rbc"), 1)

                    @block.vector
                    def _(vector):
                        if gis[0] == 0:
                            wait_deps(ctx, vector, kp_done)
                        for gi in gis:
                            h, ib = blocks_sched[gi]
                            jc = jcounts[gi]
                            if layer == 0:
                                ctx.wge(vector, S("qld") + str(gi % 2), 16 * (gi // 2 + 1))
                                if gi > 0:
                                    ctx.wait(vector, S("mm"), cumS[gi])
                                ctx.inc(vector.tensor_mul(qt1[:], qraw[:],
                                                          cosT[:, ib * 512:(ib + 1) * 512]), S("qch"), 1)
                                ctx.wait(vector, S("qsc"), gi + 1)
                                ctx.inc(vector.tensor_mul(qt2[:], qs[:],
                                                          sinT[:, ib * 512:(ib + 1) * 512]), S("qch"), 1)
                                ctx.wait(vector, S("qch"))
                                ctx.inc(vector.tensor_add(qfin[:], qt1[:], qt2[:]), S("qp"), 1)
                            for dpos in range(4):
                                j = jc - 4 + dpos
                                si = cumS[gi] + j
                                ctx.wait(vector, S("mm"), si + 1)
                                dr = diag_rank[si]
                                if dr >= 2:
                                    prev_si = diag_idx[dr - 2]
                                    ctx.wait(vector, S("exp"), prev_si + 1)
                                ctx.inc(vector.tensor_add(mskb[si % 2][:], ps_s[si % 2][:],
                                                          dmask[:, dpos * 512:(dpos + 1) * 512]),
                                        S("mskd"), 1)
                            ctx.wait(vector, S("avd"), cumS[gi + 1])
                            if gi > 0:
                                ctx.wait(vector, S("bcm"), gi)
                            ctx.inc(vector.reciprocal(rrow[:], ps_y[64:65, :]), S("rec"), 1)
                            ctx.wait(vector, S("rbc"), gi + 1)
                            inst = vector.tensor_mul(
                                y_cat[h // 2][(h % 2) * 64:(h % 2) * 64 + 64, ib * 512:(ib + 1) * 512],
                                ps_y[0:64, :], rb_sb[:])
                            ctx.inc(inst, S("yfin"), 1)

                    @block.tensor
                    def _(tensor):
                        if gis[0] == 0:
                            ctx.wait(tensor, "cinit", 16)
                            wait_deps(ctx, tensor, kp_done)
                            if layer == 1:
                                ctx.wait(tensor, S("wqlc"))
                        for gi in gis:
                            h, ib = blocks_sched[gi]
                            jc = jcounts[gi]
                            if layer == 1:
                                ctx.wge(tensor, S("qld") + str(gi % 2), 16 * (gi // 2 + 1))
                                if gi > 0:
                                    ctx.wait(tensor, S("qp"), gi)
                                inst = tensor.matmul(ps_ql[:], wql_bf[:, h * 32:(h + 1) * 32],
                                                     qraw[:], start=True, stop=True)
                                ctx.inc(inst, S("qlrm"), 1)
                            ctx.wait(tensor, S("qp"), gi + 1)
                            if gi > 0:
                                ctx.wait(tensor, S("yfin"), gi)
                            for j in range(jc):
                                si = cumS[gi] + j
                                if si >= 2:
                                    ctx.wait(tensor, S("exp"), si - 1)
                                if layer == 0:
                                    lhs = kk[h][:, j * 128:(j + 1) * 128]
                                else:
                                    lhs = klr[h][:, j * 128:(j + 1) * 128]
                                inst = tensor.matmul(ps_s[si % 2][:], lhs, qfin[:],
                                                     start=True, stop=True)
                                ctx.inc(inst, S("mm"), 1)
                                if j >= 1:
                                    siprev = si - 1
                                    ctx.wait(tensor, S("exp"), siprev + 1)
                                    jp = j - 1
                                    inst = tensor.matmul(
                                        ps_y[:],
                                        v_all[:, (jp * 4 + h) * 65:(jp * 4 + h) * 65 + 65],
                                        pt[siprev % 2][:],
                                        start=(jp == 0), stop=False)
                                    ctx.inc(inst, S("avd"), 1)
                            silast = cumS[gi] + jc - 1
                            ctx.wait(tensor, S("exp"), silast + 1)
                            jp = jc - 1
                            inst = tensor.matmul(
                                ps_y[:], v_all[:, (jp * 4 + h) * 65:(jp * 4 + h) * 65 + 65],
                                pt[silast % 2][:], start=(jp == 0), stop=True)
                            ctx.inc(inst, S("avd"), 1)
                            ctx.wait(tensor, S("rec"), gi + 1)
                            if gi > 0:
                                ctx.wait(tensor, S("rbc"), gi)
                            inst = tensor.matmul(ps_bc[:], consts["ones1_f32"][:, 0:64], rrow[:],
                                                 start=True, stop=True)
                            ctx.inc(inst, S("bcm"), 1)

            CHUNK = 4
            for c0 in range(0, len(blocks_sched), CHUNK):
                emit_core_chunk(list(range(c0, min(c0 + CHUNK, len(blocks_sched)))))

        core_done = [(S("yfin"), ctx.val(S("yfin")))]
        st_core.close()

        with ExitStack() as st2:
            ridx_sb = st2.enter_context(nc.sbuf_tensor(pfx + "ridx", [128, 8], I32))
            yw = [st2.enter_context(nc.sbuf_tensor(f"{pfx}yw{k}", [128, TOWN], BF16)) for k in range(8)]
            wtmp2 = st2.enter_context(nc.sbuf_tensor(pfx + "owt", [128, D], F32))
            wo_bf = st2.enter_context(nc.sbuf_tensor(pfx + "wobf", [128, 8 * D], BF16))
            osb = [st2.enter_context(nc.sbuf_tensor(f"{pfx}osb{i}", [128, 512], F32)) for i in range(2)]
            xr = [st2.enter_context(nc.sbuf_tensor(f"{pfx}xr{i}", [128, D], F32)) for i in range(2)]
            xnew = [st2.enter_context(nc.sbuf_tensor(f"{pfx}xnew{i}", [128, D], F32)) for i in range(2)]
            if layer == 1:
                bo_f = st2.enter_context(nc.sbuf_tensor(pfx + "bof", [1, D], F32))
            ps_o = [st2.enter_context(nc.psum_tensor(f"{pfx}pso{i}", [128, 512], F32)) for i in range(2)]

            with nc.Block() as block:
                @block.gpsimd
                def _(gpsimd):
                    wait_deps(ctx, gpsimd, core_done)
                    for i in range(2):
                        ctx.inc(gpsimd.dma_start(out=y_dram.ap()[i * 128:(i + 1) * 128, :],
                                                 in_=y_cat[i][:]), S("yst"), 16)
                    ctx.wge(gpsimd, S("yst"), 32)
                    ctx.inc(gpsimd.collective_compute(
                        "AllGather", OP.bypass, replica_groups=grp,
                        ins=[y_dram.ap()], outs=[Y_dram.ap()]), S("cc2"), 1)

                @block.sync
                def _(sync):
                    wait_deps(ctx, sync, core_done)
                    ctx.inc(sync.dma_start(out=ridx_sb[:], in_=ridx_d[:, :]), S("ridx"), 16)
                    for k in range(8):
                        if k >= 1:
                            ctx.wait(sync, S("woc"), k)
                        ctx.inc(sync.dma_start(out=wtmp2[:], in_=wo_d[k * 128:(k + 1) * 128, :]),
                                S("wol"), 16)
                    if layer == 1:
                        ctx.inc(sync.dma_start(out=bo_f[:], in_=bo_d[:, :]), S("bol"), 16)

                @block.vector
                def _(vector):
                    wait_deps(ctx, vector, core_done)
                    for k in range(8):
                        ctx.wge(vector, S("wol"), 16 * (k + 1))
                        ctx.inc(vector.tensor_copy(wo_bf[:, k * D:(k + 1) * D], wtmp2[:]), S("woc"), 1)
                    if layer == 1:
                        ctx.wge(vector, S("bol"), 16)
                        ctx.inc(vector.tensor_copy(bo_bf[:], bo_f[:]), S("boc"), 1)

                @block.gpsimd
                def _(gpsimd):
                    ctx.wge(gpsimd, S("cc2"), 1)
                    ctx.wge(gpsimd, S("ridx"), 16)
                    Yv = Y_dram.ap().rearrange("d (w t) -> (d w) t", w=4)
                    for k in range(8):
                        ctx.inc(gpsimd.indirect_dma_start(
                            out=yw[k][:], out_offset=None, in_=Yv,
                            in_offset=IndirectOffsetOnAxis(ap=ridx_sb[:, k:k + 1], axis=0)),
                            S("ywl"), 16)

                @block.sync
                def _(sync):
                    for j in range(NTILE):
                        if j >= 2:
                            ctx.wait(sync, S("res"), 2 * (j - 1))
                        ctx.inc(sync.dma_start(out=xr[j % 2][:], in_=xin[j * 128:(j + 1) * 128, :]),
                                S("xld") + str(j % 2), 16)

                @block.tensor
                def _(tensor):
                    ctx.wge(tensor, S("ywl"), 16 * 8)
                    ctx.wait(tensor, S("woc"))
                    if layer == 1:
                        ctx.wait(tensor, S("boc"))
                        ctx.wait(tensor, S("ob"))
                    g = 0
                    for j in range(NTILE):
                        for n in range(2):
                            if g >= 2:
                                ctx.wait(tensor, S("oev"), g - 1)
                            for k in range(8):
                                inst = tensor.matmul(
                                    ps_o[g % 2][:],
                                    yw[k][:, j * 128:(j + 1) * 128],
                                    wo_bf[:, k * D + n * 512:k * D + (n + 1) * 512],
                                    start=(k == 0), stop=(layer == 0 and k == 7))
                            if layer == 1:
                                inst = tensor.matmul(ps_o[g % 2][:], ones_bf[:],
                                                     bo_bf[0:1, n * 512:(n + 1) * 512],
                                                     start=False, stop=True)
                            ctx.inc(inst, S("omm"), 1)
                            g += 1

                @block.scalar
                def _(scalar):
                    g = 0
                    for j in range(NTILE):
                        for n in range(2):
                            ctx.wait(scalar, S("omm"), g + 1)
                            if g >= 2:
                                ctx.wait(scalar, S("res"), g - 1)
                            ctx.inc(scalar.copy(osb[g % 2][:], ps_o[g % 2][:]), S("oev"), 1)
                            g += 1

                @block.vector
                def _(vector):
                    g = 0
                    for j in range(NTILE):
                        ctx.wge(vector, S("xld") + str(j % 2), 16 * (j // 2 + 1))
                        if j >= 2:
                            ctx.wge(vector, S("out") + str(j % 2), 16 * (j // 2))
                        for n in range(2):
                            ctx.wait(vector, S("oev"), g + 1)
                            ctx.inc(vector.tensor_add(xnew[j % 2][:, n * 512:(n + 1) * 512],
                                                      osb[g % 2][:],
                                                      xr[j % 2][:, n * 512:(n + 1) * 512]),
                                    S("res"), 1)
                            g += 1

                @block.gpsimd
                def _(gpsimd):
                    for j in range(NTILE):
                        ctx.wait(gpsimd, S("res"), 2 * (j + 1))
                        ctx.inc(gpsimd.dma_start(out=xout[j * 128:(j + 1) * 128, :], in_=xnew[j % 2][:]),
                                S("out") + str(j % 2), 16)

    return [(S("out0"), ctx.val(S("out0"))), (S("out1"), ctx.val(S("out1")))]


def emit_ffn0(ctx, io, xin, xout, pfx, consts):
    nc = ctx.nc
    gw_d = io.inp("l0_gwT", [D, NE0], F32)
    w1_d = io.inp("l0_w1", [2 * D, DFF0], F32)
    w2_d = io.inp("l0_w2", [2 * DFF0, D], F32)
    um_d = io.inp("ustrict", [128, 128], F32)
    iom_d = io.inp("iotam", [128, NE0], F32)
    isl_d = io.inp("islot8", [128, 8], F32)
    pm_d = io.inp("prevmask", [8, 1], F32)
    es_d = io.inp("eselr", [2, NE0], F32)
    sb_d = io.inp("sendbase", [128, NE0], F32)
    lt_d = io.inp("ltri8", [8, 8], F32)

    xln_dram = nc.dram_tensor(pfx + "xln", [TOWN, D], BF16)
    PADR = NCORES * 2 * SCAP
    send_dram = nc.dram_tensor(pfx + "send", [PADR + 8, D], BF16)
    recv_dram = nc.dram_tensor(pfx + "recv", [PADR + 8, D], BF16)
    disp_dram = nc.dram_tensor(pfx + "disp", [2 * CAP0, D], BF16)
    back_dram = nc.dram_tensor(pfx + "back", [PADR + 8, D], BF16)
    ret_dram = nc.dram_tensor(pfx + "ret", [PADR + 8, D], BF16)
    cnt_dram = nc.dram_tensor(pfx + "cnt", [1, NE0], F32)
    cag_dram = nc.dram_tensor(pfx + "cag", [8, NE0], F32)

    with ExitStack() as st_ln:
        ln_deps = emit_ln_sweep(ctx, st_ln, xin, xln_dram.ap(), pfx)

    S = lambda n: pfx + n
    NSLE = CAP0 // 128

    with ExitStack() as st:
        dst_sb = st.enter_context(nc.sbuf_tensor(pfx + "dst", [128, NTILE], I32))
        srcx_sb = st.enter_context(nc.sbuf_tensor(pfx + "srcx", [128, 2 * NSLE], I32))
        ph_sb = st.enter_context(nc.sbuf_tensor(pfx + "phv", [128, NTILE], F32))
        zt_sb = st.enter_context(nc.sbuf_tensor(pfx + "zt", [128, D], BF16))

        with ExitStack() as st2:
            xlnT = st2.enter_context(nc.sbuf_tensor(pfx + "xlnT", [128, 8 * TOWN], BF16))
            gwt = st2.enter_context(nc.sbuf_tensor(pfx + "gwt", [128, NE0], F32))
            gw_bf = st2.enter_context(nc.sbuf_tensor(pfx + "gwbf", [128, 8 * NE0], BF16))
            um_sb = st2.enter_context(nc.sbuf_tensor(pfx + "um", [128, 128], F32))
            iom_sb = st2.enter_context(nc.sbuf_tensor(pfx + "iom", [128, NE0], F32))
            isl_sb = st2.enter_context(nc.sbuf_tensor(pfx + "isl", [128, 8], F32))
            pm_sb = st2.enter_context(nc.sbuf_tensor(pfx + "pm", [8, 1], F32))
            es_sb = [st2.enter_context(nc.sbuf_tensor(f"{pfx}es{i}", [1, NE0], F32)) for i in range(2)]
            sb_sb = st2.enter_context(nc.sbuf_tensor(pfx + "sbase", [128, NE0], F32))
            lt_sb = st2.enter_context(nc.sbuf_tensor(pfx + "lt", [8, 8], F32))
            capt = st2.enter_context(nc.sbuf_tensor(pfx + "capt", [8, NE0], F32))
            c640 = st2.enter_context(nc.sbuf_tensor(pfx + "c640", [1, 8], F32))
            lg_sb = st2.enter_context(nc.sbuf_tensor(pfx + "lg", [128, NE0], F32))
            st_sb = st2.enter_context(nc.sbuf_tensor(pfx + "rstats", [128, 8], F32))
            ex_sb = st2.enter_context(nc.sbuf_tensor(pfx + "exs", [128, NE0], F32))
            msk_sb = st2.enter_context(nc.sbuf_tensor(pfx + "mask", [128, NTILE * NE0], F32))
            rnk_sb = st2.enter_context(nc.sbuf_tensor(pfx + "rnk", [128, NTILE * NE0], F32))
            carry = [st2.enter_context(nc.sbuf_tensor(f"{pfx}carry{i}", [1, NE0], F32)) for i in range(2)]
            cagg = st2.enter_context(nc.sbuf_tensor(pfx + "cagg", [8, NE0], F32))
            kcA = st2.enter_context(nc.sbuf_tensor(pfx + "kcA", [8, NE0], F32))
            kcB = st2.enter_context(nc.sbuf_tensor(pfx + "kcB", [8, NE0], F32))
            kc_sb = st2.enter_context(nc.sbuf_tensor(pfx + "kc", [8, NE0], F32))
            thr_sb = st2.enter_context(nc.sbuf_tensor(pfx + "thr", [128, NE0], F32))
            eselb = st2.enter_context(nc.sbuf_tensor(pfx + "eselb", [8, NE0], F32))
            kce = st2.enter_context(nc.sbuf_tensor(pfx + "kce", [8, 1], F32))
            cbrow = st2.enter_context(nc.sbuf_tensor(pfx + "cbrow", [1, 8], F32))
            kincb = st2.enter_context(nc.sbuf_tensor(pfx + "kincb", [128, 8], F32))
            cbb = st2.enter_context(nc.sbuf_tensor(pfx + "cbb", [128, 8], F32))
            dscr = st2.enter_context(nc.sbuf_tensor(pfx + "dscr", [128, 4 * NE0], F32))
            dcol = st2.enter_context(nc.sbuf_tensor(pfx + "dcol", [128, 8], F32))
            dgb = [st2.enter_context(nc.sbuf_tensor(f"{pfx}dgb{i}", [128, D], BF16)) for i in range(2)]
            xrl = st2.enter_context(nc.sbuf_tensor(pfx + "xrl", [128, D], BF16))
            ps_g = [st2.enter_context(nc.psum_tensor(f"{pfx}psg{i}", [128, NE0], F32)) for i in range(2)]
            ps_r = st2.enter_context(nc.psum_tensor(pfx + "psr", [128, NE0], F32))
            ps_c = st2.enter_context(nc.psum_tensor(pfx + "psc", [1, NE0], F32))
            ps_m = st2.enter_context(nc.psum_tensor(pfx + "psm", [8, NE0], F32))
            ps_b = st2.enter_context(nc.psum_tensor(pfx + "psb", [128, 16], F32))
            ps_t = st2.enter_context(nc.psum_tensor(pfx + "pstt", [8, 8], F32))

            with nc.Block() as block:
                @block.sync
                def _(sync):
                    wait_deps(ctx, sync, ln_deps)
                    for t, d_ in [(um_sb, um_d), (iom_sb, iom_d), (isl_sb, isl_d),
                                  (pm_sb, pm_d), (sb_sb, sb_d), (lt_sb, lt_d)]:
                        ctx.inc(sync.dma_start(out=t[:], in_=d_[:, :]), S("cl"), 16)
                    for i in range(2):
                        ctx.inc(sync.dma_start(out=es_sb[i][:], in_=es_d[i:i + 1, :]), S("cl"), 16)
                    for k in range(8):
                        if k >= 1:
                            ctx.wait(sync, S("gwc"), k)
                        ctx.inc(sync.dma_start(out=gwt[:], in_=gw_d[k * 128:(k + 1) * 128, :]),
                                S("gwl"), 16)
                        ctx.wge(sync, S("gwl"), 16 * (k + 1))
                    for k in range(8):
                        ctx.inc(sync.dma_start(out=xlnT[:, k * TOWN:(k + 1) * TOWN],
                                               in_=xln_dram.ap()[:, k * 128:(k + 1) * 128],
                                               transpose=True), S("ldT"), 16)

                @block.vector
                def _(vector):
                    wait_deps(ctx, vector, ln_deps)
                    vector.memset(zt_sb[:], 0.0)
                    ctx.inc(vector.memset(carry[1][:], 0.0), S("cu"), 1)
                    vector.memset(capt[:], float(CAP0))
                    ctx.inc(vector.memset(c640[:], float(2 * SCAP)), S("cz"), 1)
                    for k in range(8):
                        ctx.wge(vector, S("gwl"), 16 * (k + 1))
                        ctx.inc(vector.tensor_copy(gw_bf[:, k * NE0:(k + 1) * NE0], gwt[:]), S("gwc"), 1)

                @block.sync
                def _(sync):
                    ctx.wait(sync, S("cz"), 1)
                    for t in range(NCORES * 2 * SCAP // 128):
                        ctx.inc(sync.dma_start(out=send_dram.ap()[t * 128:(t + 1) * 128, :],
                                               in_=zt_sb[:]), S("sz"), 16)
                    ctx.inc(sync.dma_start(out=recv_dram.ap()[PADR:PADR + 8, :],
                                           in_=zt_sb[0:8, :]), S("pz"), 16)
                    ctx.inc(sync.dma_start(out=ret_dram.ap()[PADR:PADR + 8, :],
                                           in_=zt_sb[0:8, :]), S("pz"), 16)

            with nc.Block() as block:
                @block.tensor
                def _(tensor):
                    ctx.wait(tensor, S("gwc"))
                    ctx.wge(tensor, S("ldT"), 16 * 8)
                    for j in range(NTILE):
                        if j >= 2:
                            ctx.wait(tensor, S("gev"), j - 1)
                        for k in range(8):
                            inst = tensor.matmul(
                                ps_g[j % 2][:],
                                xlnT[:, k * TOWN + j * 128:k * TOWN + (j + 1) * 128],
                                gw_bf[:, k * NE0:(k + 1) * NE0],
                                start=(k == 0), stop=(k == 7))
                        ctx.inc(inst, S("gmm"), 1)

                @block.scalar
                def _(scalar):
                    for j in range(NTILE):
                        ctx.wait(scalar, S("gmm"), j + 1)
                        if j >= 1:
                            ctx.wait(scalar, S("lgdone"), j)
                        ctx.inc(scalar.copy(lg_sb[:], ps_g[j % 2][:]), S("gev"), 1)
                        ctx.wait(scalar, S("rmax"), j + 1)
                        ctx.inc(scalar.activation(out=ex_sb[:], in_=lg_sb[:], func=AF.Exp,
                                                  bias=st_sb[:, 1:2]), S("gex"), 1)

                @block.vector
                def _(vector):
                    ch = S("rch")
                    for j in range(NTILE):
                        ctx.wge(vector, S("gev"), j + 1)
                        ctx.inc(vector.reduce_max(st_sb[:, 0:1], lg_sb[:], axis=AX), ch, 1)
                        ctx.wait(vector, ch)
                        ctx.inc(vector.tensor_scalar_mul(st_sb[:, 1:2], st_sb[:, 0:1], -1.0), S("rmax"), 1)
                        ctx.wait(vector, S("gex"), j + 1)
                        ctx.inc(vector.reduce_sum(st_sb[:, 2:3], ex_sb[:], axis=AX), ch, 1)
                        ctx.wait(vector, ch)
                        ctx.inc(vector.reciprocal(st_sb[:, 3:4], st_sb[:, 2:3]), ch, 1)
                        ctx.wait(vector, ch)
                        ctx.inc(vector.tensor_scalar_mul(ph_sb[:, j:j + 1], st_sb[:, 3:4], 0.5), ch, 1)
                        ctx.inc(vector.tensor_single_scalar(msk_sb[:, j * NE0:(j + 1) * NE0],
                                                            lg_sb[:], st_sb[:, 0:1], OP.is_equal),
                                S("lgdone"), 1)

            with nc.Block() as block:
                @block.tensor
                def _(tensor):
                    ctx.wait(tensor, S("cl"), ctx.val(S("cl")))
                    for j in range(NTILE):
                        ctx.wait(tensor, S("lgdone"), j + 1)
                        ctx.wait(tensor, S("cu"), j + 1)
                        if j >= 1:
                            ctx.wait(tensor, S("rnkev"), j)
                        tensor.matmul(ps_r[:], um_sb[:], msk_sb[:, j * NE0:(j + 1) * NE0],
                                      start=True, stop=False)
                        inst = tensor.matmul(ps_r[:], consts["ones1_f32"][:],
                                             carry[(j + 1) % 2][:], start=False, stop=True)
                        ctx.inc(inst, S("rnkmm"), 1)
                        inst = tensor.matmul(ps_c[:], consts["onescol_f32"][:],
                                             msk_sb[:, j * NE0:(j + 1) * NE0],
                                             start=True, stop=True)
                        ctx.inc(inst, S("cntmm"), 1)

                @block.scalar
                def _(scalar):
                    for j in range(NTILE):
                        ctx.wait(scalar, S("rnkmm"), j + 1)
                        ctx.inc(scalar.copy(rnk_sb[:, j * NE0:(j + 1) * NE0], ps_r[:]), S("rnkev"), 1)

                @block.vector
                def _(vector):
                    for j in range(NTILE):
                        ctx.wait(vector, S("cntmm"), j + 1)
                        ctx.inc(vector.tensor_add(carry[j % 2][:], carry[(j + 1) % 2][:], ps_c[:]),
                                S("cu"), 1)

                @block.gpsimd
                def _(gpsimd):
                    ctx.wait(gpsimd, S("cu"), NTILE + 1)
                    ctx.inc(gpsimd.dma_start(out=cnt_dram.ap(), in_=carry[(NTILE + 1) % 2][:]),
                            S("cst"), 16)
                    ctx.wge(gpsimd, S("cst"), 16)
                    ctx.inc(gpsimd.collective_compute(
                        "AllGather", OP.bypass, replica_groups=[list(range(NCORES))],
                        ins=[cnt_dram.ap()], outs=[cag_dram.ap()]), S("ccg"), 1)
                    ctx.wait(gpsimd, S("ccg"))
                    ctx.inc(gpsimd.dma_start(out=cagg[:], in_=cag_dram.ap()), S("cagl"), 16)

            with nc.Block() as block:
                @block.tensor
                def _(tensor):
                    ctx.wge(tensor, S("cagl"), 16)
                    ctx.wait(tensor, S("cntmm"), NTILE)
                    ctx.wait(tensor, S("cu"), NTILE + 1)
                    inst = tensor.matmul(ps_c[:], pm_sb[:], cagg[:], start=True, stop=True)
                    ctx.inc(inst, S("pcm"), 1)
                    inst = tensor.matmul(ps_m[:], um_sb[0:8, 0:8], cagg[:], start=True, stop=True)
                    ctx.inc(inst, S("kcm"), 1)

                @block.vector
                def _(vector):
                    ctx.wait(vector, S("pcm"), 1)
                    ctx.inc(vector.scalar_tensor_tensor(dscr[0:1, 0:NE0], ps_c[:], -1.0,
                                                        capt[0:1, :], OP.mult, OP.add), S("th1"), 1)
                    ctx.wait(vector, S("kcm"), 1)
                    ctx.inc(vector.scalar_tensor_tensor(kcA[:], ps_m[:], -1.0, capt[:],
                                                        OP.mult, OP.add), S("kk1"), 1)
                    ctx.wait(vector, S("kk1"))
                    ctx.inc(vector.tensor_scalar_max(kcB[:], kcA[:], 0.0), S("kk1"), 1)
                    ctx.wait(vector, S("kk1"))
                    ctx.inc(vector.tensor_tensor(kc_sb[:], kcB[:], cagg[:], OP.min), S("kk1"), 1)

                @block.tensor
                def _(tensor):
                    ctx.wait(tensor, S("th1"), 1)
                    inst = tensor.matmul(ps_r[:], consts["ones1_f32"][:], dscr[0:1, 0:NE0],
                                         start=True, stop=True)
                    ctx.inc(inst, S("thm"), 1)

                @block.scalar
                def _(scalar):
                    ctx.wait(scalar, S("thm"), 1)
                    ctx.inc(scalar.copy(thr_sb[:], ps_r[:]), S("thc"), 1)

            with nc.Block() as block:
                @block.vector
                def _(vector):
                    ch = S("dch")
                    ctx.wait(vector, S("thc"), 1)
                    for j in range(NTILE):
                        mj = msk_sb[:, j * NE0:(j + 1) * NE0]
                        rj = rnk_sb[:, j * NE0:(j + 1) * NE0]
                        ctx.wait(vector, S("rnkev"), j + 1)
                        ctx.inc(vector.tensor_mul(dscr[:, 0:NE0], rj, mj), ch, 1)
                        ctx.inc(vector.tensor_mul(dscr[:, NE0:2 * NE0], sb_sb[:], mj), ch, 1)
                        ctx.inc(vector.tensor_tensor(dscr[:, 2 * NE0:3 * NE0], rj, thr_sb[:],
                                                     OP.is_lt), ch, 1)
                        ctx.wait(vector, ch)
                        ctx.inc(vector.tensor_mul(dscr[:, 3 * NE0:4 * NE0],
                                                  dscr[:, 2 * NE0:3 * NE0], mj), ch, 1)
                        ctx.inc(vector.reduce_sum(dcol[:, 0:1], dscr[:, 0:NE0], axis=AX), ch, 1)
                        ctx.inc(vector.reduce_sum(dcol[:, 1:2], dscr[:, NE0:2 * NE0], axis=AX), ch, 1)
                        ctx.wait(vector, ch)
                        ctx.inc(vector.reduce_sum(dcol[:, 2:3], dscr[:, 3 * NE0:4 * NE0], axis=AX),
                                ch, 1)
                        ctx.wait(vector, ch)
                        ctx.inc(vector.tensor_scalar(dcol[:, 3:4], dcol[:, 2:3], 1.0, -1e9,
                                                     OP.subtract, OP.mult), ch, 1)
                        ctx.inc(vector.tensor_scalar(dcol[:, 4:5], dcol[:, 0:1], float(SCAP), 1e9,
                                                     OP.is_ge, OP.mult), ch, 1)
                        ctx.wait(vector, ch)
                        ctx.inc(vector.tensor_add(dcol[:, 5:6], dcol[:, 0:1], dcol[:, 1:2]), ch, 1)
                        ctx.wait(vector, ch)
                        ctx.inc(vector.tensor_add(dcol[:, 6:7], dcol[:, 5:6], dcol[:, 3:4]), ch, 1)
                        ctx.wait(vector, ch)
                        ctx.inc(vector.tensor_add(dcol[:, 7:8], dcol[:, 6:7], dcol[:, 4:5]), ch, 1)
                        ctx.wait(vector, ch)
                        ctx.inc(vector.tensor_scalar_min(dcol[:, 6:7], dcol[:, 7:8],
                                                         float(PADR)), ch, 1)
                        ctx.wait(vector, ch)
                        ctx.inc(vector.tensor_copy(dst_sb[:, j:j + 1], dcol[:, 6:7]), S("dstd"), 1)

            for sl in range(2):
                with nc.Block() as block:
                    @block.tensor
                    def _(tensor):
                        if sl == 0:
                            ctx.wait(tensor, S("kk1"), 3)
                        else:
                            ctx.wait(tensor, S("srcd"), NSLE)
                        inst = tensor.matmul(ps_m[:], consts["ones1_f32"][:, 0:8],
                                             es_sb[sl][:], start=True, stop=True)
                        ctx.inc(inst, S("esm"), 1)
                        ctx.wait(tensor, S("kce"), sl + 1)
                        inst = tensor.transpose(ps_c[0:1, 0:8], kce[:],
                                                consts["ident_f32"][0:8, 0:8])
                        ctx.inc(inst, S("ktr"), 1)
                        inst = tensor.matmul(ps_t[:, 0:1], lt_sb[:], kce[:], start=True, stop=True)
                        ctx.inc(inst, S("kim"), 1)
                        ctx.wait(tensor, S("kiev"), sl + 1)
                        inst = tensor.transpose(ps_c[0:1, 8:16], dcol[0:8, 0:1],
                                                consts["ident_f32"][0:8, 0:8])
                        ctx.inc(inst, S("kirt"), 1)
                        ctx.wait(tensor, S("x1"), sl + 1)
                        inst = tensor.matmul(ps_b[:, 0:8], consts["ones1_f32"][:],
                                             dscr[0:1, 32:40], start=True, stop=True)
                        ctx.inc(inst, S("kbm"), 1)
                        ctx.wait(tensor, S("cbr"), sl + 1)
                        inst = tensor.matmul(ps_b[:, 8:16], consts["ones1_f32"][:], cbrow[:],
                                             start=True, stop=True)
                        ctx.inc(inst, S("cbm"), 1)

                    @block.scalar
                    def _(scalar):
                        ctx.wait(scalar, S("esm"), sl + 1)
                        ctx.inc(scalar.copy(eselb[:], ps_m[:]), S("esb"), 1)
                        ctx.wait(scalar, S("ktr"), sl + 1)
                        ctx.inc(scalar.copy(dscr[0:1, 40:48], ps_c[0:1, 0:8]), S("x2"), 1)
                        ctx.wait(scalar, S("kim"), sl + 1)
                        ctx.inc(scalar.copy(dcol[0:8, 0:1], ps_t[:, 0:1]), S("kiev"), 1)
                        ctx.wait(scalar, S("kirt"), sl + 1)
                        ctx.inc(scalar.copy(dscr[0:1, 32:40], ps_c[0:1, 8:16]), S("x1"), 1)
                        ctx.wait(scalar, S("kbm"), sl + 1)
                        ctx.inc(scalar.copy(kincb[:], ps_b[:, 0:8]), S("kbc"), 1)
                        ctx.wait(scalar, S("cbm"), sl + 1)
                        ctx.inc(scalar.copy(cbb[:], ps_b[:, 8:16]), S("cbc"), 1)

                    @block.vector
                    def _(vector):
                        ch = S("sch")
                        ctx.wait(vector, S("esb"), sl + 1)
                        ctx.inc(vector.tensor_mul(dscr[0:8, 0:NE0], kc_sb[:], eselb[:]), ch, 1)
                        ctx.wait(vector, ch)
                        ctx.inc(vector.reduce_sum(kce[:], dscr[0:8, 0:NE0], axis=AX),
                                S("kce"), 1)
                        ctx.wait(vector, S("x2"), sl + 1)
                        ctx.inc(vector.scalar_tensor_tensor(cbrow[:], dscr[0:1, 40:48], -1.0,
                                                            c640[:], OP.mult, OP.add),
                                S("cbr"), 1)
                        ctx.wait(vector, S("kbc"), sl + 1)
                        ctx.wait(vector, S("cbc"), sl + 1)
                        for t in range(NSLE):
                            ctx.inc(vector.tensor_scalar_add(dcol[:, 1:2], isl_sb[:, 0:1],
                                                             float(128 * t)), ch, 1)
                            ctx.wait(vector, ch)
                            ctx.inc(vector.tensor_single_scalar(dscr[:, 0:8], kincb[:],
                                                                dcol[:, 1:2], OP.is_le), ch, 1)
                            ctx.wait(vector, ch)
                            ctx.inc(vector.tensor_mul(dscr[:, 8:16], dscr[:, 0:8], cbb[:]), ch, 1)
                            ctx.wait(vector, ch)
                            ctx.inc(vector.reduce_sum(dcol[:, 2:3], dscr[:, 8:16], axis=AX), ch, 1)
                            ctx.wait(vector, ch)
                            ctx.inc(vector.tensor_add(dcol[:, 3:4], dcol[:, 1:2], dcol[:, 2:3]),
                                    ch, 1)
                            ctx.wait(vector, ch)
                            ctx.inc(vector.tensor_scalar_add(dcol[:, 4:5], dcol[:, 3:4],
                                                             float(SCAP * sl)), ch, 1)
                            ctx.wait(vector, ch)
                            ctx.inc(vector.tensor_scalar_min(dcol[:, 5:6], dcol[:, 4:5],
                                                             float(PADR)), ch, 1)
                            ctx.wait(vector, ch)
                            ctx.inc(vector.tensor_copy(srcx_sb[:, sl * NSLE + t:sl * NSLE + t + 1],
                                                       dcol[:, 5:6]), S("srcd"), 1)

            with nc.Block() as block:
                @block.sync
                def _(sync):
                    for j in range(NTILE):
                        if j > 0:
                            ctx.wait(sync, S("sct"), 16 * j)
                        ctx.inc(sync.dma_start(out=xrl[:], in_=xln_dram.ap()[j * 128:(j + 1) * 128, :]),
                                S("xrl"), 16)

                @block.gpsimd
                def _(gpsimd):
                    ctx.wge(gpsimd, S("sz"), ctx.val(S("sz")))
                    ctx.wait(gpsimd, S("dstd"), NTILE)
                    for j in range(NTILE):
                        ctx.wge(gpsimd, S("xrl"), 16 * (j + 1))
                        ctx.inc(gpsimd.indirect_dma_start(
                            out=send_dram.ap(), out_offset=IndirectOffsetOnAxis(
                                ap=dst_sb[:, j:j + 1], axis=0),
                            in_=xrl[:], in_offset=None), S("sct"), 16)
                    ctx.wge(gpsimd, S("sct"), 16 * NTILE)
                    with allow_ici_all_to_all():
                        ctx.inc(gpsimd.collective_compute(
                            "AllToAll", OP.bypass, replica_groups=[list(range(NCORES))],
                            ins=[send_dram.ap()[0:PADR, :]],
                            outs=[recv_dram.ap()[0:PADR, :]]), S("a2a1"), 1)

            with nc.Block() as block:
                @block.vector
                def _(vector):
                    for t in range(2 * NSLE):
                        if t >= 2:
                            ctx.wge(vector, S("dgs") + str(t % 2), 16 * (t // 2))
                        ctx.inc(vector.memset(dgb[t % 2][:], 0.0), S("dgz"), 1)

                @block.sync
                def _(sync):
                    ctx.wait(sync, S("cz"), 1)
                    for t in range(NCORES * 2 * SCAP // 128):
                        ctx.inc(sync.dma_start(out=back_dram.ap()[t * 128:(t + 1) * 128, :],
                                               in_=zt_sb[:]), S("bz"), 16)

                @block.gpsimd
                def _(gpsimd):
                    ctx.wait(gpsimd, S("a2a1"))
                    ctx.wait(gpsimd, S("pz"))
                    ctx.wait(gpsimd, S("srcd"), 2 * NSLE)
                    for t in range(2 * NSLE):
                        ctx.wait(gpsimd, S("dgz"), t + 1)
                        ctx.inc(gpsimd.indirect_dma_start(
                            out=dgb[t % 2][:], out_offset=None,
                            in_=recv_dram.ap(), in_offset=IndirectOffsetOnAxis(
                                ap=srcx_sb[:, t:t + 1], axis=0)),
                            S("dgg") + str(t % 2), 16)

                @block.sync
                def _(sync):
                    for t in range(2 * NSLE):
                        ctx.wge(sync, S("dgg") + str(t % 2), 16 * (t // 2 + 1))
                        ctx.inc(sync.dma_start(out=disp_dram.ap()[t * 128:(t + 1) * 128, :],
                                               in_=dgb[t % 2][:]), S("dgs") + str(t % 2), 16)


        with ExitStack() as st2:
            DT = st2.enter_context(nc.sbuf_tensor(pfx + "DT", [128, 8 * 2 * CAP0], BF16))
            wt4 = [st2.enter_context(nc.sbuf_tensor(f"{pfx}wt4{i}", [128, DFF0 // 2], F32))
                   for i in range(2)]
            wbf = st2.enter_context(nc.sbuf_tensor(pfx + "wbf", [128, 8 * DFF0], BF16))
            HCAP = CAP0 // 2
            hT = st2.enter_context(nc.sbuf_tensor(pfx + "hT", [128, 32 * HCAP], BF16))
            ob_sb = [st2.enter_context(nc.sbuf_tensor(f"{pfx}ob{i}", [128, D], BF16)) for i in range(2)]
            ps_h = [st2.enter_context(nc.psum_tensor(f"{pfx}psh{i}", [128, 512], F32)) for i in range(2)]
            ps_o = [st2.enter_context(nc.psum_tensor(f"{pfx}pso{i}", [128, 512], F32)) for i in range(2)]

            with nc.Block() as block:
                @block.sync
                def _(sync):
                    ctx.wait(sync, S("dgs0"), ctx.val(S("dgs0")))
                    ctx.wait(sync, S("dgs1"), ctx.val(S("dgs1")))
                    for k in range(8):
                        ctx.inc(sync.dma_start(out=DT[:, k * 2 * CAP0:(k + 1) * 2 * CAP0],
                                               in_=disp_dram.ap()[:, k * 128:(k + 1) * 128],
                                               transpose=True), S("dT"), 16)

            SB_H = [(b0, min(512, HCAP - b0)) for b0 in range(0, HCAP, 512)]
            NBH = len(SB_H)
            NSLH = HCAP // 128
            cnt = {"w1l": 0, "w1c": 0, "w2l": 0, "w2c": 0, "hmm": 0, "hev": 0,
                   "omm": 0, "oev": 0, "obst": 0}
            for sl in range(2):
                for sh in range(2):
                    with nc.Block() as block:
                        @block.sync
                        def _(sync):
                            if cnt["hmm"] > 0:
                                ctx.wait(sync, S("hmm"), cnt["hmm"])
                            if cnt["omm"] > 0:
                                ctx.wait(sync, S("omm"), cnt["omm"])
                            for kc in range(8):
                                for hf in range(2):
                                    g = cnt["w1l"] + kc * 2 + hf
                                    if kc * 2 + hf >= 2:
                                        ctx.wait(sync, S("w1c"), g - 1)
                                    ctx.inc(sync.dma_start(
                                        out=wt4[g % 2][:],
                                        in_=w1_d[sl * D + kc * 128:sl * D + (kc + 1) * 128,
                                                 hf * 2048:(hf + 1) * 2048]), S("w1l"), 16)
                                    ctx.wge(sync, S("w1l"), 16 * (g + 1))

                        @block.vector
                        def _(vector):
                            for kc in range(8):
                                for hf in range(2):
                                    g = cnt["w1l"] + kc * 2 + hf
                                    ctx.wge(vector, S("w1l"), 16 * (g + 1))
                                    ctx.inc(vector.tensor_copy(
                                        wbf[:, kc * DFF0 + hf * 2048:kc * DFF0 + (hf + 1) * 2048],
                                        wt4[g % 2][:]), S("w1c"), 1)
                    cnt["w1l"] += 16
                    cnt["w1c"] += 16

                    for fgrp in range(4):
                        with nc.Block() as block:
                            @block.tensor
                            def _(tensor):
                                if fgrp == 0:
                                    ctx.wait(tensor, S("w1c"), cnt["w1c"])
                                    if sl == 0 and sh == 0:
                                        ctx.wge(tensor, S("dT"), 16 * 8)
                                    if cnt["oev"] > 0:
                                        ctx.wait(tensor, S("oev"), cnt["oev"])
                                for f in range(fgrp * 8, (fgrp + 1) * 8):
                                    for bi, (b0, bw) in enumerate(SB_H):
                                        g = cnt["hmm"] + (f - 0) * NBH + bi
                                        if g >= 2:
                                            ctx.wait(tensor, S("hev"), g - 1)
                                        base = sl * CAP0 + sh * HCAP + b0
                                        for k in range(8):
                                            inst = tensor.matmul(
                                                ps_h[g % 2][:, 0:bw],
                                                wbf[:, k * DFF0 + f * 128:k * DFF0 + (f + 1) * 128],
                                                DT[:, k * 2 * CAP0 + base:k * 2 * CAP0 + base + bw],
                                                start=(k == 0), stop=(k == 7))
                                        ctx.inc(inst, S("hmm"), 1)

                            @block.scalar
                            def _(scalar):
                                for f in range(fgrp * 8, (fgrp + 1) * 8):
                                    for bi, (b0, bw) in enumerate(SB_H):
                                        g = cnt["hmm"] + f * NBH + bi
                                        ctx.wait(scalar, S("hmm"), g + 1)
                                        ctx.inc(scalar.activation(
                                            out=hT[:, f * HCAP + b0:f * HCAP + b0 + bw],
                                            in_=ps_h[g % 2][:, 0:bw], func=AF.Gelu_apprx_tanh),
                                            S("hev"), 1)
                    cnt["hmm"] += 32 * NBH
                    cnt["hev"] += 32 * NBH

                    with nc.Block() as block:
                        @block.sync
                        def _(sync):
                            ctx.wait(sync, S("hmm"), cnt["hmm"])
                            for kc in range(32):
                                g = cnt["w2l"] + kc
                                if kc >= 2:
                                    ctx.wait(sync, S("w2c"), g - 1)
                                ctx.inc(sync.dma_start(
                                    out=wt4[g % 2][:, 0:1024],
                                    in_=w2_d[sl * DFF0 + kc * 128:sl * DFF0 + (kc + 1) * 128, :]),
                                    S("w2l"), 16)
                                ctx.wge(sync, S("w2l"), 16 * (g + 1))

                        @block.vector
                        def _(vector):
                            for kc in range(32):
                                g = cnt["w2l"] + kc
                                ctx.wge(vector, S("w2l"), 16 * (g + 1))
                                ctx.inc(vector.tensor_copy(wbf[:, kc * 1024:(kc + 1) * 1024],
                                                           wt4[g % 2][:, 0:1024]), S("w2c"), 1)
                    cnt["w2l"] += 32
                    cnt["w2c"] += 32

                    with nc.Block() as block:
                        @block.tensor
                        def _(tensor):
                            ctx.wait(tensor, S("w2c"), cnt["w2c"])
                            ctx.wait(tensor, S("hev"), cnt["hev"])
                            for t in range(NSLH):
                                for n in range(2):
                                    g = cnt["omm"] + t * 2 + n
                                    if g >= 2:
                                        ctx.wait(tensor, S("oev"), g - 1)
                                    for f in range(32):
                                        inst = tensor.matmul(
                                            ps_o[g % 2][:],
                                            hT[:, f * HCAP + t * 128:f * HCAP + (t + 1) * 128],
                                            wbf[:, f * 1024 + n * 512:f * 1024 + (n + 1) * 512],
                                            start=(f == 0), stop=(f == 31))
                                    ctx.inc(inst, S("omm"), 1)

                        @block.scalar
                        def _(scalar):
                            for t in range(NSLH):
                                for n in range(2):
                                    g = cnt["omm"] + t * 2 + n
                                    ctx.wait(scalar, S("omm"), g + 1)
                                    gt = cnt["obst"] // 16 + t
                                    if gt >= 2:
                                        ctx.wait(scalar, S("obst"), 16 * (gt - 1))
                                    ctx.inc(scalar.copy(ob_sb[t % 2][:, n * 512:(n + 1) * 512],
                                                        ps_o[g % 2][:]), S("oev"), 1)

                        @block.gpsimd
                        def _(gpsimd):
                            if sl == 0 and sh == 0:
                                ctx.wge(gpsimd, S("bz"), ctx.val(S("bz")))
                            for t in range(NSLH):
                                g2 = cnt["omm"] + t * 2
                                ctx.wait(gpsimd, S("oev"), g2 + 2)
                                col = sl * NSLE + sh * NSLH + t
                                ctx.inc(gpsimd.indirect_dma_start(
                                    out=back_dram.ap(), out_offset=IndirectOffsetOnAxis(
                                        ap=srcx_sb[:, col:col + 1], axis=0),
                                    in_=ob_sb[t % 2][:], in_offset=None),
                                    S("obst"), 16)
                    cnt["omm"] += 2 * NSLH
                    cnt["oev"] += 2 * NSLH
                    cnt["obst"] += 16 * NSLH

        with ExitStack() as st2:
            rws = st2.enter_context(nc.sbuf_tensor(pfx + "rws", [128, D], BF16))
            r32 = st2.enter_context(nc.sbuf_tensor(pfx + "r32", [128, D], F32))
            rt2 = st2.enter_context(nc.sbuf_tensor(pfx + "rt2", [128, D], F32))
            xr = [st2.enter_context(nc.sbuf_tensor(f"{pfx}xr{i}", [128, D], F32)) for i in range(2)]
            xnew = [st2.enter_context(nc.sbuf_tensor(f"{pfx}xnw{i}", [128, D], F32)) for i in range(2)]

            with nc.Block() as block:
                @block.gpsimd
                def _(gpsimd):
                    ctx.wge(gpsimd, S("obst"), ctx.val(S("obst")))
                    with allow_ici_all_to_all():
                        ctx.inc(gpsimd.collective_compute(
                            "AllToAll", OP.bypass, replica_groups=[list(range(NCORES))],
                            ins=[back_dram.ap()[0:PADR, :]],
                            outs=[ret_dram.ap()[0:PADR, :]]), S("a2a2"), 1)

                @block.vector
                def _(vector):
                    pass

                @block.sync
                def _(sync):
                    for j in range(NTILE):
                        if j >= 2:
                            ctx.wait(sync, S("cres"), j - 1)
                        ctx.inc(sync.dma_start(out=xr[j % 2][:], in_=xin[j * 128:(j + 1) * 128, :]),
                                S("cxl") + str(j % 2), 16)

                @block.gpsimd
                def _(gpsimd):
                    ctx.wait(gpsimd, S("a2a2"))
                    ctx.wge(gpsimd, S("pz"), 32)
                    for j in range(NTILE):
                        ctx.wait(gpsimd, S("rz"), j + 1)
                        ctx.inc(gpsimd.indirect_dma_start(
                            out=rws[:], out_offset=None,
                            in_=ret_dram.ap(), in_offset=IndirectOffsetOnAxis(
                                ap=dst_sb[:, j:j + 1], axis=0)), S("rg"), 16)
                        if j >= 1:
                            ctx.wait(gpsimd, S("cres"), j)
                            ctx.inc(gpsimd.dma_start(out=xout[(j - 1) * 128:j * 128, :],
                                                     in_=xnew[(j - 1) % 2][:]),
                                    S("cout") + str((j - 1) % 2), 16)
                    ctx.wait(gpsimd, S("cres"), NTILE)
                    ctx.inc(gpsimd.dma_start(out=xout[(NTILE - 1) * 128:NTILE * 128, :],
                                             in_=xnew[(NTILE - 1) % 2][:]),
                            S("cout") + str((NTILE - 1) % 2), 16)

                @block.vector
                def _(vector):
                    ch = S("cch")
                    ctx.inc(vector.memset(rws[:], 0.0), S("rz"), 1)
                    for j in range(NTILE):
                        ctx.wge(vector, S("rg"), 16 * (j + 1))
                        ctx.inc(vector.tensor_copy(r32[:], rws[:]), ch, 1)
                        ctx.wait(vector, ch)
                        if j < NTILE - 1:
                            ctx.inc(vector.memset(rws[:], 0.0), S("rz"), 1)
                        ctx.inc(vector.tensor_scalar_mul(rt2[:], r32[:], ph_sb[:, j:j + 1]), ch, 1)
                        ctx.wge(vector, S("cxl") + str(j % 2), 16 * (j // 2 + 1))
                        if j >= 2:
                            ctx.wge(vector, S("cout") + str(j % 2), 16 * (j // 2))
                        ctx.wait(vector, ch)
                        ctx.inc(vector.tensor_add(xnew[j % 2][:], rt2[:], xr[j % 2][:]), S("cres"), 1)

    return [(S("cout0"), ctx.val(S("cout0"))), (S("cout1"), ctx.val(S("cout1")))]


def build_consts(ctx, st, io):
    nc = ctx.nc
    ident = st.enter_context(nc.sbuf_tensor("ident_f32", [128, 128], F32))
    ones1 = st.enter_context(nc.sbuf_tensor("ones1_f32", [1, 128], F32))
    onescol = st.enter_context(nc.sbuf_tensor("onescol_f32", [128, 1], F32))
    epsc = st.enter_context(nc.sbuf_tensor("eps_col", [128, 1], F32))
    ident_d = io.inp("ident", [128, 128], F32)
    with nc.Block() as block:
        @block.sync
        def _(sync):
            ctx.inc(sync.dma_start(out=ident[:], in_=ident_d[:, :]), "cinit", 16)

        @block.vector
        def _(vector):
            vector.memset(ones1[:], 1.0)
            vector.memset(onescol[:], 1.0)
            ctx.inc(vector.memset(epsc[:], EPS), "cinit", 1)
    consts = {"ident_f32": ident, "ones1_f32": ones1, "eps_col": epsc, "onescol_f32": onescol}
    ctx.consts = consts
    return consts


def emit_lndump(ctx, io, xin, xout, pfx, consts):
    nc = ctx.nc
    xln_dram = nc.dram_tensor(pfx + "xln", [TOWN, D], BF16)
    rowin = nc.dram_tensor(pfx + "rowin", [1, D], F32)
    with ExitStack() as st_ln:
        ln_deps = emit_ln_sweep(ctx, st_ln, xin, xln_dram.ap(), pfx, row0_dram=rowin.ap())
    with ExitStack() as st:
        t_b = st.enter_context(nc.sbuf_tensor(pfx + "tb", [128, D], BF16))
        t_f = st.enter_context(nc.sbuf_tensor(pfx + "tf", [128, D], F32))
        with nc.Block() as block:
            @block.sync
            def _(sync):
                wait_deps(ctx, sync, ln_deps)
                for j in range(NTILE):
                    if j > 0:
                        ctx.wait(sync, pfx + "dcp", j)
                    ctx.inc(sync.dma_start(out=t_b[:], in_=xln_dram.ap()[j * 128:(j + 1) * 128, :]),
                            pfx + "dld", 16)

            @block.vector
            def _(vector):
                for j in range(NTILE):
                    ctx.wge(vector, pfx + "dld", 16 * (j + 1))
                    if j > 0:
                        ctx.wait(vector, pfx + "dout", 16 * j)
                    ctx.inc(vector.tensor_copy(t_f[:], t_b[:]), pfx + "dcp", 1)

            @block.gpsimd
            def _(gpsimd):
                for j in range(NTILE):
                    ctx.wge(gpsimd, pfx + "dcp", j + 1)
                    ctx.inc(gpsimd.dma_start(out=xout[j * 128:(j + 1) * 128, :], in_=t_f[:]),
                            pfx + "dout", 16)
    return [(pfx + "dout", ctx.val(pfx + "dout"))]


PHASE_EMITTERS = {
    "lndump": emit_lndump,
    "attn0": lambda ctx, io, xi, xo, pfx, c: emit_attn(ctx, io, xi, xo, pfx, c, 0),
    "attn1": lambda ctx, io, xi, xo, pfx, c: emit_attn(ctx, io, xi, xo, pfx, c, 1),
    "ffn0": emit_ffn0,
}


def build_program(phases):
    nc = bass.Bass("TRN2", target_bir_lowering=False, debug=False, detect_race_conditions=False,
                   dynamic_dma_scratch_size=16384)
    ctx = Ctx(nc)
    io = IO(nc)
    with ExitStack() as st:
        consts = build_consts(ctx, st, io)
        xin = io.inp("x", [TOWN, D], F32)
        xout = io.out("y", [TOWN, D], F32)
        n = len(phases)
        bufs = [xin.ap()]
        for i in range(1, n):
            bufs.append(nc.dram_tensor(f"xmid{i}", [TOWN, D], F32).ap())
        bufs.append(xout.ap())
        prev = None
        for i, ph in enumerate(phases):
            pfx = f"p{i}_"
            snap = nc.snapshot_sems()
            if ph == "ffn1":
                prev = emit_ffn1(ctx, io, bufs[i], bufs[i + 1], pfx, consts)
            elif ph in PHASE_EMITTERS:
                prev = PHASE_EMITTERS[ph](ctx, io, bufs[i], bufs[i + 1], pfx, consts)
            else:
                raise NotImplementedError(ph)
            deps = prev
            with nc.Block() as block:
                @block.sync
                def _(sync):
                    wait_deps(ctx, sync, deps)

                @block.gpsimd
                def _(gpsimd):
                    wait_deps(ctx, gpsimd, deps)
            if i + 1 < n:
                nc.all_engine_barrier()
                nc.clear_and_free_semaphores(nc.allocated_since(snap))
                nc.all_engine_barrier()
                nc.restore_sems(snap)
                ctx.reset_phase()
    return nc, io


def make_core_inputs(inputs, c, names):
    m = {}
    if "ident" in names:
        m["ident"] = np.eye(128, dtype=np.float32)
    if "iota8" in names:
        m["iota8"] = np.arange(8, dtype=np.float32).reshape(1, 8)
    if "rowidx" in names:
        m["rowidx"] = np.array([[(c // 4) * 4], [(c // 4) * 4]], dtype=np.int32)
    if "x" in names:
        x = np.asarray(inputs["L0_attn_x"], np.float32).reshape(2 * TBATCH, D)
        m["x"] = np.ascontiguousarray(x[c * TOWN:(c + 1) * TOWN])
    if "l0_wqkv" in names or "l1_wqkvT" in names:
        q = c % 4
        H = [4 * q + i for i in range(4)]
        if "l0_wqkv" in names:
            W = np.asarray(inputs["L0_attn_W_qkv"], np.float32)
            pk = np.zeros((D, 768), np.float32)
            for i, h in enumerate(H):
                pk[:, 128 * i:128 * i + 64] = W[:, h * 64:(h + 1) * 64]
                pk[:, 128 * i + 64:128 * i + 128] = W[:, D + h * 64:D + (h + 1) * 64]
                pk[:, 512 + 64 * i:512 + 64 * (i + 1)] = W[:, 2 * D + h * 64:2 * D + (h + 1) * 64]
            m["l0_wqkv"] = pk
            m["l0_wo_full"] = np.ascontiguousarray(np.asarray(inputs["L0_attn_W_o"], np.float32))
            half = 32
            ii = np.arange(half, dtype=np.float64)
            theta = 1.0 / (10000.0 ** (ii / half))
            t = np.arange(TBATCH, dtype=np.float64)
            ang = t[:, None] * theta[None, :]
            cos = np.cos(ang).T.astype(np.float32)
            sin = np.sin(ang).T.astype(np.float32)
            m["rope_cos"] = np.concatenate([cos, cos], axis=0).astype(ml_dtypes.bfloat16)
            m["rope_sin"] = np.concatenate([-sin, sin], axis=0).astype(ml_dtypes.bfloat16)
        if "l1_wqkvT" in names:
            Wq = np.asarray(inputs["L1_attn_Wq"], np.float32)
            Wk = np.asarray(inputs["L1_attn_Wk"], np.float32)
            Wv = np.asarray(inputs["L1_attn_Wv"], np.float32)
            bq = np.asarray(inputs["L1_attn_bq"], np.float32)
            bk = np.asarray(inputs["L1_attn_bk"], np.float32)
            bv = np.asarray(inputs["L1_attn_bv"], np.float32)
            pk = np.zeros((D, 768), np.float32)
            bqk = np.zeros((128, 4), np.float32)
            bvp = np.zeros((1, 256), np.float32)
            wql = np.zeros((64, 128), np.float32)
            wkl = np.zeros((64, 128), np.float32)
            Wql = np.asarray(inputs["L1_attn_Wq_lsr"], np.float32)
            Wkl = np.asarray(inputs["L1_attn_Wk_lsr"], np.float32)
            for i, h in enumerate(H):
                pk[:, 128 * i:128 * i + 64] = Wq[h * 64:(h + 1) * 64, :].T
                pk[:, 128 * i + 64:128 * i + 128] = Wk[h * 64:(h + 1) * 64, :].T
                pk[:, 512 + 64 * i:512 + 64 * (i + 1)] = Wv[h * 64:(h + 1) * 64, :].T
                bqk[0:64, i] = bq[h * 64:(h + 1) * 64]
                bqk[64:128, i] = bk[h * 64:(h + 1) * 64]
                bvp[0, 64 * i:64 * (i + 1)] = bv[h * 64:(h + 1) * 64]
                wql[:, 32 * i:32 * (i + 1)] = Wql[h]
                wkl[:, 32 * i:32 * (i + 1)] = Wkl[h]
            m["l1_wqkvT"] = pk
            m["l1_bqk"] = bqk
            m["l1_bv"] = bvp
            m["l1_wqlsr"] = wql
            m["l1_wklsr"] = wkl
            m["l1_woT"] = np.ascontiguousarray(np.asarray(inputs["L1_attn_Wo"], np.float32).T)
            m["l1_bo"] = np.asarray(inputs["L1_attn_bo"], np.float32).reshape(1, D)
    if "diagmask" in names:
        dm = np.zeros((512, 512), np.float32)
        for dvar in range(4):
            srow = np.arange(128)[:, None] + dvar * 128
            col = np.arange(512)[None, :]
            dm[dvar * 128:(dvar + 1) * 128] = np.where(srow <= col, 0.0, -1e30)
        m["diagmask"] = dm
    if "rebase_idx" in names:
        w = c % 4
        p = np.arange(128)[:, None]
        k = np.arange(8)[None, :]
        m["rebase_idx"] = ((128 * k + p) * 4 + w).astype(np.int32)
    if "l0_gwT" in names:
        m["l0_gwT"] = np.ascontiguousarray(np.asarray(inputs["L0_ffn_gate_w"], np.float32).T)
        m["l0_w1"] = np.ascontiguousarray(
            np.asarray(inputs["L0_ffn_w1"], np.float32)[2 * c:2 * c + 2].reshape(2 * D, DFF0))
        m["l0_w2"] = np.ascontiguousarray(
            np.asarray(inputs["L0_ffn_w2"], np.float32)[2 * c:2 * c + 2].reshape(2 * DFF0, D))
        ss = np.arange(128)[:, None]
        tt = np.arange(128)[None, :]
        m["ustrict"] = (ss < tt).astype(np.float32)
        m["iotam"] = np.tile(np.arange(NE0, dtype=np.float32)[None, :], (128, 1))
        m["islot8"] = np.tile(np.arange(128, dtype=np.float32)[:, None], (1, 8))
        pmv = np.zeros((8, 1), np.float32)
        pmv[:c, 0] = 1.0
        m["prevmask"] = pmv
        esr = np.zeros((2, NE0), np.float32)
        esr[0, 2 * c] = 1.0
        esr[1, 2 * c + 1] = 1.0
        m["eselr"] = esr
        ee = np.arange(NE0)
        sbase = (ee // 2) * (2 * SCAP) + (ee % 2) * SCAP
        m["sendbase"] = np.tile(sbase.astype(np.float32)[None, :], (128, 1))
        s8 = np.arange(8)[:, None]
        o8 = np.arange(8)[None, :]
        m["ltri8"] = (s8 <= o8).astype(np.float32)
    if "l1_wr" in names:
        m["l1_wr"] = np.ascontiguousarray(np.asarray(inputs["L1_ffn_Wr"], np.float32))
        m["l1_br"] = np.asarray(inputs["L1_ffn_br"], np.float32).reshape(1, NE1)
        m["l1_w1"] = np.ascontiguousarray(np.asarray(inputs["L1_ffn_W1"], np.float32).reshape(NE1 * D, D))
        m["l1_b1"] = np.ascontiguousarray(np.asarray(inputs["L1_ffn_b1"], np.float32))
        m["l1_w2"] = np.ascontiguousarray(np.asarray(inputs["L1_ffn_W2"], np.float32).reshape(NE1 * D, D))
        m["l1_b2"] = np.ascontiguousarray(np.asarray(inputs["L1_ffn_b2"], np.float32))
    return m


_PROG_CACHE = {}


def _get_program(phases):
    key = tuple(phases)
    if key not in _PROG_CACHE:
        _PROG_CACHE[key] = build_program(phases)
    return _PROG_CACHE[key]


def run_phases(phases, inputs):
    nc, io = _get_program(phases)
    names = set(io.inputs.keys())
    in_maps = [make_core_inputs(inputs, c, names) for c in range(NCORES)]
    res = run_bass_kernel_spmd(nc, in_maps, core_ids=list(range(NCORES)))
    y = np.concatenate([res.results[c]["y"] for c in range(NCORES)], axis=0)
    return y.reshape(2, TBATCH, D)


def kernel(**inputs):
    x = run_phases(["ffn0", "attn0", "ffn0", "ffn1", "attn1", "ffn1"], dict(inputs))
    return np.ascontiguousarray(x.astype(np.float32))
